# version 06
# baseline (speedup 1.0000x reference)
import numpy as np
import ml_dtypes
from contextlib import ExitStack

import concourse.bass as bass
import concourse.mybir as mybir
from concourse.alu_op_type import AluOpType as OP
from concourse.bass_utils import run_bass_kernel_spmd

AF = mybir.ActivationFunctionType
AX = mybir.AxisListType
F32 = mybir.dt.float32
BF16 = mybir.dt.bfloat16
NPBF = ml_dtypes.bfloat16

D = 1024
DI = 2048
NS = 16
DTR = 64
L = 2048
H, HD = 16, 64
TC = 256
NCH = L // TC
EPS_RMS = float(np.float32(1.1920929e-07))
EPS_LN = 1e-5
NEG = -1e30

ENGINES = ("sync", "tensor", "vector", "scalar", "gpsimd")
CHUNK_BARRIER = False


class Sem:
    __slots__ = ("h", "n", "name")
    def __init__(self, h, name):
        self.h, self.n, self.name = h, 0, name


class Arena:
    def __init__(self, tensor, cols):
        self.t = tensor
        self.cols = cols
        self.free = [(0, cols)]
        self.peak = 0

    def _alloc(self, n, align):
        for i, (off, sz) in enumerate(self.free):
            a = (off + align - 1) // align * align
            pad = a - off
            if sz - pad >= n:
                ent = []
                if pad:
                    ent.append((off, pad))
                if sz - pad - n:
                    ent.append((a + n, sz - pad - n))
                self.free[i:i + 1] = ent
                self.peak = max(self.peak, self.cols - sum(s for _, s in self.free))
                return a
        raise MemoryError(f"arena OOM want {n} free={self.free}")

    def release(self, key):
        off, n = key
        self.free.append((off, n))
        self.free.sort()
        m = []
        for o, s in self.free:
            if m and m[-1][0] + m[-1][1] == o:
                m[-1] = (m[-1][0], m[-1][1] + s)
            else:
                m.append((o, s))
        self.free = m

    def t16(self, n):
        off = self._alloc(n, 16)
        return self.t[:, off:off + n], (off, n)

    def t32(self, n):
        off = self._alloc(2 * n, 16)
        return self.t[:, off:off + 2 * n].bitcast(F32), (off, 2 * n)


class KB:
    def __init__(self):
        self.nc = bass.Bass("TRN2", target_bir_lowering=False)
        self.ctx = ExitStack()
        self.streams = {e: [] for e in ENGINES}
        self.sems = {}

    def sem(self, name):
        s = Sem(self.ctx.enter_context(self.nc.semaphore(name)), name)
        self.sems[name] = s
        return s

    def sb(self, name, shape, dtype):
        return self.ctx.enter_context(self.nc.sbuf_tensor(name, list(shape), dtype))

    def ps(self, name, shape, dtype=F32):
        return self.ctx.enter_context(self.nc.psum_tensor(name, list(shape), dtype))

    def op(self, eng, fn, inc=None):
        if inc is not None:
            sem, kk = inc
            sem.n += kk
            def run(e, fn=fn, sem=sem, kk=kk):
                fn(e).then_inc(sem.h, kk)
        else:
            def run(e, fn=fn):
                fn(e)
        self.streams[eng].append(run)

    def vw(self):
        self.wait("vector", self.sems["sV"])

    def wait(self, eng, sem, val=None):
        v = sem.n if val is None else val
        assert v <= sem.n, f"{eng} waits {sem.name}>={v} planned {sem.n}"
        if v > 0:
            self.streams[eng].append(lambda e, s=sem, v=v: e.wait_ge(s.h, v))

    def dma(self, eng, out, in_, sem, slow=False):
        if slow:
            self.op(eng, lambda e, o=out, i=in_: e.dma_start(
                out=o, in_=i, allow_slow_non_contiguous=True), inc=(sem, 16))
        else:
            self.op(eng, lambda e, o=out, i=in_: e.dma_start(out=o, in_=i), inc=(sem, 16))
        return sem.n

    def barrier(self, engines=ENGINES):
        if not hasattr(self, "dma_owner"):
            snap = [(s, s.n) for s in self.sems.values() if s.n > 0]
            for e in engines:
                for s, v in snap:
                    self.wait(e, s, v)
            return
        comp = [self.sems[n] for n in ("sT", "sV", "sA", "sG") if n in self.sems]
        for e in ENGINES:
            for s in self.dma_owner[e]:
                if s.n > 0:
                    self.wait(e, s, s.n)
            self.streams[e].append(lambda eng, b=self.bar: eng.sem_inc(b.h, 1))
            self.bar.n += 1
        tgt = self.bar.n
        for e in ENGINES:
            self.wait(e, self.bar, tgt)
            for s in comp:
                if s.n > 0:
                    self.wait(e, s, s.n)

    def finalize(self):
        with self.nc.Block() as blk:
            for name in ENGINES:
                def body(eng, name=name):
                    for f in self.streams[name]:
                        f(eng)
                getattr(blk, name)(body)
        return self.nc



def host_arrays(inputs):
    f32 = np.float32
    def T(a):
        return np.ascontiguousarray(np.asarray(a, f32).T)
    def chanvec(v, d):
        return np.ascontiguousarray(np.asarray(v, f32).reshape(d // 128, 128).T)

    sh = {
        "w_inproj": T(inputs["L0_attn_in_proj_w"]),
        "w_xproj_f": T(inputs["L0_attn_x_proj_w"]),
        "w_xproj_b": T(inputs["L0_attn_x_proj_wb"]),
        "w_dtproj_f": T(inputs["L0_attn_dt_proj_w"]),
        "w_dtproj_b": T(inputs["L0_attn_dt_proj_wb"]),
        "w_outproj": T(inputs["L0_attn_out_proj_w"]),
        "convw_f": np.ascontiguousarray(np.asarray(inputs["L0_attn_conv_w"], f32)
                                        .reshape(16, 128, 4).transpose(1, 0, 2).reshape(128, 64)),
        "convw_b": np.ascontiguousarray(np.asarray(inputs["L0_attn_conv_wb"], f32)
                                        .reshape(16, 128, 4).transpose(1, 0, 2).reshape(128, 64)),
        "convb_f": chanvec(inputs["L0_attn_conv_b"], DI),
        "convb_b": chanvec(inputs["L0_attn_conv_bb"], DI),
        "dtb_f": chanvec(inputs["L0_attn_dt_proj_b"], DI),
        "dtb_b": chanvec(inputs["L0_attn_dt_proj_bb"], DI),
        "alog_f": np.ascontiguousarray(np.asarray(inputs["L0_attn_A_log"], f32)
                                       .reshape(16, 128, NS).transpose(1, 0, 2).reshape(128, 256)),
        "alog_b": np.ascontiguousarray(np.asarray(inputs["L0_attn_A_b_log"], f32)
                                       .reshape(16, 128, NS).transpose(1, 0, 2).reshape(128, 256)),
        "dvec_f": chanvec(inputs["L0_attn_D"], DI),
        "dvec_b": chanvec(inputs["L0_attn_D_b"], DI),
        "w_q": T(inputs["L0_ffn_wq"]), "w_k": T(inputs["L0_ffn_wk"]),
        "w_v": T(inputs["L0_ffn_wv"]), "w_o": T(inputs["L0_ffn_wo"]),
        "w_gate": T(inputs["L0_ffn_gate_w"]),
        "n1w": chanvec(inputs["L0_ffn_n1_w"], D),
        "n2w": chanvec(inputs["L0_ffn_n2_w"], D),
        "qnw": np.ascontiguousarray(np.tile(np.asarray(inputs["L0_ffn_qn_w"], f32), 2)[:, None]),
        "knw": np.ascontiguousarray(np.tile(np.asarray(inputs["L0_ffn_kn_w"], f32), 2)[:, None]),
        "qnrow": np.asarray(inputs["L0_ffn_qn_w"], f32)[None, :].copy(),
        "knrow": np.asarray(inputs["L0_ffn_kn_w"], f32)[None, :].copy(),
        "w_e0up": T(inputs["L0_ffn_e0_up"]), "w_e0dn": T(inputs["L0_ffn_e0_down"]),
        "w_e1a": T(inputs["L0_ffn_e1_w1"]), "w_e1b": T(inputs["L0_ffn_e1_w2"]),
        "w_e1dn": T(inputs["L0_ffn_e1_down"]),
        "w_e2a": T(inputs["L0_ffn_e2_l1"]), "w_e2b": T(inputs["L0_ffn_e2_l2"]),
        "w_e2c": T(inputs["L0_ffn_e2_l3"]), "w_e2d": T(inputs["L0_ffn_e2_l4"]),
        "w_e3a": T(inputs["L0_ffn_e3_down"]), "w_e3b": T(inputs["L0_ffn_e3_up"]),
        "w_e3c": T(inputs["L0_ffn_e3_out"]),
        "c_id4": np.eye(4, dtype=f32),
    }
    kk, qq = np.meshgrid(np.arange(128), np.arange(128), indexing="ij")
    sh["c_maskT"] = np.where(kk <= qq, 0.0, NEG).astype(f32)
    sh["c_ones"] = np.ones((128, 128), NPBF)
    bd = np.zeros((128, 2), NPBF); bd[:64, 0] = 1; bd[64:, 1] = 1
    sh["c_bd64"] = bd
    ex = np.zeros((2, 128), NPBF); ex[0, :64] = 1; ex[1, 64:] = 1
    sh["c_ex64"] = ex
    z1 = np.zeros((128, 65), NPBF); z1[:, 64] = 1
    sh["c_z64o"] = z1
    hl = np.zeros((128, 128), NPBF); hl[:, :64] = 1
    sh["c_hlo"] = hl
    hh = np.zeros((128, 128), NPBF); hh[:, 64:] = 1
    sh["c_hhi"] = hh
    sh["c_idbf"] = np.eye(128, dtype=NPBF)
    sh["c_sc"] = np.tile(np.array([EPS_LN, EPS_RMS, 1.0 / HD, 1.0 / D,
                                   -1.0, 15.0, 1.0, 0.0], f32), (128, 1))
    nm = np.ones((128, 1), f32); nm[64:80] = -1.0
    sh["c_negmix"] = nm
    x = np.asarray(inputs["L0_attn_hidden_states"], f32)
    percore = [{"x_fm": np.ascontiguousarray(x[c].T)} for c in range(x.shape[0])]
    return sh, percore


WSPEC_F32 = [
    ("w_inproj", [D, 2 * DI]), ("w_xproj_f", [DI, 96]), ("w_xproj_b", [DI, 96]),
    ("w_dtproj_f", [DTR, DI]), ("w_dtproj_b", [DTR, DI]), ("w_outproj", [DI, D]),
    ("convw_f", [128, 64]), ("convw_b", [128, 64]),
    ("convb_f", [128, 16]), ("convb_b", [128, 16]),
    ("dtb_f", [128, 16]), ("dtb_b", [128, 16]),
    ("alog_f", [128, 256]), ("alog_b", [128, 256]),
    ("dvec_f", [128, 16]), ("dvec_b", [128, 16]),
    ("w_q", [D, D]), ("w_k", [D, D]), ("w_v", [D, D]), ("w_o", [D, D]),
    ("w_gate", [D, 4]), ("n1w", [128, 8]), ("n2w", [128, 8]),
    ("qnw", [128, 1]), ("knw", [128, 1]), ("qnrow", [1, HD]), ("knrow", [1, HD]),
    ("w_e0up", [D, DI]), ("w_e0dn", [DI, D]),
    ("w_e1a", [D, DI]), ("w_e1b", [D, DI]), ("w_e1dn", [DI, D]),
    ("w_e2a", [D, DI]), ("w_e2b", [DI, DI]), ("w_e2c", [DI, DI]), ("w_e2d", [DI, D]),
    ("w_e3a", [D, D]), ("w_e3b", [D, DI]), ("w_e3c", [DI, D]),
    ("c_maskT", [128, 128]), ("c_id4", [4, 4]), ("c_sc", [128, 8]),
    ("c_negmix", [128, 1]),
]
WSPEC_BF = [("c_ones", [128, 128]), ("c_bd64", [128, 2]), ("c_ex64", [2, 128]),
            ("c_z64o", [128, 65]), ("c_hlo", [128, 128]), ("c_hhi", [128, 128]),
            ("c_idbf", [128, 128])]

SMALL = ["convw_f", "convw_b", "convb_f", "convb_b", "dtb_f", "dtb_b",
         "alog_f", "alog_b", "dvec_f", "dvec_b", "n1w", "n2w",
         "qnw", "knw", "qnrow", "knrow", "c_maskT", "c_id4", "c_sc", "c_negmix"]


class Prog:
    def __init__(self, taps=()):
        self.k = KB()
        self.taps = set(taps)
        self.dbg = {}
        nc = self.k.nc
        self.dx = nc.dram_tensor("x_fm", [D, L], F32, kind="ExternalInput")
        self.dout = nc.dram_tensor("out_fm", [D, L], F32, kind="ExternalOutput")
        self.dw = {n: nc.dram_tensor(n, s, F32, kind="ExternalInput") for n, s in WSPEC_F32}
        for n, s in WSPEC_BF:
            self.dw[n] = nc.dram_tensor(n, s, BF16, kind="ExternalInput")
        self.scr_xn = nc.dram_tensor("scr_xn", [D, L], F32, kind="Internal")
        self.scr_xffn = nc.dram_tensor("scr_xffn", [D, L], F32, kind="Internal")
        self.scr_sf = nc.dram_tensor("scr_sf", [D, L], BF16, kind="Internal")
        self.scr_bc = nc.dram_tensor("scr_bc", [2, 32 * TC], BF16, kind="Internal")
        self.scr_row = nc.dram_tensor("scr_row", [8, L], F32, kind="Internal")
        self.scr_gw = nc.dram_tensor("scr_gw", [4, L], BF16, kind="Internal")
        self.scr_wip = nc.dram_tensor("scr_wip", [D, 2 * DI], BF16, kind="Internal")
        self.scr_wop = nc.dram_tensor("scr_wop", [DI, D], BF16, kind="Internal")
        self.scrw = {}
        for n, s, fold in [
            ("w_q", [D, D], 1), ("w_k", [D, D], 1), ("w_v", [D, D], 1),
            ("w_o", [D, D], 0),
            ("w_e0up", [D, DI], 2), ("w_e0dn", [DI, D], 0),
            ("w_e1a", [D, DI], 2), ("w_e1b", [D, DI], 2), ("w_e1dn", [DI, D], 0),
            ("w_e2a", [D, DI], 2), ("w_e2b", [DI, DI], 0), ("w_e2c", [DI, DI], 0),
            ("w_e2d", [DI, D], 0),
            ("w_e3a", [D, D], 2), ("w_e3b", [D, DI], 0), ("w_e3c", [DI, D], 0),
        ]:
            self.scrw[n] = (nc.dram_tensor("scrw_" + n, s, BF16, kind="Internal"), fold)

        k = self.k
        self.sW = k.sem("sW")
        self.sX = k.sem("sX")
        self.sT = k.sem("sT")
        self.sV = k.sem("sV")
        self.sA = k.sem("sA")
        self.sG = k.sem("sG")
        self.sO = k.sem("sO")
        self.sWp = [k.sem("sWa"), k.sem("sWb")]
        self.sXp = [k.sem("sXa"), k.sem("sXb")]
        self.sB = k.sem("sB")
        self.sC = k.sem("sC")
        self.sD = k.sem("sD")
        self.sBar = k.sem("sBar")
        k.dma_owner = {"sync": [self.sW, self.sWp[0], self.sWp[1], self.sB],
                       "gpsimd": [self.sX, self.sXp[0], self.sXp[1], self.sO, self.sC],
                       "scalar": [self.sD], "vector": [], "tensor": []}
        k.bar = self.sBar

        self.pb = [k.ps(f"pb{i}", [128, 512]) for i in range(8)]

        self.xnbf = [k.sb(f"xnbf{i}", [128, L], BF16) for i in range(8)]
        self.gbf = [k.sb(f"gbf{i}", [128, L], BF16) for i in range(8)]
        self.cst = {}
        for n, s in WSPEC_BF:
            self.cst[n] = k.sb(n + "_s", s, BF16)
        for n in SMALL:
            self.cst[n] = k.sb(n + "_s", dict(WSPEC_F32)[n], F32)
        self.aexp = {"f": k.sb("aexp_f", [128, 256], F32),
                     "b": k.sb("aexp_b", [128, 256], F32)}
        self.ndtb = {"f": k.sb("ndtb_f", [128, 16], F32),
                     "b": k.sb("ndtb_b", [128, 16], F32)}
        self.qnw8 = k.sb("qnw8", [128, 1], F32)
        self.bcap = k.sb("bcap", [128, 1], F32)
        self.xpbf = {"f": k.sb("xpbf_f", [128, 16 * 96], BF16),
                     "b": k.sb("xpbf_b", [128, 16 * 96], BF16)}
        self.dtpbf = {"f": k.sb("dtpbf_f", [DTR, DI], BF16),
                      "b": k.sb("dtpbf_b", [DTR, DI], BF16)}
        self.state = k.sb("state", [128, 256], F32)
        self.lgt = k.sb("lgt", [4, 512], F32)

        ACOLS = 60928
        self.ar = Arena(k.sb("arena", [128, ACOLS], BF16), ACOLS)

    def tap(self, name, ap_or_list):
        if name not in self.taps:
            return
        aps = ap_or_list if isinstance(ap_or_list, list) else [ap_or_list]
        w = aps[0].shape[-1]
        t = self.k.nc.dram_tensor("dbg_" + name, [len(aps) * 128, w],
                                  aps[0].dtype, kind="ExternalOutput")
        self.dbg[name] = t
        self.k.wait("gpsimd", self.sV)
        self.k.wait("gpsimd", self.sA)
        self.k.wait("gpsimd", self.sT)
        for j, a in enumerate(aps):
            self.k.dma("gpsimd", t.ap()[j * 128:j * 128 + a.shape[0], :], a, self.sO)
        for e in ENGINES:
            self.k.wait(e, self.sO)

    def build(self, upto="all"):
        k = self.k
        self.phase_load()
        self.phase_ln()
        k.barrier()
        self.phase_precast()
        if upto == "ln":
            return k.finalize()
        k.barrier()
        self.phase_attn()
        if upto == "attn":
            return k.finalize()
        k.barrier()
        self.phase_moe()
        if upto == "moe":
            return k.finalize()
        k.barrier()
        self.phase_mamba("f")
        if upto == "mambaf":
            return k.finalize()
        self.phase_mamba("b")
        k.barrier()
        return k.finalize()

    def phase_load(self):
        k, dw, cst = self.k, self.dw, self.cst
        self.x32, self.x32_key = self.ar.t32(8 * L)
        for i in range(8):
            k.dma("sync", self.x32[:, i * L:(i + 1) * L],
                  self.dx.ap()[i * 128:(i + 1) * 128, :], self.sW)
        for n, _ in WSPEC_BF:
            k.dma("sync", cst[n][:], dw[n].ap(), self.sW)
        for n in SMALL:
            k.dma("sync", cst[n][:], dw[n].ap(), self.sW)
        wl = self.sW.n
        for e in ("vector", "scalar", "tensor"):
            k.wait(e, self.sW, wl)
        for d in ("f", "b"):
            k.op("scalar", lambda e, d=d: e.activation(
                self.aexp[d][:], cst["alog_" + d][:], AF.Exp), inc=(self.sA, 1))
            k.op("vector", lambda e, d=d: e.tensor_scalar(
                self.ndtb[d][:], cst["dtb_" + d][:], -1.0, None, OP.mult), inc=(self.sV, 1))
        k.op("vector", lambda e: e.tensor_scalar(
            self.qnw8[:], cst["qnw"][:], 0.125, None, OP.mult), inc=(self.sV, 1))
        tq, kq = self.ar.t32(4)
        k.op("vector", lambda e: e.reduce_max(
            tq[0:1, 0:1], cst["qnrow"][:], AX.X, apply_absolute_value=True), inc=(self.sV, 1))
        k.op("vector", lambda e: e.reduce_max(
            tq[0:1, 1:2], cst["knrow"][:], AX.X, apply_absolute_value=True), inc=(self.sV, 1))
        k.vw()
        k.op("vector", lambda e: e.tensor_tensor(
            tq[0:1, 2:3], tq[0:1, 0:1], tq[0:1, 1:2], OP.mult), inc=(self.sV, 1))
        k.vw()
        k.op("vector", lambda e: e.tensor_scalar(
            tq[0:1, 3:4], tq[0:1, 2:3], -8.0, 15.0, OP.mult, OP.add), inc=(self.sV, 1))
        k.vw()
        k.wait("gpsimd", self.sV)
        k.dma("gpsimd", self.scr_row.ap()[7:8, 0:1], tq[0:1, 3:4], self.sO)
        k.wait("sync", self.sO)
        k.dma("sync", self.bcap[:], bass.AP(self.scr_row, 7 * L, [[0, 128], [1, 1]]), self.sB)
        self.ar.release(kq)
        st32, key = self.ar.t32(2048)
        for d in ("f", "b"):
            for kc in range(16):
                k.dma("sync", st32[0:128, kc * 96:(kc + 1) * 96],
                      dw["w_xproj_" + d].ap()[kc * 128:(kc + 1) * 128, :], self.sB)
            wv = self.sB.n
            k.wait("vector", self.sB, wv)
            k.op("vector", lambda e, d=d: e.tensor_copy(
                self.xpbf[d][:], st32[:, 0:16 * 96]), inc=(self.sV, 1))
            k.wait("sync", self.sV)
        for d in ("f", "b"):
            k.dma("sync", st32[0:DTR, 0:DI], dw["w_dtproj_" + d].ap(), self.sB)
            k.wait("vector", self.sB)
            k.op("vector", lambda e, d=d: e.tensor_copy(
                self.dtpbf[d][:], st32[0:DTR, 0:DI]), inc=(self.sV, 1))
            k.wait("sync", self.sV)
        k.op("vector", lambda e: e.memset(self.state[:], 0.0), inc=(self.sV, 1))
        self.ar.release(key)

    def phase_ln(self):
        k, cst, pb = self.k, self.cst, self.pb
        x32 = self.x32
        mu, mu_key = self.ar.t32(L)
        rb, rb_key = self.ar.t32(L)
        sqt, sq_key = self.ar.t16(2 * L)
        sq = [sqt[:, 0:L], sqt[:, L:2 * L]]
        cast_done = []
        sq_done = []
        for i in range(8):
            k.op("vector", lambda e, i=i: e.tensor_copy(
                self.xnbf[i][:], x32[:, i * L:(i + 1) * L]), inc=(self.sV, 1))
            cast_done.append(self.sV.n)
        sq_cons = [0, 0]
        for i in range(8):
            if i >= 2:
                k.wait("scalar", self.sT, sq_cons[i % 2])
            k.op("scalar", lambda e, i=i: e.activation(
                sq[i % 2][:], x32[:, i * L:(i + 1) * L], AF.Square), inc=(self.sA, 1))
            sq_done.append(self.sA.n)
            k.wait("tensor", self.sV, cast_done[i])
            k.wait("tensor", self.sA, sq_done[i])
            for qc in range(4):
                k.op("tensor", lambda e, i=i, qc=qc: e.matmul(
                    pb[qc][:, :], cst["c_ones"][:], self.xnbf[i][:, qc * 512:(qc + 1) * 512],
                    start=(i == 0), stop=(i == 7)), inc=(self.sT, 1))
                k.op("tensor", lambda e, i=i, qc=qc: e.matmul(
                    pb[4 + qc][:, :], cst["c_ones"][:], sq[i % 2][:, qc * 512:(qc + 1) * 512],
                    start=(i == 0), stop=(i == 7)), inc=(self.sT, 1))
            sq_cons[i % 2] = self.sT.n
        mm_all = self.sT.n
        k.wait("vector", self.sT, mm_all)
        musq, msq_key = self.ar.t32(L)
        for qc in range(4):
            sl = slice(qc * 512, (qc + 1) * 512)
            k.op("vector", lambda e, qc=qc, sl=sl: e.tensor_scalar(
                mu[:, sl], pb[qc][:, :], 1.0 / D, None, OP.mult), inc=(self.sV, 1))
        k.wait("scalar", self.sV)
        for qc in range(4):
            sl = slice(qc * 512, (qc + 1) * 512)
            k.op("scalar", lambda e, sl=sl: e.activation(
                musq[:, sl], mu[:, sl], AF.Square), inc=(self.sA, 1))
        k.wait("vector", self.sA)
        for qc in range(4):
            sl = slice(qc * 512, (qc + 1) * 512)
            k.op("vector", lambda e, qc=qc, sl=sl: e.scalar_tensor_tensor(
                musq[:, sl], pb[4 + qc][:, :], 1.0 / D, musq[:, sl],
                OP.mult, OP.subtract), inc=(self.sV, 1))
        k.wait("scalar", self.sV)
        k.op("scalar", lambda e: e.activation(
            musq[:, :], musq[:, :], AF.Sqrt, bias=cst["c_sc"][:, 0:1]), inc=(self.sA, 1))
        k.wait("vector", self.sA)
        k.op("vector", lambda e: e.reciprocal(rb[:, :], musq[:, :]), inc=(self.sV, 1))
        k.vw()
        for i in range(8):
            sl = slice(i * L, (i + 1) * L)
            k.op("vector", lambda e, sl=sl: e.tensor_tensor(
                x32[:, sl], x32[:, sl], mu[:, :], OP.subtract), inc=(self.sV, 1))
            k.vw()
            k.op("vector", lambda e, sl=sl: e.tensor_tensor(
                x32[:, sl], x32[:, sl], rb[:, :], OP.mult), inc=(self.sV, 1))
            k.vw()
            k.op("vector", lambda e, i=i, sl=sl: e.tensor_copy(
                self.xnbf[i][:], x32[:, sl]), inc=(self.sV, 1))
            k.wait("gpsimd", self.sV)
            k.dma("gpsimd", self.scr_xn.ap()[i * 128:(i + 1) * 128, :],
                  x32[:, sl], self.sO)
        self.tap("xn", [x32[:, i * L:(i + 1) * L] for i in range(8)])
        k.barrier()
        for key in (mu_key, rb_key, sq_key, msq_key, self.x32_key):
            self.ar.release(key)

    def phase_precast(self):
        k, dw, cst = self.k, self.dw, self.cst
        st, skey = self.ar.t32(2 * 4096)
        bt, bkey = self.ar.t16(2 * 4096)
        jobs = []
        for r in range(8):
            jobs.append((dw["w_inproj"].ap()[r * 128:(r + 1) * 128, :],
                         self.scr_wip.ap()[r * 128:(r + 1) * 128, :], 4096, None))
        for r in range(16):
            jobs.append((dw["w_outproj"].ap()[r * 128:(r + 1) * 128, :],
                         self.scr_wop.ap()[r * 128:(r + 1) * 128, :], 1024, None))
        for n, (scrt, fold) in self.scrw.items():
            kin, kout = dict(WSPEC_F32)[n]
            for r in range(kin // 128):
                fc = None
                if fold == 1:
                    fc = cst["n1w"][:, r:r + 1]
                elif fold == 2:
                    fc = cst["n2w"][:, r:r + 1]
                jobs.append((dw[n].ap()[r * 128:(r + 1) * 128, :],
                             scrt.ap()[r * 128:(r + 1) * 128, :], kout, fc))
        cast = [0, 0]
        spill = [0, 0]
        for j, (srcap, dst, w, fc) in enumerate(jobs):
            b = j % 2
            k.wait("sync", self.sV, cast[b])
            ld = k.dma("sync", st[:, b * 4096:b * 4096 + w], srcap, self.sWp[b])
            k.wait("vector", self.sWp[b], ld)
            k.wait("vector", self.sO, spill[b])
            if fc is None:
                k.op("vector", lambda e, b=b, w=w: e.tensor_copy(
                    bt[:, b * 4096:b * 4096 + w], st[:, b * 4096:b * 4096 + w]),
                    inc=(self.sV, 1))
            else:
                k.op("vector", lambda e, b=b, w=w, fc=fc: e.tensor_scalar(
                    bt[:, b * 4096:b * 4096 + w], st[:, b * 4096:b * 4096 + w],
                    fc, None, OP.mult), inc=(self.sV, 1))
            cast[b] = self.sV.n
            k.wait("gpsimd", self.sV, cast[b])
            spill[b] = k.dma("gpsimd", dst, bt[:, b * 4096:b * 4096 + w], self.sO)
        k.barrier()
        self.ar.release(skey)
        self.ar.release(bkey)

    def _headrms(self, pq_bank, gain_col, out_ap, rot):
        k, cst, pb = self.k, self.cst, self.pb
        sqq, rr32, rrb, rqb = self.hr_bufs[rot % 2]
        bank = self.pb[2 + rot % 2]
        k.wait("scalar", self.sT)
        k.op("scalar", lambda e: e.activation(sqq, pq_bank[:, :], AF.Square),
             inc=(self.sA, 1))
        k.wait("tensor", self.sA)
        k.op("tensor", lambda e, bank=bank: e.matmul(
            bank[0:2, :], cst["c_bd64"][:], sqq, start=True, stop=True), inc=(self.sT, 1))
        k.wait("scalar", self.sT)
        k.op("scalar", lambda e, bank=bank: e.activation(
            rr32, bank[0:2, :], AF.Sqrt, bias=cst["c_sc"][0:2, 1:2],
            scale=cst["c_sc"][0:2, 2:3]), inc=(self.sA, 1))
        k.wait("vector", self.sA)
        k.op("vector", lambda e: e.reciprocal(rr32, rr32), inc=(self.sV, 1))
        k.vw()
        k.op("vector", lambda e: e.tensor_copy(rrb, rr32), inc=(self.sV, 1))
        k.wait("tensor", self.sV)
        k.op("tensor", lambda e, bank=bank: e.matmul(
            bank[:, :], cst["c_ex64"][:], rrb, start=True, stop=True), inc=(self.sT, 1))
        k.wait("vector", self.sT)
        k.op("vector", lambda e, bank=bank: e.tensor_copy(rqb, bank[:, :]), inc=(self.sV, 1))
        k.vw()
        k.op("vector", lambda e: e.scalar_tensor_tensor(
            out_ap, pq_bank[:, :], gain_col, rqb, OP.mult, OP.mult), inc=(self.sV, 1))
        return self.sV.n

    def _wslab(self, wname, mc, wbf, sem, queue="sync", ncols=128):
        k = self.k
        scrt = self.scrw[wname][0] if wname in self.scrw else None
        assert scrt is not None
        kin = scrt.shape[0]
        nkc = kin // 128
        for kc in range(nkc):
            k.dma(queue, wbf[:, kc * ncols:(kc + 1) * ncols],
                  scrt.ap()[kc * 128:(kc + 1) * 128, mc * ncols:(mc + 1) * ncols], sem)
        return sem.n

    def phase_attn(self):
        k, cst, pb = self.k, self.cst, self.pb
        n1cols = [cst["n1w"][:, i:i + 1] for i in range(8)]
        sqt, sq_key = self.ar.t16(2 * L)
        sq = [sqt[:, 0:L], sqt[:, L:2 * L]]
        r1b, r1_key = self.ar.t32(L)
        sq_cons = [0, 0]
        for i in range(8):
            if i >= 2:
                k.wait("scalar", self.sT, sq_cons[i % 2])
            k.op("scalar", lambda e, i=i: e.activation(
                sq[i % 2][:], self.xnbf[i][:], AF.Square), inc=(self.sA, 1))
            k.wait("tensor", self.sA)
            for qc in range(4):
                k.op("tensor", lambda e, i=i, qc=qc: e.matmul(
                    pb[qc][:, :], cst["c_ones"][:], sq[i % 2][:, qc * 512:(qc + 1) * 512],
                    start=(i == 0), stop=(i == 7)), inc=(self.sT, 1))
            sq_cons[i % 2] = self.sT.n
        k.wait("scalar", self.sT)
        for qc in range(4):
            k.op("scalar", lambda e, qc=qc: e.activation(
                r1b[:, qc * 512:(qc + 1) * 512], pb[qc][:, :], AF.Sqrt,
                bias=cst["c_sc"][:, 1:2], scale=cst["c_sc"][:, 3:4]), inc=(self.sA, 1))
        k.wait("vector", self.sA)
        k.op("vector", lambda e: e.reciprocal(r1b[:, :], r1b[:, :]), inc=(self.sV, 1))
        k.wait("gpsimd", self.sV)
        k.dma("gpsimd", self.scr_row.ap()[0:1, :], r1b[0:1, :], self.sO)
        k.wait("sync", self.sO)
        r1tm, r1tm_key = self.ar.t32(16)
        k.dma("sync", r1tm[:, 0:16],
              bass.AP(self.scr_row, 0, [[1, 128], [128, 16]]), self.sB, slow=True)
        self.ar.release(sq_key)
        self.ar.release(r1_key)
        k.barrier()

        kh, kh_key = self.ar.t16(8 * L)
        wbf, wbf_key = self.ar.t16(2 * 1024)
        hrb32, hrb32_key = self.ar.t32(2 * 1024)
        hrb16, hrb16_key = self.ar.t16(2 * 1024)
        self.hrb32, self.hrb16 = hrb32, hrb16
        self.hr_bufs = []
        for b in range(2):
            o32 = b * 1024
            o16 = b * 1024
            sqq = hrb16[:, o16:o16 + 512]
            rr32 = hrb32[0:2, o32:o32 + 512]
            rrb = hrb16[0:2, o16 + 512:o16 + 1024]
            rqb = hrb32[:, o32 + 512:o32 + 1024]
            self.hr_bufs.append((sqq, rr32, rrb, rqb))

        wready = [0, 0]
        wcons = [0, 0]
        for mc in range(8):
            b = mc % 2
            k.wait("sync", self.sT, wcons[b])
            wready[b] = self._wslab("w_k", mc, wbf[:, b * 1024:(b + 1) * 1024],
                                    self.sWp[b])
            k.wait("tensor", self.sWp[b], wready[b])
            for qc in range(4):
                r = mc * 4 + qc
                bank = pb[r % 2]
                k.wait("tensor", self.sV)
                for kc in range(8):
                    k.op("tensor", lambda e, b=b, kc=kc, qc=qc, bank=bank: e.matmul(
                        bank[:, :], wbf[:, b * 1024 + kc * 128:b * 1024 + (kc + 1) * 128],
                        self.xnbf[kc][:, qc * 512:(qc + 1) * 512],
                        start=(kc == 0), stop=(kc == 7)), inc=(self.sT, 1))
                self._headrms(bank, cst["knw"][:, 0:1],
                              kh[:, mc * L + qc * 512:mc * L + (qc + 1) * 512], r)
            wcons[b] = self.sT.n
        k.barrier()

        vt, vt_key = self.ar.t16(16 * 1024)
        vwbf, vwbf_key = self.ar.t16(8 * 1024)
        for dc in range(8):
            k.dma("sync", vwbf[:, dc * 1024:(dc + 1) * 1024],
                  self.scrw["w_v"][0].ap()[dc * 128:(dc + 1) * 128, :], self.sB)
        k.wait("tensor", self.sB)
        k.wait("vector", self.sB)
        vevac = 0
        for kc in range(16):
            for hf in range(2):
                bank = pb[(kc * 2 + hf) % 2]
                k.wait("tensor", self.sV, vevac)
                for dc in range(8):
                    k.op("tensor", lambda e, kc=kc, hf=hf, dc=dc, bank=bank: e.matmul(
                        bank[:, :], self.xnbf[dc][:, kc * 128:(kc + 1) * 128],
                        vwbf[:, dc * 1024 + hf * 512:dc * 1024 + (hf + 1) * 512],
                        start=(dc == 0), stop=(dc == 7)), inc=(self.sT, 1))
                k.wait("vector", self.sT)
                k.op("vector", lambda e, kc=kc, hf=hf, bank=bank: e.tensor_scalar(
                    vt[:, kc * 1024 + hf * 512:kc * 1024 + (hf + 1) * 512],
                    bank[:, :], r1tm[:, kc:kc + 1], None, OP.mult), inc=(self.sV, 1))
                vevac = self.sV.n
        self.ar.release(vwbf_key)
        k.barrier()
        self.tap("bcap", self.bcap[:])
        self.tap("r1tm", r1tm[:, 0:16])
        self.tap("kh", [kh[:, i * L:(i + 1) * L] for i in range(8)])
        self.tap("vt", [vt[:, i * 1024:(i + 1) * 1024] for i in range(16)])

        pt, pt_key = self.ar.t16(2 * 16 * 512)
        qh, qh_key = self.ar.t16(2 * 512)
        avdone = 0
        for mc in range(8):
            b = mc % 2
            k.wait("sync", self.sT, wcons[b])
            wready[b] = self._wslab("w_q", mc, wbf[:, b * 1024:(b + 1) * 1024],
                                    self.sWp[b])
            k.wait("tensor", self.sWp[b], wready[b])
            for qc in range(4):
                r = mc * 4 + qc
                bank = pb[r % 2]
                k.wait("tensor", self.sV)
                k.wait("vector", self.sT)
                for kc in range(8):
                    k.op("tensor", lambda e, b=b, kc=kc, qc=qc, bank=bank: e.matmul(
                        bank[:, :], wbf[:, b * 1024 + kc * 128:b * 1024 + (kc + 1) * 128],
                        self.xnbf[kc][:, qc * 512:(qc + 1) * 512],
                        start=(kc == 0), stop=(kc == 7)), inc=(self.sT, 1))
                qv = self._headrms(bank, self.qnw8[:, 0:1], qh[:, b * 512:(b + 1) * 512], r)
                k.wait("tensor", self.sV, qv)
                o32 = (r % 2) * 1024
                o16 = (r % 2) * 1024
                nrow = self.hrb32[:, o32:o32 + 512]
                nrowb = self.hrb16[:, o16 + 512:o16 + 1024]
                nb32 = self.hrb32[:, o32 + 512:o32 + 1024]
                nkc = 4 * qc + 4
                for h2 in range(2):
                    h = 2 * mc + h2
                    for kc in range(nkc):
                        sbank = pb[4 + (kc % 2)]
                        k.wait("tensor", self.sA)
                        k.op("tensor", lambda e, h2=h2, kc=kc, mc=mc, b=b, sbank=sbank: e.matmul(
                            sbank[:, :],
                            kh[64 * h2:64 * h2 + 64,
                               mc * L + kc * 128:mc * L + (kc + 1) * 128],
                            qh[64 * h2:64 * h2 + 64, b * 512:(b + 1) * 512],
                            start=True, stop=True), inc=(self.sT, 1))
                        s = kc - 4 * qc
                        if 0 <= s <= 3:
                            k.wait("vector", self.sT)
                            k.op("vector", lambda e, s=s, sbank=sbank: e.tensor_tensor(
                                sbank[:, s * 128:(s + 1) * 128],
                                sbank[:, s * 128:(s + 1) * 128],
                                cst["c_maskT"][:], OP.add), inc=(self.sV, 1))
                            k.wait("scalar", self.sV)
                        k.wait("scalar", self.sT)
                        k.op("scalar", lambda e, h2=h2, kc=kc, sbank=sbank: e.activation(
                            pt[:, h2 * 8192 + kc * 512:h2 * 8192 + (kc + 1) * 512],
                            sbank[:, :], AF.Exp, bias=self.bcap[:, 0:1]), inc=(self.sA, 1))
                        for s2 in range(min(kc - 4 * qc, 4)):
                            k.wait("vector", self.sA)
                            k.op("vector", lambda e, h2=h2, kc=kc, s2=s2: e.memset(
                                pt[:, h2 * 8192 + kc * 512 + s2 * 128:
                                   h2 * 8192 + kc * 512 + (s2 + 1) * 128], 0.0),
                                 inc=(self.sV, 1))
                ptv = self.sA.n
                ptv2 = self.sV.n
                k.wait("tensor", self.sA, ptv)
                k.wait("tensor", self.sV, ptv2)
                for kc in range(nkc):
                    k.op("tensor", lambda e, mc=mc, kc=kc: e.matmul(
                        pb[6][0:64, :], vt[:, kc * 1024 + 128 * mc:kc * 1024 + 128 * mc + 64],
                        pt[:, 0 * 8192 + kc * 512:0 * 8192 + (kc + 1) * 512],
                        start=(kc == 0), stop=False), inc=(self.sT, 1))
                    k.op("tensor", lambda e, mc=mc, kc=kc, nkc=nkc: e.matmul(
                        pb[6][64:128, :], vt[:, kc * 1024 + 128 * mc + 64:kc * 1024 + 128 * mc + 128],
                        pt[:, 1 * 8192 + kc * 512:1 * 8192 + (kc + 1) * 512],
                        start=(kc == 0), stop=(kc == nkc - 1), tile_position=(0, 64)),
                        inc=(self.sT, 1))
                    k.op("tensor", lambda e, kc=kc: e.matmul(
                        pb[7][0:1, :], cst["c_ones"][:, 0:1],
                        pt[:, 0 * 8192 + kc * 512:0 * 8192 + (kc + 1) * 512],
                        start=(kc == 0), stop=False), inc=(self.sT, 1))
                    k.op("tensor", lambda e, kc=kc, nkc=nkc: e.matmul(
                        pb[7][64:65, :], cst["c_ones"][:, 0:1],
                        pt[:, 1 * 8192 + kc * 512:1 * 8192 + (kc + 1) * 512],
                        start=(kc == 0), stop=(kc == nkc - 1), tile_position=(0, 64)),
                        inc=(self.sT, 1))
                avmm = self.sT.n
                k.wait("vector", self.sT, avmm)
                if mc == 0 and qc == 0 and ("avraw00" in self.taps):
                    avr = qh[:, 512:1024]
                    smr = self.gbf[7][:, 0:512]
                    k.op("vector", lambda e, avr=avr: e.tensor_copy(avr, pb[6][:, :]),
                         inc=(self.sV, 1))
                    k.op("vector", lambda e, smr=smr: e.tensor_copy(smr[0:1, :], pb[7][0:1, :]),
                         inc=(self.sV, 1))
                    k.op("vector", lambda e, smr=smr: e.tensor_copy(smr[64:65, :], pb[7][64:65, :]),
                         inc=(self.sV, 1))
                    k.vw()
                    self.tap("avraw00", avr)
                    self.tap("sums00", smr)
                k.op("vector", lambda e, nrow=nrow: e.reciprocal(nrow[0:1, :], pb[7][0:1, :]),
                     inc=(self.sV, 1))
                k.op("vector", lambda e, nrow=nrow: e.reciprocal(nrow[64:65, :], pb[7][64:65, :]),
                     inc=(self.sV, 1))
                k.vw()
                k.op("vector", lambda e, nrow=nrow, nrowb=nrowb: e.tensor_copy(
                    nrowb[0:1, :], nrow[0:1, :]), inc=(self.sV, 1))
                k.op("vector", lambda e, nrow=nrow, nrowb=nrowb: e.tensor_copy(
                    nrowb[64:65, :], nrow[64:65, :]), inc=(self.sV, 1))
                k.wait("tensor", self.sV)
                k.op("tensor", lambda e, nrowb=nrowb: e.matmul(
                    pb[7][0:64, :], cst["c_ones"][0:1, 0:64], nrowb[0:1, :],
                    start=True, stop=False), inc=(self.sT, 1))
                k.op("tensor", lambda e, nrowb=nrowb: e.matmul(
                    pb[7][64:128, :], cst["c_ones"][64:65, 0:64], nrowb[64:65, :],
                    start=True, stop=True, tile_position=(64, 64)), inc=(self.sT, 1))
                k.wait("vector", self.sT)
                k.op("vector", lambda e, nb32=nb32: e.tensor_copy(nb32[:, :], pb[7][:, :]),
                     inc=(self.sV, 1))
                k.vw()
                k.op("vector", lambda e, mc=mc, qc=qc, nb32=nb32: e.tensor_tensor(
                    self.gbf[mc][:, qc * 512:(qc + 1) * 512], pb[6][:, :],
                    nb32[:, :], OP.mult), inc=(self.sV, 1))
                avdone = self.sV.n
                if mc == 0 and qc == 0:
                    self.tap("pt00", [pt[:, h2 * 8192:h2 * 8192 + 4 * 512]
                                      for h2 in range(2)])
                    self.tap("avb00", self.gbf[0][:, 0:512])
                    self.tap("qh00", qh[:, 0:512])
            wcons[b] = self.sT.n
        k.barrier()
        for key in (pt_key, qh_key, kh_key, vt_key,
                    r1tm_key):
            self.ar.release(key)

        xf32, self.xf32_key = self.ar.t32(8 * L)
        for i in range(8):
            k.dma("sync", xf32[:, i * L:(i + 1) * L],
                  self.scr_xn.ap()[i * 128:(i + 1) * 128, :], self.sB)
        xloaded = self.sB.n
        k.wait("vector", self.sB, xloaded)
        aocons = 0
        for mc in range(2, 8):
            b = mc % 2
            k.wait("sync", self.sT, wcons[b])
            wready[b] = self._wslab("w_o", mc, wbf[:, b * 1024:(b + 1) * 1024],
                                    self.sWp[b])
            k.wait("tensor", self.sWp[b], wready[b])
            for qc in range(4):
                bank = pb[(mc * 4 + qc) % 2]
                k.wait("tensor", self.sV, aocons)
                for kc in range(8):
                    k.op("tensor", lambda e, b=b, kc=kc, qc=qc, bank=bank: e.matmul(
                        bank[:, :], wbf[:, b * 1024 + kc * 128:b * 1024 + (kc + 1) * 128],
                        self.gbf[kc][:, qc * 512:(qc + 1) * 512],
                        start=(kc == 0), stop=(kc == 7)), inc=(self.sT, 1))
                k.wait("vector", self.sT)
                k.op("vector", lambda e, mc=mc, qc=qc, bank=bank: e.tensor_tensor(
                    xf32[:, mc * L + qc * 512:mc * L + (qc + 1) * 512],
                    xf32[:, mc * L + qc * 512:mc * L + (qc + 1) * 512],
                    bank[:, :], OP.add), inc=(self.sV, 1))
                aocons = self.sV.n
            wcons[b] = self.sT.n
        self.tap("xffn", [xf32[:, i * L:(i + 1) * L] for i in range(8)])
        k.barrier()
        self.ar.release(wbf_key)
        self.ar.release(hrb32_key)
        self.ar.release(hrb16_key)

        sqt2, sq2_key = self.ar.t16(2 * L)
        sq2 = [sqt2[:, 0:L], sqt2[:, L:2 * L]]
        self.r2b, self.r2b_key = self.ar.t32(L)
        sq_cons = [0, 0]
        for i in range(8):
            if i >= 2:
                k.wait("scalar", self.sT, sq_cons[i % 2])
            k.op("scalar", lambda e, i=i: e.activation(
                sq2[i % 2][:], xf32[:, i * L:(i + 1) * L], AF.Square), inc=(self.sA, 1))
            k.wait("tensor", self.sA)
            for qc in range(4):
                k.op("tensor", lambda e, i=i, qc=qc: e.matmul(
                    pb[qc][:, :], cst["c_ones"][:], sq2[i % 2][:, qc * 512:(qc + 1) * 512],
                    start=(i == 0), stop=(i == 7)), inc=(self.sT, 1))
            sq_cons[i % 2] = self.sT.n
        k.wait("scalar", self.sT)
        for qc in range(4):
            k.op("scalar", lambda e, qc=qc: e.activation(
                self.r2b[:, qc * 512:(qc + 1) * 512], pb[qc][:, :], AF.Sqrt,
                bias=cst["c_sc"][:, 1:2], scale=cst["c_sc"][:, 3:4]), inc=(self.sA, 1))
        k.wait("vector", self.sA)
        k.op("vector", lambda e: e.reciprocal(self.r2b[:, :], self.r2b[:, :]),
             inc=(self.sV, 1))
        k.wait("gpsimd", self.sV)
        for i in range(8):
            k.dma("gpsimd", self.scr_xffn.ap()[i * 128:(i + 1) * 128, :],
                  xf32[:, i * L:(i + 1) * L], self.sO)
        k.barrier()
        self.ar.release(sq2_key)
        self.ar.release(self.xf32_key)

    def phase_moe(self):
        k, cst, pb = self.k, self.cst, self.pb
        n2cols = [cst["n2w"][:, i:i + 1] for i in range(8)]
        xfh, xfh_key = self.ar.t16(8 * L)
        xt32, xt_key = self.ar.t32(2 * 512)
        for i in range(8):
            for qc in range(4):
                b = (i * 4 + qc) % 2
                k.wait("sync", self.sV)
                ld = k.dma("sync", xt32[:, b * 512:(b + 1) * 512],
                           self.scr_xffn.ap()[i * 128:(i + 1) * 128, qc * 512:(qc + 1) * 512],
                           self.sWp[b])
                k.wait("vector", self.sWp[b], ld)
                k.op("vector", lambda e, i=i, qc=qc, b=b: e.tensor_tensor(
                    xfh[:, i * L + qc * 512:i * L + (qc + 1) * 512],
                    xt32[:, b * 512:(b + 1) * 512],
                    self.r2b[:, qc * 512:(qc + 1) * 512], OP.mult), inc=(self.sV, 1))
        self.tap("xfh", [xfh[:, i * L:(i + 1) * L] for i in range(8)])
        k.barrier()
        self.ar.release(xt_key)
        self.ar.release(self.r2b_key)

        wgbf, wgbf_key = self.ar.t16(32)
        wg32, wg32_key = self.ar.t32(32)
        for kc in range(8):
            k.dma("sync", wg32[:, kc * 4:(kc + 1) * 4],
                  self.dw["w_gate"].ap()[kc * 128:(kc + 1) * 128, :], self.sB)
        k.wait("vector", self.sB)
        for kc in range(8):
            k.op("vector", lambda e, kc=kc: e.tensor_scalar(
                wgbf[:, kc * 4:(kc + 1) * 4], wg32[:, kc * 4:(kc + 1) * 4],
                n2cols[kc], None, OP.mult), inc=(self.sV, 1))
        k.wait("tensor", self.sV)
        self.tap("wgbf", wgbf[:, 0:32])
        lg32 = self.lgt[:]
        gt, gt_key = self.ar.t32(16 + 16 + 16 + 16 + 4 + 16)
        wfm, wfm_key = self.ar.t16(L)
        k.wait("tensor", self.sV)
        for qc in range(4):
            for kc in range(8):
                k.op("tensor", lambda e, kc=kc, qc=qc: e.matmul(
                    pb[2][0:4, :], wgbf[:, kc * 4:(kc + 1) * 4],
                    xfh[:, kc * L + qc * 512:kc * L + (qc + 1) * 512],
                    start=(kc == 0), stop=(kc == 7)), inc=(self.sT, 1))
            k.wait("vector", self.sT)
            k.op("vector", lambda e: e.tensor_copy(lg32[0:4, :], pb[2][0:4, :]),
                 inc=(self.sV, 1))
            k.wait("tensor", self.sV)
            for sub in range(4):
                k.op("tensor", lambda e, sub=sub: e.transpose(
                    pb[3][0:128, sub * 4:(sub + 1) * 4], lg32[0:4, sub * 128:(sub + 1) * 128],
                    cst["c_id4"][:]), inc=(self.sT, 1))
            k.wait("vector", self.sT)
            k.vw()
            k.op("vector", lambda e: e.tensor_copy(gt[:, 0:16], pb[3][:, 0:16]),
                 inc=(self.sV, 1))
            if qc == 0:
                k.vw()
                self.tap("lgfm", lg32[0:4, :])
                k.op("vector", lambda e: e.tensor_copy(wfm[:, 0:16], gt[:, 0:16]),
                     inc=(self.sV, 1))
                k.vw()
                self.tap("gtlg", wfm[:, 0:16])
            lgv = gt[:, 0:16].rearrange("p (s e) -> p s e", e=4)
            k.vw()
            k.op("vector", lambda e, lgv=lgv: e.tensor_reduce(
                gt[:, 64:68], lgv, AX.X, OP.max), inc=(self.sV, 1))
            k.vw()
            k.op("vector", lambda e: e.tensor_scalar(
                gt[:, 64:68], gt[:, 64:68], -1.0, None, OP.mult), inc=(self.sV, 1))
            k.wait("scalar", self.sV)
            for sub in range(4):
                k.op("scalar", lambda e, sub=sub: e.activation(
                    gt[:, 68 + sub * 4:68 + (sub + 1) * 4], gt[:, sub * 4:(sub + 1) * 4],
                    AF.Exp, bias=gt[:, 64 + sub:65 + sub]), inc=(self.sA, 1))
            k.wait("vector", self.sA)
            ev = gt[:, 68:84].rearrange("p (s e) -> p s e", e=4)
            k.vw()
            k.op("vector", lambda e, ev=ev: e.tensor_reduce(
                gt[:, 64:68], ev, AX.X, OP.add), inc=(self.sV, 1))
            k.vw()
            k.op("vector", lambda e: e.reciprocal(gt[:, 64:68], gt[:, 64:68]),
                 inc=(self.sV, 1))
            for sub in range(4):
                k.vw()
                k.op("vector", lambda e, sub=sub: e.tensor_scalar(
                    gt[:, 16 + sub * 4:16 + (sub + 1) * 4],
                    gt[:, 68 + sub * 4:68 + (sub + 1) * 4],
                    gt[:, 64 + sub:65 + sub], None, OP.mult), inc=(self.sV, 1))
            k.vw()
            k.op("vector", lambda e: e.memset(gt[:, 32:48], 0.0), inc=(self.sV, 1))
            for a in range(4):
                for e2 in range(4):
                    if a == e2:
                        continue
                    k.vw()
                    k.op("vector", lambda e, a=a, e2=e2: e.tensor_tensor(
                        gt[:, 48 + e2:64:4][:, 0:4],
                        gt[:, 16 + a:32:4][:, 0:4],
                        gt[:, 16 + e2:32:4][:, 0:4], OP.is_gt), inc=(self.sV, 1))
                    k.vw()
                    k.op("vector", lambda e, e2=e2: e.tensor_tensor(
                        gt[:, 32 + e2:48:4][:, 0:4],
                        gt[:, 32 + e2:48:4][:, 0:4],
                        gt[:, 48 + e2:64:4][:, 0:4], OP.add), inc=(self.sV, 1))
            k.vw()
            k.op("vector", lambda e: e.tensor_scalar(
                gt[:, 32:48], gt[:, 32:48], 1.0, None, OP.is_le), inc=(self.sV, 1))
            k.vw()
            k.op("vector", lambda e: e.tensor_tensor(
                gt[:, 48:64], gt[:, 16:32], gt[:, 32:48], OP.mult), inc=(self.sV, 1))
            wv2 = gt[:, 48:64].rearrange("p (s e) -> p s e", e=4)
            k.vw()
            k.op("vector", lambda e, wv2=wv2: e.tensor_reduce(
                gt[:, 64:68], wv2, AX.X, OP.add), inc=(self.sV, 1))
            k.vw()
            k.op("vector", lambda e: e.tensor_scalar(
                gt[:, 64:68], gt[:, 64:68], 1e-8, None, OP.add), inc=(self.sV, 1))
            k.vw()
            k.op("vector", lambda e: e.reciprocal(gt[:, 64:68], gt[:, 64:68]),
                 inc=(self.sV, 1))
            for sub in range(4):
                k.vw()
                k.op("vector", lambda e, sub=sub: e.tensor_scalar(
                    gt[:, 48 + sub * 4:48 + (sub + 1) * 4],
                    gt[:, 48 + sub * 4:48 + (sub + 1) * 4],
                    gt[:, 64 + sub:65 + sub], None, OP.mult), inc=(self.sV, 1))
            wtb, wtb_key = self.ar.t16(16)
            k.vw()
            k.op("vector", lambda e, wtb=wtb: e.tensor_copy(wtb[:, 0:16], gt[:, 48:64]),
                 inc=(self.sV, 1))
            k.wait("tensor", self.sV)
            pb3bf = pb[3][:].bitcast(BF16)
            for sub in range(4):
                k.op("tensor", lambda e, sub=sub, wtb=wtb, pb3bf=pb3bf: e.transpose(
                    pb3bf[0:4, sub * 128:(sub + 1) * 128],
                    wtb[:, sub * 4:(sub + 1) * 4], cst["c_idbf"][:]), inc=(self.sT, 1))
            k.wait("vector", self.sT)
            k.vw()
            k.op("vector", lambda e, qc=qc, pb3bf=pb3bf: e.tensor_copy(
                wfm[0:4, qc * 512:(qc + 1) * 512], pb3bf[0:4, 0:512]), inc=(self.sV, 1))
            self.ar.release(wtb_key)
        k.wait("gpsimd", self.sV)
        for e2 in range(4):
            k.dma("gpsimd", self.scr_gw.ap()[e2:e2 + 1, :], wfm[e2:e2 + 1, :], self.sO)
        k.wait("sync", self.sO)
        self.tap("gatew", wfm[0:4, :])
        k.barrier()
        for key in (wgbf_key, wg32_key, gt_key, wfm_key):
            self.ar.release(key)

        mid0, mid0_key = self.ar.t16(16 * 1024)
        mid1h, mid1h_key = self.ar.t16(8 * 1024)
        macc, macc_key = self.ar.t32(6 * 1024)
        wbc, wbc_key = self.ar.t16(1024)
        t32b, t32b_key = self.ar.t32(2 * 512)
        wbf2, wbf2_key = self.ar.t16(2 * 2048)
        slab_cons = [0, 0]
        slab_ready = [0, 0]

        for tch in range(2):
            jid = 0

            def load_slab(wname, mc, ncols=128):
                nonlocal jid
                b = jid % 2
                jid += 1
                k.wait("sync", self.sT, slab_cons[b])
                slab_ready[b] = self._wslab(wname, mc, wbf2[:, b * 2048:(b + 1) * 2048],
                                            self.sWp[b], ncols=ncols)
                k.wait("tensor", self.sWp[b], slab_ready[b])
                return b

            def l1(wname, outbuf, act, ncin=8, src=None, obase=1024):
                nonlocal jid
                for mc in range(dict(WSPEC_F32)[wname][1] // 128):
                    b = load_slab(wname, mc)
                    for q2 in range(2):
                        bank = pb[(jid * 2 + q2) % 2]
                        k.wait("tensor", self.sA)
                        for kc in range(ncin):
                            if src is None:
                                rhs = xfh[:, kc * L + tch * 1024 + q2 * 512:
                                          kc * L + tch * 1024 + (q2 + 1) * 512]
                            else:
                                rhs = src[:, kc * 1024 + q2 * 512:kc * 1024 + (q2 + 1) * 512]
                            k.op("tensor", lambda e, b=b, kc=kc, rhs=rhs, bank=bank: e.matmul(
                                bank[:, :], wbf2[:, b * 2048 + kc * 128:b * 2048 + (kc + 1) * 128],
                                rhs, start=(kc == 0), stop=(kc == ncin - 1)), inc=(self.sT, 1))
                        k.wait("scalar", self.sT)
                        k.op("scalar", lambda e, mc=mc, q2=q2, act=act, bank=bank,
                             outbuf=outbuf, obase=obase: e.activation(
                                 outbuf[:, mc * obase + q2 * 512:mc * obase + (q2 + 1) * 512],
                                 bank[:, :], act), inc=(self.sA, 1))
                    slab_cons[b] = self.sT.n

            def dn(wname, e2, srcbuf, ncin=16):
                nonlocal jid
                k.wait("sync", self.sV)
                wb_ld = k.dma("sync", wbc[:, 0:1024],
                              bass.AP(self.scr_gw, e2 * L + tch * 1024,
                                      [[0, 128], [1, 1024]]), self.sB)
                k.wait("vector", self.sB, wb_ld)
                for mc in range(2, 8):
                    b = load_slab(wname, mc)
                    for q2 in range(2):
                        bank = pb[(jid * 2 + q2) % 2]
                        k.wait("tensor", self.sV)
                        for kc in range(ncin):
                            k.op("tensor", lambda e, b=b, kc=kc, q2=q2, srcbuf=srcbuf,
                                 bank=bank: e.matmul(
                                     bank[:, :],
                                     wbf2[:, b * 2048 + kc * 128:b * 2048 + (kc + 1) * 128],
                                     srcbuf[:, kc * 1024 + q2 * 512:kc * 1024 + (q2 + 1) * 512],
                                     start=(kc == 0), stop=(kc == ncin - 1)), inc=(self.sT, 1))
                        k.wait("vector", self.sT)
                        bt = q2 % 2
                        if e2 == 0:
                            k.op("vector", lambda e, mc=mc, q2=q2, bank=bank: e.tensor_tensor(
                                macc[:, (mc - 2) * 1024 + q2 * 512:(mc - 2) * 1024 + (q2 + 1) * 512],
                                bank[:, :], wbc[:, q2 * 512:(q2 + 1) * 512],
                                OP.mult), inc=(self.sV, 1))
                        else:
                            k.op("vector", lambda e, q2=q2, bt=bt, bank=bank: e.tensor_tensor(
                                t32b[:, bt * 512:(bt + 1) * 512], bank[:, :],
                                wbc[:, q2 * 512:(q2 + 1) * 512], OP.mult), inc=(self.sV, 1))
                            k.vw()
                            k.op("vector", lambda e, mc=mc, q2=q2, bt=bt: e.tensor_tensor(
                                macc[:, (mc - 2) * 1024 + q2 * 512:(mc - 2) * 1024 + (q2 + 1) * 512],
                                macc[:, (mc - 2) * 1024 + q2 * 512:(mc - 2) * 1024 + (q2 + 1) * 512],
                                t32b[:, bt * 512:(bt + 1) * 512], OP.add), inc=(self.sV, 1))
                    slab_cons[b] = self.sT.n

            l1("w_e0up", mid0, AF.Gelu)
            if tch == 0:
                self.tap("mide0", [mid0[:, i * 1024:(i + 1) * 1024] for i in range(16)])
            dn("w_e0dn", 0, mid0)
            if tch == 0:
                self.tap("macc_e0", [macc[:, i * 1024:(i + 1) * 1024] for i in range(6)])
            for mc in range(16):
                ba = load_slab("w_e1a", mc)
                bb = load_slab("w_e1b", mc)
                if tch == 0 and mc == 0:
                    self.k.wait("gpsimd", self.sW)
                    self.tap("we1a0", wbf2[:, ba * 2048:(ba + 1) * 2048])
                    self.tap("we1b0", wbf2[:, bb * 2048:(bb + 1) * 2048])
                for q2 in range(2):
                    bka, bkb = pb[4], pb[5]
                    k.wait("tensor", self.sA)
                    k.wait("tensor", self.sV)
                    for kc in range(8):
                        k.op("tensor", lambda e, ba=ba, kc=kc, q2=q2, bka=bka, tch=tch: e.matmul(
                            bka[:, :], wbf2[:, ba * 2048 + kc * 128:ba * 2048 + (kc + 1) * 128],
                            xfh[:, kc * L + tch * 1024 + q2 * 512:kc * L + tch * 1024 + (q2 + 1) * 512],
                            start=(kc == 0), stop=(kc == 7)), inc=(self.sT, 1))
                    for kc in range(8):
                        k.op("tensor", lambda e, bb=bb, kc=kc, q2=q2, bkb=bkb, tch=tch: e.matmul(
                            bkb[:, :], wbf2[:, bb * 2048 + kc * 128:bb * 2048 + (kc + 1) * 128],
                            xfh[:, kc * L + tch * 1024 + q2 * 512:kc * L + tch * 1024 + (q2 + 1) * 512],
                            start=(kc == 0), stop=(kc == 7)), inc=(self.sT, 1))
                    k.wait("scalar", self.sT)
                    k.wait("scalar", self.sV)
                    if tch == 0 and mc == 0 and q2 == 0 and "pe1raw" in self.taps:
                        k.wait("vector", self.sT)
                        k.op("vector", lambda e, bka=bka: e.tensor_copy(
                            mid1h[:, 0:512], bka[:, :]), inc=(self.sV, 1))
                        k.op("vector", lambda e, bkb=bkb: e.tensor_copy(
                            mid1h[:, 512:1024], bkb[:, :]), inc=(self.sV, 1))
                        k.vw()
                        self.tap("pe1raw", mid1h[:, 0:1024])
                    bt = q2 % 2
                    k.op("scalar", lambda e, bt=bt, bka=bka: e.activation(
                        t32b[:, bt * 512:(bt + 1) * 512], bka[:, :], AF.Silu),
                        inc=(self.sA, 1))
                    k.wait("vector", self.sA)
                    k.op("vector", lambda e, mc=mc, q2=q2, bt=bt, bkb=bkb: e.tensor_tensor(
                        mid0[:, mc * 1024 + q2 * 512:mc * 1024 + (q2 + 1) * 512],
                        t32b[:, bt * 512:(bt + 1) * 512], bkb[:, :], OP.mult),
                        inc=(self.sV, 1))
                slab_cons[ba] = self.sT.n
                slab_cons[bb] = self.sT.n
            if tch == 0:
                self.tap("mide1", [mid0[:, i * 1024:(i + 1) * 1024] for i in range(16)])
            dn("w_e1dn", 1, mid0)
            if tch == 0:
                self.tap("macc_e1", [macc[:, i * 1024:(i + 1) * 1024] for i in range(6)])
            l1("w_e2a", mid0, AF.Gelu)
            for q2h in range(2):
                for mc in range(16):
                    b = load_slab("w_e2b", mc)
                    bank = pb[jid % 2]
                    k.wait("tensor", self.sA)
                    for kc in range(16):
                        k.op("tensor", lambda e, b=b, kc=kc, q2h=q2h, bank=bank: e.matmul(
                            bank[:, :], wbf2[:, b * 2048 + kc * 128:b * 2048 + (kc + 1) * 128],
                            mid0[:, kc * 1024 + q2h * 512:kc * 1024 + (q2h + 1) * 512],
                            start=(kc == 0), stop=(kc == 15)), inc=(self.sT, 1))
                    k.wait("scalar", self.sT)
                    k.op("scalar", lambda e, mc=mc, bank=bank: e.activation(
                        mid1h[:, mc * 512:(mc + 1) * 512], bank[:, :], AF.Gelu),
                        inc=(self.sA, 1))
                    slab_cons[b] = self.sT.n
                for mc in range(16):
                    b = load_slab("w_e2c", mc)
                    bank = pb[jid % 2]
                    k.wait("tensor", self.sA)
                    for kc in range(16):
                        k.op("tensor", lambda e, b=b, kc=kc, bank=bank: e.matmul(
                            bank[:, :], wbf2[:, b * 2048 + kc * 128:b * 2048 + (kc + 1) * 128],
                            mid1h[:, kc * 512:(kc + 1) * 512],
                            start=(kc == 0), stop=(kc == 15)), inc=(self.sT, 1))
                    k.wait("scalar", self.sT)
                    k.op("scalar", lambda e, mc=mc, q2h=q2h, bank=bank: e.activation(
                        mid0[:, mc * 1024 + q2h * 512:mc * 1024 + (q2h + 1) * 512],
                        bank[:, :], AF.Gelu), inc=(self.sA, 1))
                    slab_cons[b] = self.sT.n
            dn("w_e2d", 2, mid0)
            if tch == 0:
                self.tap("macc_e2", [macc[:, i * 1024:(i + 1) * 1024] for i in range(6)])
            l1("w_e3a", mid1h, AF.Gelu, obase=1024)
            l1("w_e3b", mid0, AF.Gelu, ncin=8, src=mid1h)
            dn("w_e3c", 3, mid0)
            if tch == 0:
                self.tap("macc_all", [macc[:, i * 1024:(i + 1) * 1024] for i in range(6)])
            if tch == 1:
                self.tap("macc_t1", [macc[:, i * 1024:(i + 1) * 1024] for i in range(6)])
            for mc in range(8):
                for q2 in range(2):
                    b = (mc * 2 + q2) % 2
                    k.wait("sync", self.sA)
                    k.wait("sync", self.sV)
                    ld = k.dma("sync", t32b[:, b * 512:(b + 1) * 512],
                               self.scr_xffn.ap()[mc * 128:(mc + 1) * 128,
                                                  tch * 1024 + q2 * 512:tch * 1024 + (q2 + 1) * 512],
                               self.sB)
                    if mc >= 2:
                        k.wait("vector", self.sB, ld)
                        k.wait("vector", self.sT)
                        k.op("vector", lambda e, mc=mc, q2=q2, b=b: e.tensor_tensor(
                            t32b[:, b * 512:(b + 1) * 512], t32b[:, b * 512:(b + 1) * 512],
                            macc[:, (mc - 2) * 1024 + q2 * 512:(mc - 2) * 1024 + (q2 + 1) * 512],
                            OP.add), inc=(self.sV, 1))
                        k.wait("scalar", self.sV)
                    else:
                        k.wait("scalar", self.sB, ld)
                    k.op("scalar", lambda e, mc=mc, tch=tch, q2=q2, b=b: e.activation(
                        self.gbf[mc][:, tch * 1024 + q2 * 512:tch * 1024 + (q2 + 1) * 512],
                        t32b[:, b * 512:(b + 1) * 512], AF.Silu), inc=(self.sA, 1))
        self.tap("g", [self.gbf[i][:] for i in range(8)])
        k.barrier()
        for key in (xfh_key, mid0_key, mid1h_key, macc_key, wbc_key, t32b_key,
                    wbf2_key):
            self.ar.release(key)

    def phase_mamba(self, d):
        k, cst, pb = self.k, self.cst, self.pb
        fwd = (d == "f")
        ext, ext_key = self.ar.t32(16 * (TC + 3))
        ubf, ubf_key = self.ar.t16(16 * TC)
        Lt, Lt_key = self.ar.t32(16 * TC)
        dtu, dtu_key = self.ar.t16(16 * TC)
        sgb, sgb_key = self.ar.t16(16 * TC)
        hbuf, hbuf_key = self.ar.t32(16 * TC)
        yg, yg_key = self.ar.t16(16 * TC)
        yt, yt_key = self.ar.t32(2 * TC)
        ab, ab_key = self.ar.t32(4 * TC)
        stk, stk_key = self.ar.t16(2 * 16 * TC)
        dtv, dtv_key = self.ar.t16(2 * TC)
        bc16, bc16_key = self.ar.t16(TC)
        wsl, wsl_key = self.ar.t16(2 * 2048)
        sfb, sfb_key = self.ar.t16(TC)
        xre, xre_key = self.ar.t32(TC)
        sfr, sfr_key = self.ar.t16(TC)
        state = self.state
        k.op("vector", lambda e: e.memset(state[:], 0.0), inc=(self.sV, 1))
        for dc in range(16):
            k.op("vector", lambda e, dc=dc: e.memset(
                ext[:, dc * (TC + 3):dc * (TC + 3) + (TC + 3)], 0.0), inc=(self.sV, 1))
        k.barrier(("vector", "tensor", "scalar", "sync", "gpsimd"))

        convw = cst["convw_" + d]
        convb = cst["convb_" + d]
        aexp = self.aexp[d]
        ndtb = self.ndtb[d]
        dvec = cst["dvec_" + d]
        xp = self.xpbf[d]
        dtp = self.dtpbf[d]
        chunks = list(range(NCH)) if fwd else list(range(NCH - 1, -1, -1))
        slab_cons = [0, 0]
        stk_cons = 0

        for ci, c in enumerate(chunks):
            tsl = slice(c * TC, (c + 1) * TC)
            if ci > 0:
                for dc in range(16):
                    eb = ext[:, dc * (TC + 3):(dc + 1) * (TC + 3)]
                    if fwd:
                        k.op("vector", lambda e, eb=eb: e.tensor_copy(
                            eb[:, 0:3], eb[:, TC:TC + 3]), inc=(self.sV, 1))
                    else:
                        k.op("vector", lambda e, eb=eb: e.tensor_copy(
                            eb[:, TC:TC + 3], eb[:, 0:3]), inc=(self.sV, 1))
            hist_saved = self.sV.n
            jid = 0
            evac_done = 0
            for mcp in range(8):
                b = mcp % 2
                k.wait("gpsimd", self.sT, slab_cons[b])
                for kc in range(8):
                    k.dma("gpsimd", wsl[:, b * 2048 + kc * 256:b * 2048 + (kc + 1) * 256],
                          self.scr_wip.ap()[kc * 128:(kc + 1) * 128,
                                            mcp * 256:(mcp + 1) * 256], self.sXp[b])
                wld = self.sXp[b].n
                k.wait("tensor", self.sXp[b], wld)
                for m2 in range(2):
                    mc = mcp * 2 + m2
                    bank = pb[jid % 2]
                    k.wait("tensor", self.sV, evac_done if jid >= 2 else 0)
                    if jid < 2:
                        k.wait("tensor", self.sV, hist_saved)
                        k.wait("tensor", self.sA)
                    for kc in range(8):
                        k.op("tensor", lambda e, b=b, kc=kc, m2=m2, bank=bank, tsl=tsl: e.matmul(
                            bank[:, 0:TC], wsl[:, b * 2048 + kc * 256 + m2 * 128:
                                               b * 2048 + kc * 256 + (m2 + 1) * 128],
                            self.xnbf[kc][:, tsl], start=(kc == 0), stop=(kc == 7)),
                            inc=(self.sT, 1))
                    k.wait("vector", self.sT)
                    k.op("vector", lambda e, mc=mc, bank=bank: e.tensor_copy(
                        ext[:, mc * (TC + 3) + (3 if fwd else 0):
                            mc * (TC + 3) + (3 if fwd else 0) + TC], bank[:, 0:TC]),
                        inc=(self.sV, 1))
                    evac_done = self.sV.n
                    jid += 1
                slab_cons[b] = self.sT.n
            k.wait("vector", self.sV, evac_done)
            k.wait("vector", self.sO)
            conv_done = []
            for dc in range(16):
                eb = ext[:, dc * (TC + 3):(dc + 1) * (TC + 3)]
                tt = yt[:, (dc % 2) * TC:(dc % 2) * TC + TC]
                if len(conv_done) >= 2:
                    k.wait("vector", self.sA, conv_done[-2])
                wcol = lambda kk2, dc=dc: convw[:, dc * 4 + kk2:dc * 4 + kk2 + 1]
                offs = (lambda kk2: kk2) if fwd else (lambda kk2: 3 - kk2)
                k.op("vector", lambda e, eb=eb, tt=tt, wcol=wcol, offs=offs: e.tensor_scalar(
                    tt, eb[:, offs(0):offs(0) + TC], wcol(0), None, OP.mult), inc=(self.sV, 1))
                for kk2 in range(1, 4):
                    k.vw()
                    k.op("vector", lambda e, eb=eb, tt=tt, wcol=wcol, offs=offs, kk2=kk2:
                         e.scalar_tensor_tensor(tt, eb[:, offs(kk2):offs(kk2) + TC],
                                                wcol(kk2), tt, OP.mult, OP.add),
                         inc=(self.sV, 1))
                k.wait("scalar", self.sV)
                k.op("scalar", lambda e, dc=dc, tt=tt: e.activation(
                    ubf[:, dc * TC:(dc + 1) * TC], tt, AF.Silu,
                    bias=convb[:, dc:dc + 1]), inc=(self.sA, 1))
                conv_done.append(self.sA.n)
            usil = self.sA.n
            for mcp in range(8, 16):
                b = mcp % 2
                k.wait("gpsimd", self.sT, slab_cons[b])
                for kc in range(8):
                    k.dma("gpsimd", wsl[:, b * 2048 + kc * 256:b * 2048 + (kc + 1) * 256],
                          self.scr_wip.ap()[kc * 128:(kc + 1) * 128,
                                            mcp * 256:(mcp + 1) * 256], self.sXp[b])
                wld = self.sXp[b].n
                k.wait("tensor", self.sXp[b], wld)
                for m2 in range(2):
                    mc = (mcp - 8) * 2 + m2
                    bank = pb[jid % 2]
                    k.wait("tensor", self.sA)
                    k.wait("tensor", self.sV, evac_done)
                    for kc in range(8):
                        k.op("tensor", lambda e, b=b, kc=kc, m2=m2, bank=bank, tsl=tsl: e.matmul(
                            bank[:, 0:TC], wsl[:, b * 2048 + kc * 256 + m2 * 128:
                                               b * 2048 + kc * 256 + (m2 + 1) * 128],
                            self.xnbf[kc][:, tsl], start=(kc == 0), stop=(kc == 7)),
                            inc=(self.sT, 1))
                    k.wait("scalar", self.sT)
                    k.op("scalar", lambda e, mc=mc, bank=bank: e.activation(
                        sgb[:, mc * TC:(mc + 1) * TC], bank[:, 0:TC], AF.Silu),
                        inc=(self.sA, 1))
                    jid += 1
                slab_cons[b] = self.sT.n
            k.wait("tensor", self.sA, usil)
            for dc in range(16):
                k.op("tensor", lambda e, dc=dc: e.matmul(
                    pb[4][0:96, 0:TC], xp[:, dc * 96:(dc + 1) * 96],
                    ubf[:, dc * TC:(dc + 1) * TC], start=(dc == 0), stop=(dc == 15)),
                    inc=(self.sT, 1))
            xdblv = self.sT.n
            k.wait("vector", self.sT, xdblv)
            k.op("vector", lambda e: e.tensor_copy(dtv[0:DTR, 0:TC], pb[4][0:DTR, 0:TC]),
                 inc=(self.sV, 1))
            k.wait("scalar", self.sT, xdblv)
            k.op("scalar", lambda e: e.activation(
                bc16[DTR:96, 0:TC], pb[4][DTR:96, 0:TC], AF.Copy,
                scale=cst["c_negmix"][DTR:96, 0:1]), inc=(self.sA, 1))
            bcv = self.sA.n
            bcb = ci % 2
            k.wait("scalar", self.sA)
            sp = k.dma("scalar", self.scr_bc.ap()[bcb:bcb + 1, :].rearrange(
                "a (r t) -> (a r) t", t=TC), bc16[DTR:96, 0:TC], self.sD)
            k.wait("gpsimd", self.sD, sp)
            k.wait("gpsimd", self.sV, stk_cons)
            k.dma("gpsimd", stk[:, 0:16 * TC],
                  bass.AP(self.scr_bc, bcb * 32 * TC, [[0, 128], [1, 16 * TC]]), self.sC)
            k.dma("gpsimd", stk[:, 16 * TC:32 * TC],
                  bass.AP(self.scr_bc, bcb * 32 * TC + 16 * TC, [[0, 128], [1, 16 * TC]]),
                  self.sC)
            stks = self.sC.n
            if ci == 0 and fwd and "stk0" in self.taps:
                self.k.wait("gpsimd", self.sX)
                self.tap("stk0", stk[:, 0:32 * TC])
            dtmm = []
            k.wait("tensor", self.sV)
            k.wait("tensor", self.sA, bcv)
            for mc in range(16):
                bank = pb[4 + mc % 2]
                if mc >= 2:
                    k.wait("tensor", self.sA, dtmm[-2])
                k.op("tensor", lambda e, mc=mc, bank=bank: e.matmul(
                    bank[:, 0:TC], dtp[:, mc * 128:(mc + 1) * 128], dtv[0:DTR, 0:TC],
                    start=True, stop=True), inc=(self.sT, 1))
                k.wait("scalar", self.sT)
                k.op("scalar", lambda e, mc=mc, bank=bank: e.activation(
                    Lt[:, mc * TC:(mc + 1) * TC], bank[:, 0:TC], AF.Sigmoid,
                    bias=ndtb[:, mc:mc + 1], scale=cst["c_sc"][:, 4:5]), inc=(self.sA, 1))
                dtmm.append(self.sA.n)
            for mc in range(16):
                k.op("scalar", lambda e, mc=mc: e.activation(
                    Lt[:, mc * TC:(mc + 1) * TC], Lt[:, mc * TC:(mc + 1) * TC],
                    AF.Ln), inc=(self.sA, 1))
            lnv = self.sA.n
            k.wait("vector", self.sA, lnv)
            for dc in range(16):
                k.op("vector", lambda e, dc=dc: e.tensor_tensor(
                    dtu[:, dc * TC:(dc + 1) * TC], Lt[:, dc * TC:(dc + 1) * TC],
                    ubf[:, dc * TC:(dc + 1) * TC], OP.mult), inc=(self.sV, 1))
            dtuv = self.sV.n
            k.wait("vector", self.sC, stks)
            k.wait("scalar", self.sA, lnv)
            scan_done = []
            for dc in range(16):
                for n in range(NS):
                    rr = n % 2
                    a_t = ab[:, rr * TC:(rr + 1) * TC]
                    b_t = ab[:, (2 + rr) * TC:(3 + rr) * TC]
                    if len(scan_done) >= 2:
                        k.wait("scalar", self.sV, scan_done[-2])
                    k.op("scalar", lambda e, dc=dc, n=n, a_t=a_t: e.activation(
                        a_t, Lt[:, dc * TC:(dc + 1) * TC], AF.Exp,
                        scale=aexp[:, dc * 16 + n:dc * 16 + n + 1]), inc=(self.sA, 1))
                    k.op("vector", lambda e, dc=dc, n=n, b_t=b_t: e.tensor_tensor(
                        b_t, dtu[:, dc * TC:(dc + 1) * TC],
                        stk[:, n * TC:(n + 1) * TC], OP.mult), inc=(self.sV, 1))
                    k.wait("vector", self.sA)
                    k.vw()
                    if ci == 0 and fwd and dc == 0 and n == 0 and "ab0" in self.taps:
                        k.op("vector", lambda e, a_t=a_t: e.tensor_copy(
                            yg[:, 0:TC], a_t), inc=(self.sV, 1))
                        k.op("vector", lambda e, b_t=b_t: e.tensor_copy(
                            yg[:, TC:2 * TC], b_t), inc=(self.sV, 1))
                        k.vw()
                        self.tap("ab0", yg[:, 0:2 * TC])
                    hsl = hbuf[:, n * TC:(n + 1) * TC]
                    if fwd:
                        k.op("vector", lambda e, hsl=hsl, a_t=a_t, b_t=b_t, dc=dc, n=n:
                             e.tensor_tensor_scan(hsl, a_t, b_t,
                                                  state[:, dc * 16 + n:dc * 16 + n + 1],
                                                  OP.mult, OP.add), inc=(self.sV, 1))
                    else:
                        k.op("vector", lambda e, hsl=hsl, a_t=a_t, b_t=b_t, dc=dc, n=n:
                             e.tensor_tensor_scan(hsl[:, ::-1], a_t[:, ::-1], b_t[:, ::-1],
                                                  state[:, dc * 16 + n:dc * 16 + n + 1],
                                                  OP.mult, OP.add), inc=(self.sV, 1))
                    scan_done.append(self.sV.n)
                if ci == 0 and fwd and dc == 0 and "h0" in self.taps:
                    k.vw()
                    self.tap("h0", hbuf[:, 0:16 * TC])
                lastcol = TC - 1 if fwd else 0
                k.vw()
                k.op("vector", lambda e, dc=dc, lastcol=lastcol: e.tensor_copy(
                    state[:, dc * 16:(dc + 1) * 16], hbuf[:, lastcol::TC]),
                    inc=(self.sV, 1))
                for n in range(NS):
                    k.op("vector", lambda e, n=n: e.tensor_tensor(
                        hbuf[:, n * TC:(n + 1) * TC], hbuf[:, n * TC:(n + 1) * TC],
                        stk[:, (16 + n) * TC:(16 + n + 1) * TC], OP.mult), inc=(self.sV, 1))
                ytile = yt[:, (dc % 2) * TC:(dc % 2) * TC + TC]
                hv = hbuf[:, 0:16 * TC].rearrange("p (n t) -> p t n", n=16)
                k.vw()
                k.op("vector", lambda e, hv=hv, ytile=ytile: e.tensor_reduce(
                    ytile, hv, AX.X, OP.add), inc=(self.sV, 1))
                k.vw()
                k.op("vector", lambda e, dc=dc, ytile=ytile: e.scalar_tensor_tensor(
                    ytile, ubf[:, dc * TC:(dc + 1) * TC], dvec[:, dc:dc + 1],
                    ytile, OP.mult, OP.add), inc=(self.sV, 1))
                k.vw()
                k.op("vector", lambda e, dc=dc, ytile=ytile: e.tensor_tensor(
                    yg[:, dc * TC:(dc + 1) * TC], ytile, sgb[:, dc * TC:(dc + 1) * TC],
                    OP.mult), inc=(self.sV, 1))
            stk_cons = self.sV.n
            ygv = self.sV.n
            if ci == 0 and fwd and "yg0" in self.taps:
                self.tap("ub0", ubf[:, 0:16 * TC])
                self.tap("lt0", Lt[:, 0:16 * TC])
                self.tap("sg0", sgb[:, 0:16 * TC])
                self.tap("yg0", yg[:, 0:16 * TC])
            k.wait("tensor", self.sV, ygv)
            prev_comb = 0
            opbank_cons = [0, 0]
            for mc in range(8):
                b = mc % 2
                k.wait("gpsimd", self.sT, slab_cons[b])
                for kc in range(16):
                    k.dma("gpsimd", wsl[:, b * 2048 + kc * 128:b * 2048 + (kc + 1) * 128],
                          self.scr_wop.ap()[kc * 128:(kc + 1) * 128,
                                            mc * 128:(mc + 1) * 128], self.sXp[b])
                wld = self.sXp[b].n
                k.wait("tensor", self.sXp[b], wld)
                bank = pb[6 + mc % 2]
                k.wait("tensor", self.sV, opbank_cons[mc % 2])
                for kc in range(16):
                    k.op("tensor", lambda e, b=b, kc=kc, bank=bank: e.matmul(
                        bank[:, 0:TC], wsl[:, b * 2048 + kc * 128:b * 2048 + (kc + 1) * 128],
                        yg[:, kc * TC:(kc + 1) * TC], start=(kc == 0), stop=(kc == 15)),
                        inc=(self.sT, 1))
                smm = self.sT.n
                slab_cons[b] = smm
                if fwd:
                    k.wait("vector", self.sT, smm)
                    k.wait("vector", self.sO)
                    k.op("vector", lambda e, bank=bank: e.tensor_copy(
                        sfb[:, 0:TC], bank[:, 0:TC]), inc=(self.sV, 1))
                    opbank_cons[mc % 2] = self.sV.n
                    k.wait("gpsimd", self.sV)
                    k.dma("gpsimd", self.scr_sf.ap()[mc * 128:(mc + 1) * 128, tsl],
                          sfb[:, 0:TC], self.sO)
                else:
                    k.wait("gpsimd", self.sV, prev_comb)
                    ldx = k.dma("gpsimd", xre[:, 0:TC],
                                self.dx.ap()[mc * 128:(mc + 1) * 128, tsl], self.sX)
                    lds = k.dma("gpsimd", sfr[:, 0:TC],
                                self.scr_sf.ap()[mc * 128:(mc + 1) * 128, tsl], self.sX)
                    k.wait("vector", self.sT, smm)
                    k.wait("vector", self.sX, lds)
                    k.wait("vector", self.sO)
                    k.op("vector", lambda e, bank=bank: e.tensor_tensor(
                        yt[:, 0:TC], bank[:, 0:TC], sfr[:, 0:TC], OP.add), inc=(self.sV, 1))
                    opbank_cons[mc % 2] = self.sV.n
                    k.vw()
                    k.op("vector", lambda e, mc=mc, tsl=tsl: e.tensor_tensor(
                        yt[:, 0:TC], yt[:, 0:TC], self.gbf[mc][:, tsl], OP.mult),
                        inc=(self.sV, 1))
                    k.vw()
                    k.op("vector", lambda e: e.tensor_tensor(
                        yt[:, 0:TC], yt[:, 0:TC], xre[:, 0:TC], OP.add), inc=(self.sV, 1))
                    prev_comb = self.sV.n
                    k.wait("gpsimd", self.sV, prev_comb)
                    k.dma("gpsimd", self.dout.ap()[mc * 128:(mc + 1) * 128, tsl],
                          yt[:, 0:TC], self.sO)
            if CHUNK_BARRIER:
                k.barrier()
        k.barrier()
        for key in (ext_key, ubf_key, Lt_key, dtu_key, sgb_key, hbuf_key, yg_key,
                    yt_key, ab_key, stk_key, dtv_key, bc16_key, wsl_key, sfb_key,
                    xre_key, sfr_key):
            self.ar.release(key)


_CACHE = {}


def _get_nc(taps=()):
    key = tuple(sorted(taps))
    if key not in _CACHE:
        p = Prog(taps)
        nc = p.build()
        _CACHE[key] = (p, nc)
    return _CACHE[key]


def kernel(**inputs):
    sh, percore = host_arrays(inputs)
    p, nc = _get_nc()
    in_maps = []
    for c in range(8):
        m = dict(sh)
        m["x_fm"] = percore[c]["x_fm"]
        in_maps.append(m)
    res = run_bass_kernel_spmd(nc, in_maps, core_ids=list(range(8)))
    outs = [res.results[c]["out_fm"].T for c in range(8)]
    return np.ascontiguousarray(np.stack(outs)).astype(np.float32)


# version 08
# speedup vs baseline: 1.0213x; 1.0213x over previous
import numpy as np
import ml_dtypes
from contextlib import ExitStack

import concourse.bass as bass
import concourse.mybir as mybir
from concourse.alu_op_type import AluOpType as OP
from concourse.bass_utils import run_bass_kernel_spmd

AF = mybir.ActivationFunctionType
AX = mybir.AxisListType
F32 = mybir.dt.float32
BF16 = mybir.dt.bfloat16
NPBF = ml_dtypes.bfloat16

D = 1024
DI = 2048
NS = 16
DTR = 64
L = 2048
H, HD = 16, 64
TC = 256
NCH = L // TC
EPS_RMS = float(np.float32(1.1920929e-07))
EPS_LN = 1e-5
NEG = -1e30

ENGINES = ("sync", "tensor", "vector", "scalar", "gpsimd")
CHUNK_BARRIER = False


class Sem:
    __slots__ = ("h", "n", "name")
    def __init__(self, h, name):
        self.h, self.n, self.name = h, 0, name


class Arena:
    def __init__(self, tensor, cols):
        self.t = tensor
        self.cols = cols
        self.free = [(0, cols)]
        self.peak = 0

    def _alloc(self, n, align):
        for i, (off, sz) in enumerate(self.free):
            a = (off + align - 1) // align * align
            pad = a - off
            if sz - pad >= n:
                ent = []
                if pad:
                    ent.append((off, pad))
                if sz - pad - n:
                    ent.append((a + n, sz - pad - n))
                self.free[i:i + 1] = ent
                self.peak = max(self.peak, self.cols - sum(s for _, s in self.free))
                return a
        raise MemoryError(f"arena OOM want {n} free={self.free}")

    def release(self, key):
        off, n = key
        self.free.append((off, n))
        self.free.sort()
        m = []
        for o, s in self.free:
            if m and m[-1][0] + m[-1][1] == o:
                m[-1] = (m[-1][0], m[-1][1] + s)
            else:
                m.append((o, s))
        self.free = m

    def t16(self, n):
        off = self._alloc(n, 16)
        return self.t[:, off:off + n], (off, n)

    def t32(self, n):
        off = self._alloc(2 * n, 16)
        return self.t[:, off:off + 2 * n].bitcast(F32), (off, 2 * n)


class KB:
    def __init__(self):
        self.nc = bass.Bass("TRN2", target_bir_lowering=False)
        self.ctx = ExitStack()
        self.streams = {e: [] for e in ENGINES}
        self.sems = {}

    def sem(self, name):
        s = Sem(self.ctx.enter_context(self.nc.semaphore(name)), name)
        self.sems[name] = s
        return s

    def sb(self, name, shape, dtype):
        return self.ctx.enter_context(self.nc.sbuf_tensor(name, list(shape), dtype))

    def ps(self, name, shape, dtype=F32):
        return self.ctx.enter_context(self.nc.psum_tensor(name, list(shape), dtype))

    def op(self, eng, fn, inc=None):
        if inc is not None:
            sem, kk = inc
            sem.n += kk
            def run(e, fn=fn, sem=sem, kk=kk):
                fn(e).then_inc(sem.h, kk)
        else:
            def run(e, fn=fn):
                fn(e)
        self.streams[eng].append(run)

    def vw(self):
        self.wait("vector", self.sems["sV"])

    def wait(self, eng, sem, val=None):
        v = sem.n if val is None else val
        assert v <= sem.n, f"{eng} waits {sem.name}>={v} planned {sem.n}"
        if v > 0:
            self.streams[eng].append(lambda e, s=sem, v=v: e.wait_ge(s.h, v))

    def dma(self, eng, out, in_, sem, slow=False):
        if slow:
            self.op(eng, lambda e, o=out, i=in_: e.dma_start(
                out=o, in_=i, allow_slow_non_contiguous=True), inc=(sem, 16))
        else:
            self.op(eng, lambda e, o=out, i=in_: e.dma_start(out=o, in_=i), inc=(sem, 16))
        return sem.n

    def barrier(self, engines=ENGINES):
        if not hasattr(self, "dma_owner"):
            snap = [(s, s.n) for s in self.sems.values() if s.n > 0]
            for e in engines:
                for s, v in snap:
                    self.wait(e, s, v)
            return
        comp = [self.sems[n] for n in ("sT", "sV", "sA", "sG") if n in self.sems]
        for e in ENGINES:
            for s in self.dma_owner[e]:
                if s.n > 0:
                    self.wait(e, s, s.n)
            self.streams[e].append(lambda eng, b=self.bar: eng.sem_inc(b.h, 1))
            self.bar.n += 1
        tgt = self.bar.n
        for e in ENGINES:
            self.wait(e, self.bar, tgt)
            for s in comp:
                if s.n > 0:
                    self.wait(e, s, s.n)

    def finalize(self):
        with self.nc.Block() as blk:
            for name in ENGINES:
                def body(eng, name=name):
                    for f in self.streams[name]:
                        f(eng)
                getattr(blk, name)(body)
        return self.nc



def host_arrays(inputs):
    f32 = np.float32
    def T(a):
        return np.ascontiguousarray(np.asarray(a, f32).T)
    def chanvec(v, d):
        return np.ascontiguousarray(np.asarray(v, f32).reshape(d // 128, 128).T)

    sh = {
        "w_inproj": T(inputs["L0_attn_in_proj_w"]),
        "w_xproj_f": T(inputs["L0_attn_x_proj_w"]),
        "w_xproj_b": T(inputs["L0_attn_x_proj_wb"]),
        "w_dtproj_f": T(inputs["L0_attn_dt_proj_w"]),
        "w_dtproj_b": T(inputs["L0_attn_dt_proj_wb"]),
        "w_outproj": T(inputs["L0_attn_out_proj_w"]),
        "convw_f": np.ascontiguousarray(np.asarray(inputs["L0_attn_conv_w"], f32)
                                        .reshape(16, 128, 4).transpose(1, 0, 2).reshape(128, 64)),
        "convw_b": np.ascontiguousarray(np.asarray(inputs["L0_attn_conv_wb"], f32)
                                        .reshape(16, 128, 4).transpose(1, 0, 2).reshape(128, 64)),
        "convb_f": chanvec(inputs["L0_attn_conv_b"], DI),
        "convb_b": chanvec(inputs["L0_attn_conv_bb"], DI),
        "dtb_f": chanvec(inputs["L0_attn_dt_proj_b"], DI),
        "dtb_b": chanvec(inputs["L0_attn_dt_proj_bb"], DI),
        "alog_f": np.ascontiguousarray(np.asarray(inputs["L0_attn_A_log"], f32)
                                       .reshape(16, 128, NS).transpose(1, 0, 2).reshape(128, 256)),
        "alog_b": np.ascontiguousarray(np.asarray(inputs["L0_attn_A_b_log"], f32)
                                       .reshape(16, 128, NS).transpose(1, 0, 2).reshape(128, 256)),
        "dvec_f": chanvec(inputs["L0_attn_D"], DI),
        "dvec_b": chanvec(inputs["L0_attn_D_b"], DI),
        "w_q": T(inputs["L0_ffn_wq"]), "w_k": T(inputs["L0_ffn_wk"]),
        "w_v": T(inputs["L0_ffn_wv"]), "w_o": T(inputs["L0_ffn_wo"]),
        "w_gate": T(inputs["L0_ffn_gate_w"]),
        "n1w": chanvec(inputs["L0_ffn_n1_w"], D),
        "n2w": chanvec(inputs["L0_ffn_n2_w"], D),
        "qnw": np.ascontiguousarray(np.tile(np.asarray(inputs["L0_ffn_qn_w"], f32), 2)[:, None]),
        "knw": np.ascontiguousarray(np.tile(np.asarray(inputs["L0_ffn_kn_w"], f32), 2)[:, None]),
        "qnrow": np.asarray(inputs["L0_ffn_qn_w"], f32)[None, :].copy(),
        "knrow": np.asarray(inputs["L0_ffn_kn_w"], f32)[None, :].copy(),
        "w_e0up": T(inputs["L0_ffn_e0_up"]), "w_e0dn": T(inputs["L0_ffn_e0_down"]),
        "w_e1a": T(inputs["L0_ffn_e1_w1"]), "w_e1b": T(inputs["L0_ffn_e1_w2"]),
        "w_e1dn": T(inputs["L0_ffn_e1_down"]),
        "w_e2a": T(inputs["L0_ffn_e2_l1"]), "w_e2b": T(inputs["L0_ffn_e2_l2"]),
        "w_e2c": T(inputs["L0_ffn_e2_l3"]), "w_e2d": T(inputs["L0_ffn_e2_l4"]),
        "w_e3a": T(inputs["L0_ffn_e3_down"]), "w_e3b": T(inputs["L0_ffn_e3_up"]),
        "w_e3c": T(inputs["L0_ffn_e3_out"]),
        "c_id4": np.eye(4, dtype=f32),
    }
    kk, qq = np.meshgrid(np.arange(128), np.arange(128), indexing="ij")
    sh["c_maskT"] = np.where(kk <= qq, 0.0, NEG).astype(f32)
    sh["c_ones"] = np.ones((128, 128), NPBF)
    bd = np.zeros((128, 2), NPBF); bd[:64, 0] = 1; bd[64:, 1] = 1
    sh["c_bd64"] = bd
    ex = np.zeros((2, 128), NPBF); ex[0, :64] = 1; ex[1, 64:] = 1
    sh["c_ex64"] = ex
    z1 = np.zeros((128, 65), NPBF); z1[:, 64] = 1
    sh["c_z64o"] = z1
    hl = np.zeros((128, 128), NPBF); hl[:, :64] = 1
    sh["c_hlo"] = hl
    hh = np.zeros((128, 128), NPBF); hh[:, 64:] = 1
    sh["c_hhi"] = hh
    sh["c_idbf"] = np.eye(128, dtype=NPBF)
    sh["c_sc"] = np.tile(np.array([EPS_LN, EPS_RMS, 1.0 / HD, 1.0 / D,
                                   -1.0, 15.0, 1.0, 0.0], f32), (128, 1))
    nm = np.ones((128, 1), f32); nm[64:80] = -1.0
    sh["c_negmix"] = nm
    x = np.asarray(inputs["L0_attn_hidden_states"], f32)
    percore = [{"x_fm": np.ascontiguousarray(x[c].T)} for c in range(x.shape[0])]
    return sh, percore


WSPEC_F32 = [
    ("w_inproj", [D, 2 * DI]), ("w_xproj_f", [DI, 96]), ("w_xproj_b", [DI, 96]),
    ("w_dtproj_f", [DTR, DI]), ("w_dtproj_b", [DTR, DI]), ("w_outproj", [DI, D]),
    ("convw_f", [128, 64]), ("convw_b", [128, 64]),
    ("convb_f", [128, 16]), ("convb_b", [128, 16]),
    ("dtb_f", [128, 16]), ("dtb_b", [128, 16]),
    ("alog_f", [128, 256]), ("alog_b", [128, 256]),
    ("dvec_f", [128, 16]), ("dvec_b", [128, 16]),
    ("w_q", [D, D]), ("w_k", [D, D]), ("w_v", [D, D]), ("w_o", [D, D]),
    ("w_gate", [D, 4]), ("n1w", [128, 8]), ("n2w", [128, 8]),
    ("qnw", [128, 1]), ("knw", [128, 1]), ("qnrow", [1, HD]), ("knrow", [1, HD]),
    ("w_e0up", [D, DI]), ("w_e0dn", [DI, D]),
    ("w_e1a", [D, DI]), ("w_e1b", [D, DI]), ("w_e1dn", [DI, D]),
    ("w_e2a", [D, DI]), ("w_e2b", [DI, DI]), ("w_e2c", [DI, DI]), ("w_e2d", [DI, D]),
    ("w_e3a", [D, D]), ("w_e3b", [D, DI]), ("w_e3c", [DI, D]),
    ("c_maskT", [128, 128]), ("c_id4", [4, 4]), ("c_sc", [128, 8]),
    ("c_negmix", [128, 1]),
]
WSPEC_BF = [("c_ones", [128, 128]), ("c_bd64", [128, 2]), ("c_ex64", [2, 128]),
            ("c_z64o", [128, 65]), ("c_hlo", [128, 128]), ("c_hhi", [128, 128]),
            ("c_idbf", [128, 128])]

SMALL = ["convw_f", "convw_b", "convb_f", "convb_b", "dtb_f", "dtb_b",
         "alog_f", "alog_b", "dvec_f", "dvec_b", "n1w", "n2w",
         "qnw", "knw", "qnrow", "knrow", "c_maskT", "c_id4", "c_sc", "c_negmix"]


class Prog:
    def __init__(self, taps=()):
        self.k = KB()
        self.taps = set(taps)
        self.dbg = {}
        nc = self.k.nc
        self.dx = nc.dram_tensor("x_fm", [D, L], F32, kind="ExternalInput")
        self.dout = nc.dram_tensor("out_fm", [D, L], F32, kind="ExternalOutput")
        self.dw = {n: nc.dram_tensor(n, s, F32, kind="ExternalInput") for n, s in WSPEC_F32}
        for n, s in WSPEC_BF:
            self.dw[n] = nc.dram_tensor(n, s, BF16, kind="ExternalInput")
        self.scr_xn = nc.dram_tensor("scr_xn", [D, L], F32, kind="Internal")
        self.scr_xffn = nc.dram_tensor("scr_xffn", [D, L], F32, kind="Internal")
        self.scr_sf = nc.dram_tensor("scr_sf", [D, L], BF16, kind="Internal")
        self.scr_bc = nc.dram_tensor("scr_bc", [2, 32 * TC], BF16, kind="Internal")
        self.scr_row = nc.dram_tensor("scr_row", [8, L], F32, kind="Internal")
        self.scr_gw = nc.dram_tensor("scr_gw", [4, L], BF16, kind="Internal")
        self.scr_wip = nc.dram_tensor("scr_wip", [D, 2 * DI], BF16, kind="Internal")
        self.scr_wop = nc.dram_tensor("scr_wop", [DI, D], BF16, kind="Internal")
        self.scrw = {}
        for n, s, fold in [
            ("w_q", [D, D], 1), ("w_k", [D, D], 1), ("w_v", [D, D], 1),
            ("w_o", [D, D], 0),
            ("w_e0up", [D, DI], 2), ("w_e0dn", [DI, D], 0),
            ("w_e1a", [D, DI], 2), ("w_e1b", [D, DI], 2), ("w_e1dn", [DI, D], 0),
            ("w_e2a", [D, DI], 2), ("w_e2b", [DI, DI], 0), ("w_e2c", [DI, DI], 0),
            ("w_e2d", [DI, D], 0),
            ("w_e3a", [D, D], 2), ("w_e3b", [D, DI], 0), ("w_e3c", [DI, D], 0),
        ]:
            self.scrw[n] = (nc.dram_tensor("scrw_" + n, s, BF16, kind="Internal"), fold)

        k = self.k
        self.sW = k.sem("sW")
        self.sX = k.sem("sX")
        self.sT = k.sem("sT")
        self.sV = k.sem("sV")
        self.sA = k.sem("sA")
        self.sG = k.sem("sG")
        self.sO = k.sem("sO")
        self.sWp = [k.sem("sWa"), k.sem("sWb")]
        self.sXp = [k.sem("sXa"), k.sem("sXb")]
        self.sB = k.sem("sB")
        self.sC = k.sem("sC")
        self.sD = k.sem("sD")
        self.sBar = k.sem("sBar")
        k.dma_owner = {"sync": [self.sW, self.sWp[0], self.sWp[1], self.sB],
                       "gpsimd": [self.sX, self.sXp[0], self.sXp[1], self.sO, self.sC],
                       "scalar": [self.sD], "vector": [], "tensor": []}
        k.bar = self.sBar

        self.pb = [k.ps(f"pb{i}", [128, 512]) for i in range(8)]

        self.xnbf = [k.sb(f"xnbf{i}", [128, L], BF16) for i in range(8)]
        self.gbf = [k.sb(f"gbf{i}", [128, L], BF16) for i in range(8)]
        self.cst = {}
        for n, s in WSPEC_BF:
            self.cst[n] = k.sb(n + "_s", s, BF16)
        for n in SMALL:
            self.cst[n] = k.sb(n + "_s", dict(WSPEC_F32)[n], F32)
        self.aexp = {"f": k.sb("aexp_f", [128, 256], F32),
                     "b": k.sb("aexp_b", [128, 256], F32)}
        self.ndtb = {"f": k.sb("ndtb_f", [128, 16], F32),
                     "b": k.sb("ndtb_b", [128, 16], F32)}
        self.qnw8 = k.sb("qnw8", [128, 1], F32)
        self.bcap = k.sb("bcap", [128, 1], F32)
        self.xpbf = {"f": k.sb("xpbf_f", [128, 16 * 96], BF16),
                     "b": k.sb("xpbf_b", [128, 16 * 96], BF16)}
        self.dtpbf = {"f": k.sb("dtpbf_f", [DTR, DI], BF16),
                      "b": k.sb("dtpbf_b", [DTR, DI], BF16)}
        self.state = k.sb("state", [128, 256], F32)
        self.lgt = k.sb("lgt", [4, 512], F32)

        ACOLS = 60928
        self.ar = Arena(k.sb("arena", [128, ACOLS], BF16), ACOLS)

    def tap(self, name, ap_or_list):
        if name not in self.taps:
            return
        aps = ap_or_list if isinstance(ap_or_list, list) else [ap_or_list]
        w = aps[0].shape[-1]
        t = self.k.nc.dram_tensor("dbg_" + name, [len(aps) * 128, w],
                                  aps[0].dtype, kind="ExternalOutput")
        self.dbg[name] = t
        self.k.wait("gpsimd", self.sV)
        self.k.wait("gpsimd", self.sA)
        self.k.wait("gpsimd", self.sT)
        for j, a in enumerate(aps):
            self.k.dma("gpsimd", t.ap()[j * 128:j * 128 + a.shape[0], :], a, self.sO)
        for e in ENGINES:
            self.k.wait(e, self.sO)

    def build(self, upto="all"):
        k = self.k
        self.phase_load()
        self.phase_ln()
        k.barrier()
        self.phase_precast()
        if upto == "ln":
            return k.finalize()
        k.barrier()
        self.phase_attn()
        if upto == "attn":
            return k.finalize()
        k.barrier()
        self.phase_moe()
        if upto == "moe":
            return k.finalize()
        k.barrier()
        self.phase_mamba("f")
        if upto == "mambaf":
            return k.finalize()
        self.phase_mamba("b")
        k.barrier()
        return k.finalize()

    def phase_load(self):
        k, dw, cst = self.k, self.dw, self.cst
        self.x32, self.x32_key = self.ar.t32(8 * L)
        for i in range(8):
            k.dma("sync", self.x32[:, i * L:(i + 1) * L],
                  self.dx.ap()[i * 128:(i + 1) * 128, :], self.sW)
        for n, _ in WSPEC_BF:
            k.dma("sync", cst[n][:], dw[n].ap(), self.sW)
        for n in SMALL:
            k.dma("sync", cst[n][:], dw[n].ap(), self.sW)
        wl = self.sW.n
        for e in ("vector", "scalar", "tensor"):
            k.wait(e, self.sW, wl)
        for d in ("f", "b"):
            k.op("scalar", lambda e, d=d: e.activation(
                self.aexp[d][:], cst["alog_" + d][:], AF.Exp), inc=(self.sA, 1))
            k.op("vector", lambda e, d=d: e.tensor_scalar(
                self.ndtb[d][:], cst["dtb_" + d][:], -1.0, None, OP.mult), inc=(self.sV, 1))
        k.op("vector", lambda e: e.tensor_scalar(
            self.qnw8[:], cst["qnw"][:], 0.125, None, OP.mult), inc=(self.sV, 1))
        tq, kq = self.ar.t32(4)
        k.op("vector", lambda e: e.reduce_max(
            tq[0:1, 0:1], cst["qnrow"][:], AX.X, apply_absolute_value=True), inc=(self.sV, 1))
        k.op("vector", lambda e: e.reduce_max(
            tq[0:1, 1:2], cst["knrow"][:], AX.X, apply_absolute_value=True), inc=(self.sV, 1))
        k.vw()
        k.op("vector", lambda e: e.tensor_tensor(
            tq[0:1, 2:3], tq[0:1, 0:1], tq[0:1, 1:2], OP.mult), inc=(self.sV, 1))
        k.vw()
        k.op("vector", lambda e: e.tensor_scalar(
            tq[0:1, 3:4], tq[0:1, 2:3], -8.0, 15.0, OP.mult, OP.add), inc=(self.sV, 1))
        k.vw()
        k.wait("gpsimd", self.sV)
        k.dma("gpsimd", self.scr_row.ap()[7:8, 0:1], tq[0:1, 3:4], self.sO)
        k.wait("sync", self.sO)
        k.dma("sync", self.bcap[:], bass.AP(self.scr_row, 7 * L, [[0, 128], [1, 1]]), self.sB)
        self.ar.release(kq)
        st32, key = self.ar.t32(2048)
        for d in ("f", "b"):
            for kc in range(16):
                k.dma("sync", st32[0:128, kc * 96:(kc + 1) * 96],
                      dw["w_xproj_" + d].ap()[kc * 128:(kc + 1) * 128, :], self.sB)
            wv = self.sB.n
            k.wait("vector", self.sB, wv)
            k.op("vector", lambda e, d=d: e.tensor_copy(
                self.xpbf[d][:], st32[:, 0:16 * 96]), inc=(self.sV, 1))
            k.wait("sync", self.sV)
        for d in ("f", "b"):
            k.dma("sync", st32[0:DTR, 0:DI], dw["w_dtproj_" + d].ap(), self.sB)
            k.wait("vector", self.sB)
            k.op("vector", lambda e, d=d: e.tensor_copy(
                self.dtpbf[d][:], st32[0:DTR, 0:DI]), inc=(self.sV, 1))
            k.wait("sync", self.sV)
        k.op("vector", lambda e: e.memset(self.state[:], 0.0), inc=(self.sV, 1))
        self.ar.release(key)

    def phase_ln(self):
        k, cst, pb = self.k, self.cst, self.pb
        x32 = self.x32
        mu, mu_key = self.ar.t32(L)
        rb, rb_key = self.ar.t32(L)
        sqt, sq_key = self.ar.t16(2 * L)
        sq = [sqt[:, 0:L], sqt[:, L:2 * L]]
        cast_done = []
        sq_done = []
        for i in range(8):
            k.op("vector", lambda e, i=i: e.tensor_copy(
                self.xnbf[i][:], x32[:, i * L:(i + 1) * L]), inc=(self.sV, 1))
            cast_done.append(self.sV.n)
        sq_cons = [0, 0]
        for i in range(8):
            if i >= 2:
                k.wait("scalar", self.sT, sq_cons[i % 2])
            k.op("scalar", lambda e, i=i: e.activation(
                sq[i % 2][:], x32[:, i * L:(i + 1) * L], AF.Square), inc=(self.sA, 1))
            sq_done.append(self.sA.n)
            k.wait("tensor", self.sV, cast_done[i])
            k.wait("tensor", self.sA, sq_done[i])
            for qc in range(4):
                k.op("tensor", lambda e, i=i, qc=qc: e.matmul(
                    pb[qc][:, :], cst["c_ones"][:], self.xnbf[i][:, qc * 512:(qc + 1) * 512],
                    start=(i == 0), stop=(i == 7)), inc=(self.sT, 1))
                k.op("tensor", lambda e, i=i, qc=qc: e.matmul(
                    pb[4 + qc][:, :], cst["c_ones"][:], sq[i % 2][:, qc * 512:(qc + 1) * 512],
                    start=(i == 0), stop=(i == 7)), inc=(self.sT, 1))
            sq_cons[i % 2] = self.sT.n
        mm_all = self.sT.n
        k.wait("vector", self.sT, mm_all)
        musq, msq_key = self.ar.t32(L)
        for qc in range(4):
            sl = slice(qc * 512, (qc + 1) * 512)
            k.op("vector", lambda e, qc=qc, sl=sl: e.tensor_scalar(
                mu[:, sl], pb[qc][:, :], 1.0 / D, None, OP.mult), inc=(self.sV, 1))
        k.wait("scalar", self.sV)
        for qc in range(4):
            sl = slice(qc * 512, (qc + 1) * 512)
            k.op("scalar", lambda e, sl=sl: e.activation(
                musq[:, sl], mu[:, sl], AF.Square), inc=(self.sA, 1))
        k.wait("vector", self.sA)
        for qc in range(4):
            sl = slice(qc * 512, (qc + 1) * 512)
            k.op("vector", lambda e, qc=qc, sl=sl: e.scalar_tensor_tensor(
                musq[:, sl], pb[4 + qc][:, :], 1.0 / D, musq[:, sl],
                OP.mult, OP.subtract), inc=(self.sV, 1))
        k.wait("scalar", self.sV)
        k.op("scalar", lambda e: e.activation(
            musq[:, :], musq[:, :], AF.Sqrt, bias=cst["c_sc"][:, 0:1]), inc=(self.sA, 1))
        k.wait("vector", self.sA)
        k.op("vector", lambda e: e.reciprocal(rb[:, :], musq[:, :]), inc=(self.sV, 1))
        k.vw()
        for i in range(8):
            sl = slice(i * L, (i + 1) * L)
            k.op("vector", lambda e, sl=sl: e.tensor_tensor(
                x32[:, sl], x32[:, sl], mu[:, :], OP.subtract), inc=(self.sV, 1))
            k.vw()
            k.op("vector", lambda e, sl=sl: e.tensor_tensor(
                x32[:, sl], x32[:, sl], rb[:, :], OP.mult), inc=(self.sV, 1))
            k.vw()
            k.op("vector", lambda e, i=i, sl=sl: e.tensor_copy(
                self.xnbf[i][:], x32[:, sl]), inc=(self.sV, 1))
            k.wait("gpsimd", self.sV)
            k.dma("gpsimd", self.scr_xn.ap()[i * 128:(i + 1) * 128, :],
                  x32[:, sl], self.sO)
        self.tap("xn", [x32[:, i * L:(i + 1) * L] for i in range(8)])
        k.barrier()
        for key in (mu_key, rb_key, sq_key, msq_key, self.x32_key):
            self.ar.release(key)

    def phase_precast(self):
        k, dw, cst = self.k, self.dw, self.cst
        st, skey = self.ar.t32(2 * 4096)
        bt, bkey = self.ar.t16(2 * 4096)
        jobs = []
        for r in range(8):
            jobs.append((dw["w_inproj"].ap()[r * 128:(r + 1) * 128, :],
                         self.scr_wip.ap()[r * 128:(r + 1) * 128, :], 4096, None))
        for r in range(16):
            jobs.append((dw["w_outproj"].ap()[r * 128:(r + 1) * 128, :],
                         self.scr_wop.ap()[r * 128:(r + 1) * 128, :], 1024, None))
        for n, (scrt, fold) in self.scrw.items():
            kin, kout = dict(WSPEC_F32)[n]
            for r in range(kin // 128):
                fc = None
                if fold == 1:
                    fc = cst["n1w"][:, r:r + 1]
                elif fold == 2:
                    fc = cst["n2w"][:, r:r + 1]
                jobs.append((dw[n].ap()[r * 128:(r + 1) * 128, :],
                             scrt.ap()[r * 128:(r + 1) * 128, :], kout, fc))
        cast = [0, 0]
        spill = [0, 0]
        for j, (srcap, dst, w, fc) in enumerate(jobs):
            b = j % 2
            k.wait("sync", self.sV, cast[b])
            ld = k.dma("sync", st[:, b * 4096:b * 4096 + w], srcap, self.sWp[b])
            k.wait("vector", self.sWp[b], ld)
            k.wait("vector", self.sO, spill[b])
            if fc is None:
                k.op("vector", lambda e, b=b, w=w: e.tensor_copy(
                    bt[:, b * 4096:b * 4096 + w], st[:, b * 4096:b * 4096 + w]),
                    inc=(self.sV, 1))
            else:
                k.op("vector", lambda e, b=b, w=w, fc=fc: e.tensor_scalar(
                    bt[:, b * 4096:b * 4096 + w], st[:, b * 4096:b * 4096 + w],
                    fc, None, OP.mult), inc=(self.sV, 1))
            cast[b] = self.sV.n
            k.wait("gpsimd", self.sV, cast[b])
            spill[b] = k.dma("gpsimd", dst, bt[:, b * 4096:b * 4096 + w], self.sO)
        k.barrier()
        self.ar.release(skey)
        self.ar.release(bkey)

    def _headrms(self, pq_bank, gain_col, out_ap, rot):
        k, cst, pb = self.k, self.cst, self.pb
        sqq, rr32, rrb, rqb = self.hr_bufs[rot % 2]
        bank = self.pb[2 + rot % 2]
        k.wait("scalar", self.sT)
        k.op("scalar", lambda e: e.activation(sqq, pq_bank[:, :], AF.Square),
             inc=(self.sA, 1))
        k.wait("tensor", self.sA)
        k.op("tensor", lambda e, bank=bank: e.matmul(
            bank[0:2, :], cst["c_bd64"][:], sqq, start=True, stop=True), inc=(self.sT, 1))
        k.wait("scalar", self.sT)
        k.op("scalar", lambda e, bank=bank: e.activation(
            rr32, bank[0:2, :], AF.Sqrt, bias=cst["c_sc"][0:2, 1:2],
            scale=cst["c_sc"][0:2, 2:3]), inc=(self.sA, 1))
        k.wait("vector", self.sA)
        k.op("vector", lambda e: e.reciprocal(rr32, rr32), inc=(self.sV, 1))
        k.vw()
        k.op("vector", lambda e: e.tensor_copy(rrb, rr32), inc=(self.sV, 1))
        k.wait("tensor", self.sV)
        k.op("tensor", lambda e, bank=bank: e.matmul(
            bank[:, :], cst["c_ex64"][:], rrb, start=True, stop=True), inc=(self.sT, 1))
        k.wait("vector", self.sT)
        k.op("vector", lambda e, bank=bank: e.tensor_copy(rqb, bank[:, :]), inc=(self.sV, 1))
        k.vw()
        k.op("vector", lambda e: e.scalar_tensor_tensor(
            out_ap, pq_bank[:, :], gain_col, rqb, OP.mult, OP.mult), inc=(self.sV, 1))
        return self.sV.n

    def _wslab(self, wname, mc, wbf, sem, queue="sync", ncols=128):
        k = self.k
        scrt = self.scrw[wname][0] if wname in self.scrw else None
        assert scrt is not None
        kin = scrt.shape[0]
        nkc = kin // 128
        for kc in range(nkc):
            k.dma(queue, wbf[:, kc * ncols:(kc + 1) * ncols],
                  scrt.ap()[kc * 128:(kc + 1) * 128, mc * ncols:(mc + 1) * ncols], sem)
        return sem.n

    def phase_attn(self):
        k, cst, pb = self.k, self.cst, self.pb
        n1cols = [cst["n1w"][:, i:i + 1] for i in range(8)]
        sqt, sq_key = self.ar.t16(2 * L)
        sq = [sqt[:, 0:L], sqt[:, L:2 * L]]
        r1b, r1_key = self.ar.t32(L)
        sq_cons = [0, 0]
        for i in range(8):
            if i >= 2:
                k.wait("scalar", self.sT, sq_cons[i % 2])
            k.op("scalar", lambda e, i=i: e.activation(
                sq[i % 2][:], self.xnbf[i][:], AF.Square), inc=(self.sA, 1))
            k.wait("tensor", self.sA)
            for qc in range(4):
                k.op("tensor", lambda e, i=i, qc=qc: e.matmul(
                    pb[qc][:, :], cst["c_ones"][:], sq[i % 2][:, qc * 512:(qc + 1) * 512],
                    start=(i == 0), stop=(i == 7)), inc=(self.sT, 1))
            sq_cons[i % 2] = self.sT.n
        k.wait("scalar", self.sT)
        for qc in range(4):
            k.op("scalar", lambda e, qc=qc: e.activation(
                r1b[:, qc * 512:(qc + 1) * 512], pb[qc][:, :], AF.Sqrt,
                bias=cst["c_sc"][:, 1:2], scale=cst["c_sc"][:, 3:4]), inc=(self.sA, 1))
        k.wait("vector", self.sA)
        k.op("vector", lambda e: e.reciprocal(r1b[:, :], r1b[:, :]), inc=(self.sV, 1))
        k.wait("gpsimd", self.sV)
        k.dma("gpsimd", self.scr_row.ap()[0:1, :], r1b[0:1, :], self.sO)
        k.wait("sync", self.sO)
        r1tm, r1tm_key = self.ar.t32(16)
        k.dma("sync", r1tm[:, 0:16],
              bass.AP(self.scr_row, 0, [[1, 128], [128, 16]]), self.sB, slow=True)
        self.ar.release(sq_key)
        self.ar.release(r1_key)
        k.barrier()

        kh, kh_key = self.ar.t16(8 * L)
        wbf, wbf_key = self.ar.t16(2 * 1024)
        hrb32, hrb32_key = self.ar.t32(2 * 1024)
        hrb16, hrb16_key = self.ar.t16(2 * 1024)
        self.hrb32, self.hrb16 = hrb32, hrb16
        self.hr_bufs = []
        for b in range(2):
            o32 = b * 1024
            o16 = b * 1024
            sqq = hrb16[:, o16:o16 + 512]
            rr32 = hrb32[0:2, o32:o32 + 512]
            rrb = hrb16[0:2, o16 + 512:o16 + 1024]
            rqb = hrb32[:, o32 + 512:o32 + 1024]
            self.hr_bufs.append((sqq, rr32, rrb, rqb))

        wready = [0, 0]
        wcons = [0, 0]
        for mc in range(8):
            b = mc % 2
            k.wait("sync", self.sT, wcons[b])
            wready[b] = self._wslab("w_k", mc, wbf[:, b * 1024:(b + 1) * 1024],
                                    self.sWp[b])
            k.wait("tensor", self.sWp[b], wready[b])
            for qc in range(4):
                r = mc * 4 + qc
                bank = pb[r % 2]
                k.wait("tensor", self.sV)
                for kc in range(8):
                    k.op("tensor", lambda e, b=b, kc=kc, qc=qc, bank=bank: e.matmul(
                        bank[:, :], wbf[:, b * 1024 + kc * 128:b * 1024 + (kc + 1) * 128],
                        self.xnbf[kc][:, qc * 512:(qc + 1) * 512],
                        start=(kc == 0), stop=(kc == 7)), inc=(self.sT, 1))
                self._headrms(bank, cst["knw"][:, 0:1],
                              kh[:, mc * L + qc * 512:mc * L + (qc + 1) * 512], r)
            wcons[b] = self.sT.n
        k.barrier()

        vt, vt_key = self.ar.t16(16 * 1024)
        vwbf, vwbf_key = self.ar.t16(8 * 1024)
        for dc in range(8):
            k.dma("sync", vwbf[:, dc * 1024:(dc + 1) * 1024],
                  self.scrw["w_v"][0].ap()[dc * 128:(dc + 1) * 128, :], self.sB)
        k.wait("tensor", self.sB)
        k.wait("vector", self.sB)
        vevac = 0
        for kc in range(16):
            for hf in range(2):
                bank = pb[(kc * 2 + hf) % 2]
                k.wait("tensor", self.sV, vevac)
                for dc in range(8):
                    k.op("tensor", lambda e, kc=kc, hf=hf, dc=dc, bank=bank: e.matmul(
                        bank[:, :], self.xnbf[dc][:, kc * 128:(kc + 1) * 128],
                        vwbf[:, dc * 1024 + hf * 512:dc * 1024 + (hf + 1) * 512],
                        start=(dc == 0), stop=(dc == 7)), inc=(self.sT, 1))
                k.wait("vector", self.sT)
                k.op("vector", lambda e, kc=kc, hf=hf, bank=bank: e.tensor_scalar(
                    vt[:, kc * 1024 + hf * 512:kc * 1024 + (hf + 1) * 512],
                    bank[:, :], r1tm[:, kc:kc + 1], None, OP.mult), inc=(self.sV, 1))
                vevac = self.sV.n
        self.ar.release(vwbf_key)
        k.barrier()
        self.tap("bcap", self.bcap[:])
        self.tap("r1tm", r1tm[:, 0:16])
        self.tap("kh", [kh[:, i * L:(i + 1) * L] for i in range(8)])
        self.tap("vt", [vt[:, i * 1024:(i + 1) * 1024] for i in range(16)])

        pt, pt_key = self.ar.t16(2 * 16 * 512)
        qh, qh_key = self.ar.t16(2 * 512)
        avdone = 0
        for mc in range(8):
            b = mc % 2
            k.wait("sync", self.sT, wcons[b])
            wready[b] = self._wslab("w_q", mc, wbf[:, b * 1024:(b + 1) * 1024],
                                    self.sWp[b])
            k.wait("tensor", self.sWp[b], wready[b])
            for qc in range(4):
                r = mc * 4 + qc
                bank = pb[r % 2]
                k.wait("tensor", self.sV)
                k.wait("vector", self.sT)
                for kc in range(8):
                    k.op("tensor", lambda e, b=b, kc=kc, qc=qc, bank=bank: e.matmul(
                        bank[:, :], wbf[:, b * 1024 + kc * 128:b * 1024 + (kc + 1) * 128],
                        self.xnbf[kc][:, qc * 512:(qc + 1) * 512],
                        start=(kc == 0), stop=(kc == 7)), inc=(self.sT, 1))
                qv = self._headrms(bank, self.qnw8[:, 0:1], qh[:, b * 512:(b + 1) * 512], r)
                k.wait("tensor", self.sV, qv)
                o32 = (r % 2) * 1024
                o16 = (r % 2) * 1024
                nrow = self.hrb32[:, o32:o32 + 512]
                nrowb = self.hrb16[:, o16 + 512:o16 + 1024]
                nb32 = self.hrb32[:, o32 + 512:o32 + 1024]
                nkc = 4 * qc + 4
                for h2 in range(2):
                    h = 2 * mc + h2
                    for kc in range(nkc):
                        sbank = pb[4 + (kc % 2)]
                        k.wait("tensor", self.sA)
                        k.op("tensor", lambda e, h2=h2, kc=kc, mc=mc, b=b, sbank=sbank: e.matmul(
                            sbank[:, :],
                            kh[64 * h2:64 * h2 + 64,
                               mc * L + kc * 128:mc * L + (kc + 1) * 128],
                            qh[64 * h2:64 * h2 + 64, b * 512:(b + 1) * 512],
                            start=True, stop=True), inc=(self.sT, 1))
                        s = kc - 4 * qc
                        if 0 <= s <= 3:
                            k.wait("vector", self.sT)
                            k.op("vector", lambda e, s=s, sbank=sbank: e.tensor_tensor(
                                sbank[:, s * 128:(s + 1) * 128],
                                sbank[:, s * 128:(s + 1) * 128],
                                cst["c_maskT"][:], OP.add), inc=(self.sV, 1))
                            k.wait("scalar", self.sV)
                        k.wait("scalar", self.sT)
                        k.op("scalar", lambda e, h2=h2, kc=kc, sbank=sbank: e.activation(
                            pt[:, h2 * 8192 + kc * 512:h2 * 8192 + (kc + 1) * 512],
                            sbank[:, :], AF.Exp, bias=self.bcap[:, 0:1]), inc=(self.sA, 1))
                        for s2 in range(min(kc - 4 * qc, 4)):
                            k.wait("vector", self.sA)
                            k.op("vector", lambda e, h2=h2, kc=kc, s2=s2: e.memset(
                                pt[:, h2 * 8192 + kc * 512 + s2 * 128:
                                   h2 * 8192 + kc * 512 + (s2 + 1) * 128], 0.0),
                                 inc=(self.sV, 1))
                ptv = self.sA.n
                ptv2 = self.sV.n
                k.wait("tensor", self.sA, ptv)
                k.wait("tensor", self.sV, ptv2)
                for kc in range(nkc):
                    k.op("tensor", lambda e, mc=mc, kc=kc: e.matmul(
                        pb[6][0:64, :], vt[:, kc * 1024 + 128 * mc:kc * 1024 + 128 * mc + 64],
                        pt[:, 0 * 8192 + kc * 512:0 * 8192 + (kc + 1) * 512],
                        start=(kc == 0), stop=False), inc=(self.sT, 1))
                    k.op("tensor", lambda e, mc=mc, kc=kc, nkc=nkc: e.matmul(
                        pb[6][64:128, :], vt[:, kc * 1024 + 128 * mc + 64:kc * 1024 + 128 * mc + 128],
                        pt[:, 1 * 8192 + kc * 512:1 * 8192 + (kc + 1) * 512],
                        start=(kc == 0), stop=(kc == nkc - 1), tile_position=(0, 64)),
                        inc=(self.sT, 1))
                    k.op("tensor", lambda e, kc=kc: e.matmul(
                        pb[7][0:1, :], cst["c_ones"][:, 0:1],
                        pt[:, 0 * 8192 + kc * 512:0 * 8192 + (kc + 1) * 512],
                        start=(kc == 0), stop=False), inc=(self.sT, 1))
                    k.op("tensor", lambda e, kc=kc, nkc=nkc: e.matmul(
                        pb[7][64:65, :], cst["c_ones"][:, 0:1],
                        pt[:, 1 * 8192 + kc * 512:1 * 8192 + (kc + 1) * 512],
                        start=(kc == 0), stop=(kc == nkc - 1), tile_position=(0, 64)),
                        inc=(self.sT, 1))
                avmm = self.sT.n
                k.wait("vector", self.sT, avmm)
                if mc == 0 and qc == 0 and ("avraw00" in self.taps):
                    avr = qh[:, 512:1024]
                    smr = self.gbf[7][:, 0:512]
                    k.op("vector", lambda e, avr=avr: e.tensor_copy(avr, pb[6][:, :]),
                         inc=(self.sV, 1))
                    k.op("vector", lambda e, smr=smr: e.tensor_copy(smr[0:1, :], pb[7][0:1, :]),
                         inc=(self.sV, 1))
                    k.op("vector", lambda e, smr=smr: e.tensor_copy(smr[64:65, :], pb[7][64:65, :]),
                         inc=(self.sV, 1))
                    k.vw()
                    self.tap("avraw00", avr)
                    self.tap("sums00", smr)
                k.op("vector", lambda e, nrow=nrow: e.reciprocal(nrow[0:1, :], pb[7][0:1, :]),
                     inc=(self.sV, 1))
                k.op("vector", lambda e, nrow=nrow: e.reciprocal(nrow[64:65, :], pb[7][64:65, :]),
                     inc=(self.sV, 1))
                k.vw()
                k.op("vector", lambda e, nrow=nrow, nrowb=nrowb: e.tensor_copy(
                    nrowb[0:1, :], nrow[0:1, :]), inc=(self.sV, 1))
                k.op("vector", lambda e, nrow=nrow, nrowb=nrowb: e.tensor_copy(
                    nrowb[64:65, :], nrow[64:65, :]), inc=(self.sV, 1))
                k.wait("tensor", self.sV)
                k.op("tensor", lambda e, nrowb=nrowb: e.matmul(
                    pb[7][0:64, :], cst["c_ones"][0:1, 0:64], nrowb[0:1, :],
                    start=True, stop=False), inc=(self.sT, 1))
                k.op("tensor", lambda e, nrowb=nrowb: e.matmul(
                    pb[7][64:128, :], cst["c_ones"][64:65, 0:64], nrowb[64:65, :],
                    start=True, stop=True, tile_position=(64, 64)), inc=(self.sT, 1))
                k.wait("vector", self.sT)
                k.op("vector", lambda e, nb32=nb32: e.tensor_copy(nb32[:, :], pb[7][:, :]),
                     inc=(self.sV, 1))
                k.vw()
                k.op("vector", lambda e, mc=mc, qc=qc, nb32=nb32: e.tensor_tensor(
                    self.gbf[mc][:, qc * 512:(qc + 1) * 512], pb[6][:, :],
                    nb32[:, :], OP.mult), inc=(self.sV, 1))
                avdone = self.sV.n
                if mc == 0 and qc == 0:
                    self.tap("pt00", [pt[:, h2 * 8192:h2 * 8192 + 4 * 512]
                                      for h2 in range(2)])
                    self.tap("avb00", self.gbf[0][:, 0:512])
                    self.tap("qh00", qh[:, 0:512])
            wcons[b] = self.sT.n
        k.barrier()
        for key in (pt_key, qh_key, kh_key, vt_key,
                    r1tm_key):
            self.ar.release(key)

        xf32, self.xf32_key = self.ar.t32(8 * L)
        for i in range(8):
            k.dma("sync", xf32[:, i * L:(i + 1) * L],
                  self.scr_xn.ap()[i * 128:(i + 1) * 128, :], self.sB)
        xloaded = self.sB.n
        k.wait("vector", self.sB, xloaded)
        aocons = 0
        for mc in range(2, 8):
            b = mc % 2
            k.wait("sync", self.sT, wcons[b])
            wready[b] = self._wslab("w_o", mc, wbf[:, b * 1024:(b + 1) * 1024],
                                    self.sWp[b])
            k.wait("tensor", self.sWp[b], wready[b])
            for qc in range(4):
                bank = pb[(mc * 4 + qc) % 2]
                k.wait("tensor", self.sV, aocons)
                for kc in range(8):
                    k.op("tensor", lambda e, b=b, kc=kc, qc=qc, bank=bank: e.matmul(
                        bank[:, :], wbf[:, b * 1024 + kc * 128:b * 1024 + (kc + 1) * 128],
                        self.gbf[kc][:, qc * 512:(qc + 1) * 512],
                        start=(kc == 0), stop=(kc == 7)), inc=(self.sT, 1))
                k.wait("vector", self.sT)
                k.op("vector", lambda e, mc=mc, qc=qc, bank=bank: e.tensor_tensor(
                    xf32[:, mc * L + qc * 512:mc * L + (qc + 1) * 512],
                    xf32[:, mc * L + qc * 512:mc * L + (qc + 1) * 512],
                    bank[:, :], OP.add), inc=(self.sV, 1))
                aocons = self.sV.n
            wcons[b] = self.sT.n
        self.tap("xffn", [xf32[:, i * L:(i + 1) * L] for i in range(8)])
        k.barrier()
        self.ar.release(wbf_key)
        self.ar.release(hrb32_key)
        self.ar.release(hrb16_key)

        sqt2, sq2_key = self.ar.t16(2 * L)
        sq2 = [sqt2[:, 0:L], sqt2[:, L:2 * L]]
        self.r2b, self.r2b_key = self.ar.t32(L)
        sq_cons = [0, 0]
        for i in range(8):
            if i >= 2:
                k.wait("scalar", self.sT, sq_cons[i % 2])
            k.op("scalar", lambda e, i=i: e.activation(
                sq2[i % 2][:], xf32[:, i * L:(i + 1) * L], AF.Square), inc=(self.sA, 1))
            k.wait("tensor", self.sA)
            for qc in range(4):
                k.op("tensor", lambda e, i=i, qc=qc: e.matmul(
                    pb[qc][:, :], cst["c_ones"][:], sq2[i % 2][:, qc * 512:(qc + 1) * 512],
                    start=(i == 0), stop=(i == 7)), inc=(self.sT, 1))
            sq_cons[i % 2] = self.sT.n
        k.wait("scalar", self.sT)
        for qc in range(4):
            k.op("scalar", lambda e, qc=qc: e.activation(
                self.r2b[:, qc * 512:(qc + 1) * 512], pb[qc][:, :], AF.Sqrt,
                bias=cst["c_sc"][:, 1:2], scale=cst["c_sc"][:, 3:4]), inc=(self.sA, 1))
        k.wait("vector", self.sA)
        k.op("vector", lambda e: e.reciprocal(self.r2b[:, :], self.r2b[:, :]),
             inc=(self.sV, 1))
        k.wait("gpsimd", self.sV)
        for i in range(8):
            k.dma("gpsimd", self.scr_xffn.ap()[i * 128:(i + 1) * 128, :],
                  xf32[:, i * L:(i + 1) * L], self.sO)
        k.barrier()
        self.ar.release(sq2_key)
        self.ar.release(self.xf32_key)

    def phase_moe(self):
        k, cst, pb = self.k, self.cst, self.pb
        n2cols = [cst["n2w"][:, i:i + 1] for i in range(8)]
        xfh, xfh_key = self.ar.t16(8 * L)
        xt32, xt_key = self.ar.t32(2 * 512)
        for i in range(8):
            for qc in range(4):
                b = (i * 4 + qc) % 2
                k.wait("sync", self.sV)
                ld = k.dma("sync", xt32[:, b * 512:(b + 1) * 512],
                           self.scr_xffn.ap()[i * 128:(i + 1) * 128, qc * 512:(qc + 1) * 512],
                           self.sWp[b])
                k.wait("vector", self.sWp[b], ld)
                k.op("vector", lambda e, i=i, qc=qc, b=b: e.tensor_tensor(
                    xfh[:, i * L + qc * 512:i * L + (qc + 1) * 512],
                    xt32[:, b * 512:(b + 1) * 512],
                    self.r2b[:, qc * 512:(qc + 1) * 512], OP.mult), inc=(self.sV, 1))
        self.tap("xfh", [xfh[:, i * L:(i + 1) * L] for i in range(8)])
        k.barrier()
        self.ar.release(xt_key)
        self.ar.release(self.r2b_key)

        wgbf, wgbf_key = self.ar.t16(32)
        wg32, wg32_key = self.ar.t32(32)
        for kc in range(8):
            k.dma("sync", wg32[:, kc * 4:(kc + 1) * 4],
                  self.dw["w_gate"].ap()[kc * 128:(kc + 1) * 128, :], self.sB)
        k.wait("vector", self.sB)
        for kc in range(8):
            k.op("vector", lambda e, kc=kc: e.tensor_scalar(
                wgbf[:, kc * 4:(kc + 1) * 4], wg32[:, kc * 4:(kc + 1) * 4],
                n2cols[kc], None, OP.mult), inc=(self.sV, 1))
        k.wait("tensor", self.sV)
        self.tap("wgbf", wgbf[:, 0:32])
        lg32 = self.lgt[:]
        gt, gt_key = self.ar.t32(16 + 16 + 16 + 16 + 4 + 16)
        wfm, wfm_key = self.ar.t16(L)
        k.wait("tensor", self.sV)
        for qc in range(4):
            for kc in range(8):
                k.op("tensor", lambda e, kc=kc, qc=qc: e.matmul(
                    pb[2][0:4, :], wgbf[:, kc * 4:(kc + 1) * 4],
                    xfh[:, kc * L + qc * 512:kc * L + (qc + 1) * 512],
                    start=(kc == 0), stop=(kc == 7)), inc=(self.sT, 1))
            k.wait("vector", self.sT)
            k.op("vector", lambda e: e.tensor_copy(lg32[0:4, :], pb[2][0:4, :]),
                 inc=(self.sV, 1))
            k.wait("tensor", self.sV)
            for sub in range(4):
                k.op("tensor", lambda e, sub=sub: e.transpose(
                    pb[3][0:128, sub * 4:(sub + 1) * 4], lg32[0:4, sub * 128:(sub + 1) * 128],
                    cst["c_id4"][:]), inc=(self.sT, 1))
            k.wait("vector", self.sT)
            k.vw()
            k.op("vector", lambda e: e.tensor_copy(gt[:, 0:16], pb[3][:, 0:16]),
                 inc=(self.sV, 1))
            if qc == 0:
                k.vw()
                self.tap("lgfm", lg32[0:4, :])
                k.op("vector", lambda e: e.tensor_copy(wfm[:, 0:16], gt[:, 0:16]),
                     inc=(self.sV, 1))
                k.vw()
                self.tap("gtlg", wfm[:, 0:16])
            lgv = gt[:, 0:16].rearrange("p (s e) -> p s e", e=4)
            k.vw()
            k.op("vector", lambda e, lgv=lgv: e.tensor_reduce(
                gt[:, 64:68], lgv, AX.X, OP.max), inc=(self.sV, 1))
            k.vw()
            k.op("vector", lambda e: e.tensor_scalar(
                gt[:, 64:68], gt[:, 64:68], -1.0, None, OP.mult), inc=(self.sV, 1))
            k.wait("scalar", self.sV)
            for sub in range(4):
                k.op("scalar", lambda e, sub=sub: e.activation(
                    gt[:, 68 + sub * 4:68 + (sub + 1) * 4], gt[:, sub * 4:(sub + 1) * 4],
                    AF.Exp, bias=gt[:, 64 + sub:65 + sub]), inc=(self.sA, 1))
            k.wait("vector", self.sA)
            ev = gt[:, 68:84].rearrange("p (s e) -> p s e", e=4)
            k.vw()
            k.op("vector", lambda e, ev=ev: e.tensor_reduce(
                gt[:, 64:68], ev, AX.X, OP.add), inc=(self.sV, 1))
            k.vw()
            k.op("vector", lambda e: e.reciprocal(gt[:, 64:68], gt[:, 64:68]),
                 inc=(self.sV, 1))
            for sub in range(4):
                k.vw()
                k.op("vector", lambda e, sub=sub: e.tensor_scalar(
                    gt[:, 16 + sub * 4:16 + (sub + 1) * 4],
                    gt[:, 68 + sub * 4:68 + (sub + 1) * 4],
                    gt[:, 64 + sub:65 + sub], None, OP.mult), inc=(self.sV, 1))
            k.vw()
            k.op("vector", lambda e: e.memset(gt[:, 32:48], 0.0), inc=(self.sV, 1))
            for a in range(4):
                for e2 in range(4):
                    if a == e2:
                        continue
                    k.vw()
                    k.op("vector", lambda e, a=a, e2=e2: e.tensor_tensor(
                        gt[:, 48 + e2:64:4][:, 0:4],
                        gt[:, 16 + a:32:4][:, 0:4],
                        gt[:, 16 + e2:32:4][:, 0:4], OP.is_gt), inc=(self.sV, 1))
                    k.vw()
                    k.op("vector", lambda e, e2=e2: e.tensor_tensor(
                        gt[:, 32 + e2:48:4][:, 0:4],
                        gt[:, 32 + e2:48:4][:, 0:4],
                        gt[:, 48 + e2:64:4][:, 0:4], OP.add), inc=(self.sV, 1))
            k.vw()
            k.op("vector", lambda e: e.tensor_scalar(
                gt[:, 32:48], gt[:, 32:48], 1.0, None, OP.is_le), inc=(self.sV, 1))
            k.vw()
            k.op("vector", lambda e: e.tensor_tensor(
                gt[:, 48:64], gt[:, 16:32], gt[:, 32:48], OP.mult), inc=(self.sV, 1))
            wv2 = gt[:, 48:64].rearrange("p (s e) -> p s e", e=4)
            k.vw()
            k.op("vector", lambda e, wv2=wv2: e.tensor_reduce(
                gt[:, 64:68], wv2, AX.X, OP.add), inc=(self.sV, 1))
            k.vw()
            k.op("vector", lambda e: e.tensor_scalar(
                gt[:, 64:68], gt[:, 64:68], 1e-8, None, OP.add), inc=(self.sV, 1))
            k.vw()
            k.op("vector", lambda e: e.reciprocal(gt[:, 64:68], gt[:, 64:68]),
                 inc=(self.sV, 1))
            for sub in range(4):
                k.vw()
                k.op("vector", lambda e, sub=sub: e.tensor_scalar(
                    gt[:, 48 + sub * 4:48 + (sub + 1) * 4],
                    gt[:, 48 + sub * 4:48 + (sub + 1) * 4],
                    gt[:, 64 + sub:65 + sub], None, OP.mult), inc=(self.sV, 1))
            wtb, wtb_key = self.ar.t16(16)
            k.vw()
            k.op("vector", lambda e, wtb=wtb: e.tensor_copy(wtb[:, 0:16], gt[:, 48:64]),
                 inc=(self.sV, 1))
            k.wait("tensor", self.sV)
            pb3bf = pb[3][:].bitcast(BF16)
            for sub in range(4):
                k.op("tensor", lambda e, sub=sub, wtb=wtb, pb3bf=pb3bf: e.transpose(
                    pb3bf[0:4, sub * 128:(sub + 1) * 128],
                    wtb[:, sub * 4:(sub + 1) * 4], cst["c_idbf"][:]), inc=(self.sT, 1))
            k.wait("vector", self.sT)
            k.vw()
            k.op("vector", lambda e, qc=qc, pb3bf=pb3bf: e.tensor_copy(
                wfm[0:4, qc * 512:(qc + 1) * 512], pb3bf[0:4, 0:512]), inc=(self.sV, 1))
            self.ar.release(wtb_key)
        k.wait("gpsimd", self.sV)
        for e2 in range(4):
            k.dma("gpsimd", self.scr_gw.ap()[e2:e2 + 1, :], wfm[e2:e2 + 1, :], self.sO)
        k.wait("sync", self.sO)
        self.tap("gatew", wfm[0:4, :])
        k.barrier()
        for key in (wgbf_key, wg32_key, gt_key, wfm_key):
            self.ar.release(key)

        mid0, mid0_key = self.ar.t16(16 * 1024)
        mid1h, mid1h_key = self.ar.t16(8 * 1024)
        macc, macc_key = self.ar.t32(6 * 1024)
        wbc, wbc_key = self.ar.t16(1024)
        t32b, t32b_key = self.ar.t32(2 * 512)
        wbf2, wbf2_key = self.ar.t16(2 * 2048)
        slab_cons = [0, 0]
        slab_ready = [0, 0]

        for tch in range(2):
            jid = 0

            def load_slab(wname, mc, ncols=128):
                nonlocal jid
                b = jid % 2
                jid += 1
                k.wait("sync", self.sT, slab_cons[b])
                slab_ready[b] = self._wslab(wname, mc, wbf2[:, b * 2048:(b + 1) * 2048],
                                            self.sWp[b], ncols=ncols)
                k.wait("tensor", self.sWp[b], slab_ready[b])
                return b

            def l1(wname, outbuf, act, ncin=8, src=None, obase=1024):
                nonlocal jid
                for mc in range(dict(WSPEC_F32)[wname][1] // 128):
                    b = load_slab(wname, mc)
                    for q2 in range(2):
                        bank = pb[(jid * 2 + q2) % 2]
                        k.wait("tensor", self.sA)
                        for kc in range(ncin):
                            if src is None:
                                rhs = xfh[:, kc * L + tch * 1024 + q2 * 512:
                                          kc * L + tch * 1024 + (q2 + 1) * 512]
                            else:
                                rhs = src[:, kc * 1024 + q2 * 512:kc * 1024 + (q2 + 1) * 512]
                            k.op("tensor", lambda e, b=b, kc=kc, rhs=rhs, bank=bank: e.matmul(
                                bank[:, :], wbf2[:, b * 2048 + kc * 128:b * 2048 + (kc + 1) * 128],
                                rhs, start=(kc == 0), stop=(kc == ncin - 1)), inc=(self.sT, 1))
                        k.wait("scalar", self.sT)
                        k.op("scalar", lambda e, mc=mc, q2=q2, act=act, bank=bank,
                             outbuf=outbuf, obase=obase: e.activation(
                                 outbuf[:, mc * obase + q2 * 512:mc * obase + (q2 + 1) * 512],
                                 bank[:, :], act), inc=(self.sA, 1))
                    slab_cons[b] = self.sT.n

            def dn(wname, e2, srcbuf, ncin=16):
                nonlocal jid
                k.wait("sync", self.sV)
                wb_ld = k.dma("sync", wbc[:, 0:1024],
                              bass.AP(self.scr_gw, e2 * L + tch * 1024,
                                      [[0, 128], [1, 1024]]), self.sB)
                k.wait("vector", self.sB, wb_ld)
                for mc in range(2, 8):
                    b = load_slab(wname, mc)
                    for q2 in range(2):
                        bank = pb[(jid * 2 + q2) % 2]
                        k.wait("tensor", self.sV)
                        for kc in range(ncin):
                            k.op("tensor", lambda e, b=b, kc=kc, q2=q2, srcbuf=srcbuf,
                                 bank=bank: e.matmul(
                                     bank[:, :],
                                     wbf2[:, b * 2048 + kc * 128:b * 2048 + (kc + 1) * 128],
                                     srcbuf[:, kc * 1024 + q2 * 512:kc * 1024 + (q2 + 1) * 512],
                                     start=(kc == 0), stop=(kc == ncin - 1)), inc=(self.sT, 1))
                        k.wait("vector", self.sT)
                        bt = q2 % 2
                        if e2 == 0:
                            k.op("vector", lambda e, mc=mc, q2=q2, bank=bank: e.tensor_tensor(
                                macc[:, (mc - 2) * 1024 + q2 * 512:(mc - 2) * 1024 + (q2 + 1) * 512],
                                bank[:, :], wbc[:, q2 * 512:(q2 + 1) * 512],
                                OP.mult), inc=(self.sV, 1))
                        else:
                            k.op("vector", lambda e, q2=q2, bt=bt, bank=bank: e.tensor_tensor(
                                t32b[:, bt * 512:(bt + 1) * 512], bank[:, :],
                                wbc[:, q2 * 512:(q2 + 1) * 512], OP.mult), inc=(self.sV, 1))
                            k.vw()
                            k.op("vector", lambda e, mc=mc, q2=q2, bt=bt: e.tensor_tensor(
                                macc[:, (mc - 2) * 1024 + q2 * 512:(mc - 2) * 1024 + (q2 + 1) * 512],
                                macc[:, (mc - 2) * 1024 + q2 * 512:(mc - 2) * 1024 + (q2 + 1) * 512],
                                t32b[:, bt * 512:(bt + 1) * 512], OP.add), inc=(self.sV, 1))
                    slab_cons[b] = self.sT.n

            l1("w_e0up", mid0, AF.Gelu)
            if tch == 0:
                self.tap("mide0", [mid0[:, i * 1024:(i + 1) * 1024] for i in range(16)])
            dn("w_e0dn", 0, mid0)
            if tch == 0:
                self.tap("macc_e0", [macc[:, i * 1024:(i + 1) * 1024] for i in range(6)])
            for mc in range(16):
                ba = load_slab("w_e1a", mc)
                bb = load_slab("w_e1b", mc)
                if tch == 0 and mc == 0:
                    self.k.wait("gpsimd", self.sW)
                    self.tap("we1a0", wbf2[:, ba * 2048:(ba + 1) * 2048])
                    self.tap("we1b0", wbf2[:, bb * 2048:(bb + 1) * 2048])
                for q2 in range(2):
                    bka, bkb = pb[4], pb[5]
                    k.wait("tensor", self.sA)
                    k.wait("tensor", self.sV)
                    for kc in range(8):
                        k.op("tensor", lambda e, ba=ba, kc=kc, q2=q2, bka=bka, tch=tch: e.matmul(
                            bka[:, :], wbf2[:, ba * 2048 + kc * 128:ba * 2048 + (kc + 1) * 128],
                            xfh[:, kc * L + tch * 1024 + q2 * 512:kc * L + tch * 1024 + (q2 + 1) * 512],
                            start=(kc == 0), stop=(kc == 7)), inc=(self.sT, 1))
                    for kc in range(8):
                        k.op("tensor", lambda e, bb=bb, kc=kc, q2=q2, bkb=bkb, tch=tch: e.matmul(
                            bkb[:, :], wbf2[:, bb * 2048 + kc * 128:bb * 2048 + (kc + 1) * 128],
                            xfh[:, kc * L + tch * 1024 + q2 * 512:kc * L + tch * 1024 + (q2 + 1) * 512],
                            start=(kc == 0), stop=(kc == 7)), inc=(self.sT, 1))
                    k.wait("scalar", self.sT)
                    k.wait("scalar", self.sV)
                    if tch == 0 and mc == 0 and q2 == 0 and "pe1raw" in self.taps:
                        k.wait("vector", self.sT)
                        k.op("vector", lambda e, bka=bka: e.tensor_copy(
                            mid1h[:, 0:512], bka[:, :]), inc=(self.sV, 1))
                        k.op("vector", lambda e, bkb=bkb: e.tensor_copy(
                            mid1h[:, 512:1024], bkb[:, :]), inc=(self.sV, 1))
                        k.vw()
                        self.tap("pe1raw", mid1h[:, 0:1024])
                    bt = q2 % 2
                    k.op("scalar", lambda e, bt=bt, bka=bka: e.activation(
                        t32b[:, bt * 512:(bt + 1) * 512], bka[:, :], AF.Silu),
                        inc=(self.sA, 1))
                    k.wait("vector", self.sA)
                    k.op("vector", lambda e, mc=mc, q2=q2, bt=bt, bkb=bkb: e.tensor_tensor(
                        mid0[:, mc * 1024 + q2 * 512:mc * 1024 + (q2 + 1) * 512],
                        t32b[:, bt * 512:(bt + 1) * 512], bkb[:, :], OP.mult),
                        inc=(self.sV, 1))
                slab_cons[ba] = self.sT.n
                slab_cons[bb] = self.sT.n
            if tch == 0:
                self.tap("mide1", [mid0[:, i * 1024:(i + 1) * 1024] for i in range(16)])
            dn("w_e1dn", 1, mid0)
            if tch == 0:
                self.tap("macc_e1", [macc[:, i * 1024:(i + 1) * 1024] for i in range(6)])
            l1("w_e2a", mid0, AF.Gelu)
            for q2h in range(2):
                for mc in range(16):
                    b = load_slab("w_e2b", mc)
                    bank = pb[jid % 2]
                    k.wait("tensor", self.sA)
                    for kc in range(16):
                        k.op("tensor", lambda e, b=b, kc=kc, q2h=q2h, bank=bank: e.matmul(
                            bank[:, :], wbf2[:, b * 2048 + kc * 128:b * 2048 + (kc + 1) * 128],
                            mid0[:, kc * 1024 + q2h * 512:kc * 1024 + (q2h + 1) * 512],
                            start=(kc == 0), stop=(kc == 15)), inc=(self.sT, 1))
                    k.wait("scalar", self.sT)
                    k.op("scalar", lambda e, mc=mc, bank=bank: e.activation(
                        mid1h[:, mc * 512:(mc + 1) * 512], bank[:, :], AF.Gelu),
                        inc=(self.sA, 1))
                    slab_cons[b] = self.sT.n
                for mc in range(16):
                    b = load_slab("w_e2c", mc)
                    bank = pb[jid % 2]
                    k.wait("tensor", self.sA)
                    for kc in range(16):
                        k.op("tensor", lambda e, b=b, kc=kc, bank=bank: e.matmul(
                            bank[:, :], wbf2[:, b * 2048 + kc * 128:b * 2048 + (kc + 1) * 128],
                            mid1h[:, kc * 512:(kc + 1) * 512],
                            start=(kc == 0), stop=(kc == 15)), inc=(self.sT, 1))
                    k.wait("scalar", self.sT)
                    k.op("scalar", lambda e, mc=mc, q2h=q2h, bank=bank: e.activation(
                        mid0[:, mc * 1024 + q2h * 512:mc * 1024 + (q2h + 1) * 512],
                        bank[:, :], AF.Gelu), inc=(self.sA, 1))
                    slab_cons[b] = self.sT.n
            dn("w_e2d", 2, mid0)
            if tch == 0:
                self.tap("macc_e2", [macc[:, i * 1024:(i + 1) * 1024] for i in range(6)])
            l1("w_e3a", mid1h, AF.Gelu, obase=1024)
            l1("w_e3b", mid0, AF.Gelu, ncin=8, src=mid1h)
            dn("w_e3c", 3, mid0)
            if tch == 0:
                self.tap("macc_all", [macc[:, i * 1024:(i + 1) * 1024] for i in range(6)])
            if tch == 1:
                self.tap("macc_t1", [macc[:, i * 1024:(i + 1) * 1024] for i in range(6)])
            for mc in range(8):
                for q2 in range(2):
                    b = (mc * 2 + q2) % 2
                    k.wait("sync", self.sA)
                    k.wait("sync", self.sV)
                    ld = k.dma("sync", t32b[:, b * 512:(b + 1) * 512],
                               self.scr_xffn.ap()[mc * 128:(mc + 1) * 128,
                                                  tch * 1024 + q2 * 512:tch * 1024 + (q2 + 1) * 512],
                               self.sB)
                    if mc >= 2:
                        k.wait("vector", self.sB, ld)
                        k.wait("vector", self.sT)
                        k.op("vector", lambda e, mc=mc, q2=q2, b=b: e.tensor_tensor(
                            t32b[:, b * 512:(b + 1) * 512], t32b[:, b * 512:(b + 1) * 512],
                            macc[:, (mc - 2) * 1024 + q2 * 512:(mc - 2) * 1024 + (q2 + 1) * 512],
                            OP.add), inc=(self.sV, 1))
                        k.wait("scalar", self.sV)
                    else:
                        k.wait("scalar", self.sB, ld)
                    k.op("scalar", lambda e, mc=mc, tch=tch, q2=q2, b=b: e.activation(
                        self.gbf[mc][:, tch * 1024 + q2 * 512:tch * 1024 + (q2 + 1) * 512],
                        t32b[:, b * 512:(b + 1) * 512], AF.Silu), inc=(self.sA, 1))
        self.tap("g", [self.gbf[i][:] for i in range(8)])
        k.barrier()
        for key in (xfh_key, mid0_key, mid1h_key, macc_key, wbc_key, t32b_key,
                    wbf2_key):
            self.ar.release(key)

    def phase_mamba(self, d):
        k, cst, pb = self.k, self.cst, self.pb
        fwd = (d == "f")
        ext, ext_key = self.ar.t32(16 * (TC + 3))
        ubf, ubf_key = self.ar.t16(16 * TC)
        Lt, Lt_key = self.ar.t32(16 * TC)
        dtu, dtu_key = self.ar.t16(16 * TC)
        sgb, sgb_key = self.ar.t16(16 * TC)
        hbuf, hbuf_key = self.ar.t32(16 * TC)
        yg, yg_key = self.ar.t16(16 * TC)
        yt, yt_key = self.ar.t32(2 * TC)
        ab, ab_key = self.ar.t32(4 * TC)
        stk, stk_key = self.ar.t16(2 * 16 * TC)
        dtv, dtv_key = self.ar.t16(2 * TC)
        bc16, bc16_key = self.ar.t16(TC)
        wsl, wsl_key = self.ar.t16(2 * 2048)
        sfb, sfb_key = self.ar.t16(TC)
        xre, xre_key = self.ar.t32(TC)
        sfr, sfr_key = self.ar.t16(TC)
        state = self.state
        k.op("vector", lambda e: e.memset(state[:], 0.0), inc=(self.sV, 1))
        for dc in range(16):
            k.op("vector", lambda e, dc=dc: e.memset(
                ext[:, dc * (TC + 3):dc * (TC + 3) + (TC + 3)], 0.0), inc=(self.sV, 1))
        k.barrier(("vector", "tensor", "scalar", "sync", "gpsimd"))

        convw = cst["convw_" + d]
        convb = cst["convb_" + d]
        aexp = self.aexp[d]
        ndtb = self.ndtb[d]
        dvec = cst["dvec_" + d]
        xp = self.xpbf[d]
        dtp = self.dtpbf[d]
        chunks = list(range(NCH)) if fwd else list(range(NCH - 1, -1, -1))
        slab_cons = [0, 0]
        stk_cons = 0

        for ci, c in enumerate(chunks):
            tsl = slice(c * TC, (c + 1) * TC)
            if ci > 0:
                for dc in range(16):
                    eb = ext[:, dc * (TC + 3):(dc + 1) * (TC + 3)]
                    if fwd:
                        k.op("vector", lambda e, eb=eb: e.tensor_copy(
                            eb[:, 0:3], eb[:, TC:TC + 3]), inc=(self.sV, 1))
                    else:
                        k.op("vector", lambda e, eb=eb: e.tensor_copy(
                            eb[:, TC:TC + 3], eb[:, 0:3]), inc=(self.sV, 1))
            hist_saved = self.sV.n
            jid = 0
            evac_done = 0
            for mcp in range(8):
                b = mcp % 2
                k.wait("gpsimd", self.sT, slab_cons[b])
                for kc in range(8):
                    k.dma("gpsimd", wsl[:, b * 2048 + kc * 256:b * 2048 + (kc + 1) * 256],
                          self.scr_wip.ap()[kc * 128:(kc + 1) * 128,
                                            mcp * 256:(mcp + 1) * 256], self.sXp[b])
                wld = self.sXp[b].n
                k.wait("tensor", self.sXp[b], wld)
                for m2 in range(2):
                    mc = mcp * 2 + m2
                    bank = pb[jid % 2]
                    k.wait("tensor", self.sV, evac_done if jid >= 2 else 0)
                    if jid < 2:
                        k.wait("tensor", self.sV, hist_saved)
                        k.wait("tensor", self.sA)
                    for kc in range(8):
                        k.op("tensor", lambda e, b=b, kc=kc, m2=m2, bank=bank, tsl=tsl: e.matmul(
                            bank[:, 0:TC], wsl[:, b * 2048 + kc * 256 + m2 * 128:
                                               b * 2048 + kc * 256 + (m2 + 1) * 128],
                            self.xnbf[kc][:, tsl], start=(kc == 0), stop=(kc == 7)),
                            inc=(self.sT, 1))
                    k.wait("vector", self.sT)
                    k.op("vector", lambda e, mc=mc, bank=bank: e.tensor_copy(
                        ext[:, mc * (TC + 3) + (3 if fwd else 0):
                            mc * (TC + 3) + (3 if fwd else 0) + TC], bank[:, 0:TC]),
                        inc=(self.sV, 1))
                    evac_done = self.sV.n
                    jid += 1
                slab_cons[b] = self.sT.n
            k.wait("vector", self.sV, evac_done)
            k.wait("vector", self.sO)
            conv_done = []
            for dc in range(16):
                eb = ext[:, dc * (TC + 3):(dc + 1) * (TC + 3)]
                tt = yt[:, (dc % 2) * TC:(dc % 2) * TC + TC]
                if len(conv_done) >= 2:
                    k.wait("vector", self.sA, conv_done[-2])
                wcol = lambda kk2, dc=dc: convw[:, dc * 4 + kk2:dc * 4 + kk2 + 1]
                offs = (lambda kk2: kk2) if fwd else (lambda kk2: 3 - kk2)
                k.op("vector", lambda e, eb=eb, tt=tt, wcol=wcol, offs=offs: e.tensor_scalar(
                    tt, eb[:, offs(0):offs(0) + TC], wcol(0), None, OP.mult), inc=(self.sV, 1))
                for kk2 in range(1, 4):
                    k.vw()
                    k.op("vector", lambda e, eb=eb, tt=tt, wcol=wcol, offs=offs, kk2=kk2:
                         e.scalar_tensor_tensor(tt, eb[:, offs(kk2):offs(kk2) + TC],
                                                wcol(kk2), tt, OP.mult, OP.add),
                         inc=(self.sV, 1))
                k.wait("scalar", self.sV)
                k.op("scalar", lambda e, dc=dc, tt=tt: e.activation(
                    ubf[:, dc * TC:(dc + 1) * TC], tt, AF.Silu,
                    bias=convb[:, dc:dc + 1]), inc=(self.sA, 1))
                conv_done.append(self.sA.n)
            usil = self.sA.n
            for mcp in range(8, 16):
                b = mcp % 2
                k.wait("gpsimd", self.sT, slab_cons[b])
                for kc in range(8):
                    k.dma("gpsimd", wsl[:, b * 2048 + kc * 256:b * 2048 + (kc + 1) * 256],
                          self.scr_wip.ap()[kc * 128:(kc + 1) * 128,
                                            mcp * 256:(mcp + 1) * 256], self.sXp[b])
                wld = self.sXp[b].n
                k.wait("tensor", self.sXp[b], wld)
                for m2 in range(2):
                    mc = (mcp - 8) * 2 + m2
                    bank = pb[jid % 2]
                    k.wait("tensor", self.sA)
                    k.wait("tensor", self.sV, evac_done)
                    for kc in range(8):
                        k.op("tensor", lambda e, b=b, kc=kc, m2=m2, bank=bank, tsl=tsl: e.matmul(
                            bank[:, 0:TC], wsl[:, b * 2048 + kc * 256 + m2 * 128:
                                               b * 2048 + kc * 256 + (m2 + 1) * 128],
                            self.xnbf[kc][:, tsl], start=(kc == 0), stop=(kc == 7)),
                            inc=(self.sT, 1))
                    k.wait("scalar", self.sT)
                    k.op("scalar", lambda e, mc=mc, bank=bank: e.activation(
                        sgb[:, mc * TC:(mc + 1) * TC], bank[:, 0:TC], AF.Silu),
                        inc=(self.sA, 1))
                    jid += 1
                slab_cons[b] = self.sT.n
            k.wait("tensor", self.sA, usil)
            for dc in range(16):
                k.op("tensor", lambda e, dc=dc: e.matmul(
                    pb[4][0:96, 0:TC], xp[:, dc * 96:(dc + 1) * 96],
                    ubf[:, dc * TC:(dc + 1) * TC], start=(dc == 0), stop=(dc == 15)),
                    inc=(self.sT, 1))
            xdblv = self.sT.n
            k.wait("vector", self.sT, xdblv)
            k.op("vector", lambda e: e.tensor_copy(dtv[0:DTR, 0:TC], pb[4][0:DTR, 0:TC]),
                 inc=(self.sV, 1))
            k.wait("scalar", self.sT, xdblv)
            k.op("scalar", lambda e: e.activation(
                bc16[DTR:96, 0:TC], pb[4][DTR:96, 0:TC], AF.Copy,
                scale=cst["c_negmix"][DTR:96, 0:1]), inc=(self.sA, 1))
            bcv = self.sA.n
            bcb = ci % 2
            k.wait("scalar", self.sA)
            sp = k.dma("scalar", self.scr_bc.ap()[bcb:bcb + 1, :].rearrange(
                "a (r t) -> (a r) t", t=TC), bc16[DTR:96, 0:TC], self.sD)
            k.wait("gpsimd", self.sD, sp)
            k.wait("gpsimd", self.sV, stk_cons)
            k.dma("gpsimd", stk[:, 0:16 * TC],
                  bass.AP(self.scr_bc, bcb * 32 * TC, [[0, 128], [1, 16 * TC]]), self.sC)
            k.dma("gpsimd", stk[:, 16 * TC:32 * TC],
                  bass.AP(self.scr_bc, bcb * 32 * TC + 16 * TC, [[0, 128], [1, 16 * TC]]),
                  self.sC)
            stks = self.sC.n
            if ci == 0 and fwd and "stk0" in self.taps:
                self.k.wait("gpsimd", self.sX)
                self.tap("stk0", stk[:, 0:32 * TC])
            dtmm = []
            k.wait("tensor", self.sV)
            k.wait("tensor", self.sA, bcv)
            for mc in range(16):
                bank = pb[4 + mc % 2]
                if mc >= 2:
                    k.wait("tensor", self.sA, dtmm[-2])
                k.op("tensor", lambda e, mc=mc, bank=bank: e.matmul(
                    bank[:, 0:TC], dtp[:, mc * 128:(mc + 1) * 128], dtv[0:DTR, 0:TC],
                    start=True, stop=True), inc=(self.sT, 1))
                k.wait("scalar", self.sT)
                k.op("scalar", lambda e, mc=mc, bank=bank: e.activation(
                    Lt[:, mc * TC:(mc + 1) * TC], bank[:, 0:TC], AF.Sigmoid,
                    bias=ndtb[:, mc:mc + 1], scale=cst["c_sc"][:, 4:5]), inc=(self.sA, 1))
                dtmm.append(self.sA.n)
            for mc in range(16):
                k.op("scalar", lambda e, mc=mc: e.activation(
                    Lt[:, mc * TC:(mc + 1) * TC], Lt[:, mc * TC:(mc + 1) * TC],
                    AF.Ln), inc=(self.sA, 1))
            lnv = self.sA.n
            k.wait("vector", self.sA, lnv)
            for dc in range(16):
                k.op("vector", lambda e, dc=dc: e.tensor_tensor(
                    dtu[:, dc * TC:(dc + 1) * TC], Lt[:, dc * TC:(dc + 1) * TC],
                    ubf[:, dc * TC:(dc + 1) * TC], OP.mult), inc=(self.sV, 1))
            dtuv = self.sV.n
            k.wait("vector", self.sC, stks)
            k.wait("scalar", self.sA, lnv)
            scan_done = []

            def bform(e, dc, n, b_t):
                return e.tensor_tensor(b_t, dtu[:, dc * TC:(dc + 1) * TC],
                                       stk[:, n * TC:(n + 1) * TC], OP.mult)
            for dc in range(16):
                k.op("vector", lambda e, dc=dc: bform(e, dc, 0, ab[:, 2 * TC:3 * TC]),
                     inc=(self.sV, 1))
                for n in range(NS):
                    rr = n % 2
                    a_t = ab[:, rr * TC:(rr + 1) * TC]
                    b_t = ab[:, (2 + rr) * TC:(3 + rr) * TC]
                    if len(scan_done) >= 2:
                        k.wait("scalar", self.sV, scan_done[-2])
                    k.op("scalar", lambda e, dc=dc, n=n, a_t=a_t: e.activation(
                        a_t, Lt[:, dc * TC:(dc + 1) * TC], AF.Exp,
                        scale=aexp[:, dc * 16 + n:dc * 16 + n + 1]), inc=(self.sA, 1))
                    if n < NS - 1:
                        nrr = (n + 1) % 2
                        k.op("vector", lambda e, dc=dc, n=n, nrr=nrr: bform(
                            e, dc, n + 1, ab[:, (2 + nrr) * TC:(3 + nrr) * TC]),
                            inc=(self.sV, 1))
                    k.wait("vector", self.sA)
                    if not fwd:
                        k.vw()
                    hsl = hbuf[:, n * TC:(n + 1) * TC]
                    if fwd:
                        k.op("vector", lambda e, hsl=hsl, a_t=a_t, b_t=b_t, dc=dc, n=n:
                             e.tensor_tensor_scan(hsl, a_t, b_t,
                                                  state[:, dc * 16 + n:dc * 16 + n + 1],
                                                  OP.mult, OP.add), inc=(self.sV, 1))
                    else:
                        k.op("vector", lambda e, hsl=hsl, a_t=a_t, b_t=b_t, dc=dc, n=n:
                             e.tensor_tensor_scan(hsl[:, ::-1], a_t[:, ::-1], b_t[:, ::-1],
                                                  state[:, dc * 16 + n:dc * 16 + n + 1],
                                                  OP.mult, OP.add), inc=(self.sV, 1))
                    scan_done.append(self.sV.n)
                lastcol = TC - 1 if fwd else 0
                k.vw()
                k.op("vector", lambda e, dc=dc, lastcol=lastcol: e.tensor_copy(
                    state[:, dc * 16:(dc + 1) * 16], hbuf[:, lastcol::TC]),
                    inc=(self.sV, 1))
                for n in range(NS):
                    k.op("vector", lambda e, n=n: e.tensor_tensor(
                        hbuf[:, n * TC:(n + 1) * TC], hbuf[:, n * TC:(n + 1) * TC],
                        stk[:, (16 + n) * TC:(16 + n + 1) * TC], OP.mult), inc=(self.sV, 1))
                ytile = yt[:, (dc % 2) * TC:(dc % 2) * TC + TC]
                hv = hbuf[:, 0:16 * TC].rearrange("p (n t) -> p t n", n=16)
                k.op("vector", lambda e, hv=hv, ytile=ytile: e.tensor_reduce(
                    ytile, hv, AX.X, OP.add), inc=(self.sV, 1))
                k.vw()
                k.op("vector", lambda e, dc=dc, ytile=ytile: e.scalar_tensor_tensor(
                    ytile, ubf[:, dc * TC:(dc + 1) * TC], dvec[:, dc:dc + 1],
                    ytile, OP.mult, OP.add), inc=(self.sV, 1))
                k.op("vector", lambda e, dc=dc, ytile=ytile: e.tensor_tensor(
                    yg[:, dc * TC:(dc + 1) * TC], ytile, sgb[:, dc * TC:(dc + 1) * TC],
                    OP.mult), inc=(self.sV, 1))
            stk_cons = self.sV.n
            ygv = self.sV.n
            if ci == 0 and fwd and "yg0" in self.taps:
                self.tap("ub0", ubf[:, 0:16 * TC])
                self.tap("lt0", Lt[:, 0:16 * TC])
                self.tap("sg0", sgb[:, 0:16 * TC])
                self.tap("yg0", yg[:, 0:16 * TC])
            k.wait("tensor", self.sV, ygv)
            prev_comb = 0
            opbank_cons = [0, 0]
            for mc in range(8):
                b = mc % 2
                k.wait("gpsimd", self.sT, slab_cons[b])
                for kc in range(16):
                    k.dma("gpsimd", wsl[:, b * 2048 + kc * 128:b * 2048 + (kc + 1) * 128],
                          self.scr_wop.ap()[kc * 128:(kc + 1) * 128,
                                            mc * 128:(mc + 1) * 128], self.sXp[b])
                wld = self.sXp[b].n
                k.wait("tensor", self.sXp[b], wld)
                bank = pb[6 + mc % 2]
                k.wait("tensor", self.sV, opbank_cons[mc % 2])
                for kc in range(16):
                    k.op("tensor", lambda e, b=b, kc=kc, bank=bank: e.matmul(
                        bank[:, 0:TC], wsl[:, b * 2048 + kc * 128:b * 2048 + (kc + 1) * 128],
                        yg[:, kc * TC:(kc + 1) * TC], start=(kc == 0), stop=(kc == 15)),
                        inc=(self.sT, 1))
                smm = self.sT.n
                slab_cons[b] = smm
                if fwd:
                    k.wait("vector", self.sT, smm)
                    k.wait("vector", self.sO)
                    k.op("vector", lambda e, bank=bank: e.tensor_copy(
                        sfb[:, 0:TC], bank[:, 0:TC]), inc=(self.sV, 1))
                    opbank_cons[mc % 2] = self.sV.n
                    k.wait("gpsimd", self.sV)
                    k.dma("gpsimd", self.scr_sf.ap()[mc * 128:(mc + 1) * 128, tsl],
                          sfb[:, 0:TC], self.sO)
                else:
                    k.wait("gpsimd", self.sV, prev_comb)
                    ldx = k.dma("gpsimd", xre[:, 0:TC],
                                self.dx.ap()[mc * 128:(mc + 1) * 128, tsl], self.sX)
                    lds = k.dma("gpsimd", sfr[:, 0:TC],
                                self.scr_sf.ap()[mc * 128:(mc + 1) * 128, tsl], self.sX)
                    k.wait("vector", self.sT, smm)
                    k.wait("vector", self.sX, lds)
                    k.wait("vector", self.sO)
                    k.op("vector", lambda e, bank=bank: e.tensor_tensor(
                        yt[:, 0:TC], bank[:, 0:TC], sfr[:, 0:TC], OP.add), inc=(self.sV, 1))
                    opbank_cons[mc % 2] = self.sV.n
                    k.vw()
                    k.op("vector", lambda e, mc=mc, tsl=tsl: e.tensor_tensor(
                        yt[:, 0:TC], yt[:, 0:TC], self.gbf[mc][:, tsl], OP.mult),
                        inc=(self.sV, 1))
                    k.vw()
                    k.op("vector", lambda e: e.tensor_tensor(
                        yt[:, 0:TC], yt[:, 0:TC], xre[:, 0:TC], OP.add), inc=(self.sV, 1))
                    prev_comb = self.sV.n
                    k.wait("gpsimd", self.sV, prev_comb)
                    k.dma("gpsimd", self.dout.ap()[mc * 128:(mc + 1) * 128, tsl],
                          yt[:, 0:TC], self.sO)
            if CHUNK_BARRIER:
                k.barrier()
        k.barrier()
        for key in (ext_key, ubf_key, Lt_key, dtu_key, sgb_key, hbuf_key, yg_key,
                    yt_key, ab_key, stk_key, dtv_key, bc16_key, wsl_key, sfb_key,
                    xre_key, sfr_key):
            self.ar.release(key)


_CACHE = {}


def _get_nc(taps=()):
    key = tuple(sorted(taps))
    if key not in _CACHE:
        p = Prog(taps)
        nc = p.build()
        _CACHE[key] = (p, nc)
    return _CACHE[key]


def kernel(**inputs):
    sh, percore = host_arrays(inputs)
    p, nc = _get_nc()
    in_maps = []
    for c in range(8):
        m = dict(sh)
        m["x_fm"] = percore[c]["x_fm"]
        in_maps.append(m)
    res = run_bass_kernel_spmd(nc, in_maps, core_ids=list(range(8)))
    outs = [res.results[c]["out_fm"].T for c in range(8)]
    return np.ascontiguousarray(np.stack(outs)).astype(np.float32)


# version 09 (final)
# speedup vs baseline: 1.0776x; 1.0551x over previous
import numpy as np
import ml_dtypes
from contextlib import ExitStack

import concourse.bass as bass
import concourse.mybir as mybir
from concourse.alu_op_type import AluOpType as OP
from concourse.bass_utils import run_bass_kernel_spmd

AF = mybir.ActivationFunctionType
AX = mybir.AxisListType
F32 = mybir.dt.float32
BF16 = mybir.dt.bfloat16
NPBF = ml_dtypes.bfloat16

D = 1024
DI = 2048
NS = 16
DTR = 64
L = 2048
H, HD = 16, 64
TC = 256
NCH = L // TC
EPS_RMS = float(np.float32(1.1920929e-07))
EPS_LN = 1e-5
NEG = -1e30

ENGINES = ("sync", "tensor", "vector", "scalar", "gpsimd")
CHUNK_BARRIER = False


class Sem:
    __slots__ = ("h", "n", "name")
    def __init__(self, h, name):
        self.h, self.n, self.name = h, 0, name


class Arena:
    def __init__(self, tensor, cols):
        self.t = tensor
        self.cols = cols
        self.free = [(0, cols)]
        self.peak = 0

    def _alloc(self, n, align):
        for i, (off, sz) in enumerate(self.free):
            a = (off + align - 1) // align * align
            pad = a - off
            if sz - pad >= n:
                ent = []
                if pad:
                    ent.append((off, pad))
                if sz - pad - n:
                    ent.append((a + n, sz - pad - n))
                self.free[i:i + 1] = ent
                self.peak = max(self.peak, self.cols - sum(s for _, s in self.free))
                return a
        raise MemoryError(f"arena OOM want {n} free={self.free}")

    def release(self, key):
        off, n = key
        self.free.append((off, n))
        self.free.sort()
        m = []
        for o, s in self.free:
            if m and m[-1][0] + m[-1][1] == o:
                m[-1] = (m[-1][0], m[-1][1] + s)
            else:
                m.append((o, s))
        self.free = m

    def t16(self, n):
        off = self._alloc(n, 16)
        return self.t[:, off:off + n], (off, n)

    def t32(self, n):
        off = self._alloc(2 * n, 16)
        return self.t[:, off:off + 2 * n].bitcast(F32), (off, 2 * n)


class KB:
    def __init__(self):
        self.nc = bass.Bass("TRN2", target_bir_lowering=False)
        self.ctx = ExitStack()
        self.streams = {e: [] for e in ENGINES}
        self.sems = {}

    def sem(self, name):
        s = Sem(self.ctx.enter_context(self.nc.semaphore(name)), name)
        self.sems[name] = s
        return s

    def sb(self, name, shape, dtype):
        return self.ctx.enter_context(self.nc.sbuf_tensor(name, list(shape), dtype))

    def ps(self, name, shape, dtype=F32):
        return self.ctx.enter_context(self.nc.psum_tensor(name, list(shape), dtype))

    def op(self, eng, fn, inc=None):
        if inc is not None:
            sem, kk = inc
            sem.n += kk
            def run(e, fn=fn, sem=sem, kk=kk):
                fn(e).then_inc(sem.h, kk)
        else:
            def run(e, fn=fn):
                fn(e)
        self.streams[eng].append(run)

    def vw(self):
        self.wait("vector", self.sems["sV"])

    def wait(self, eng, sem, val=None):
        v = sem.n if val is None else val
        assert v <= sem.n, f"{eng} waits {sem.name}>={v} planned {sem.n}"
        if v > 0:
            self.streams[eng].append(lambda e, s=sem, v=v: e.wait_ge(s.h, v))

    def dma(self, eng, out, in_, sem, slow=False):
        if slow:
            self.op(eng, lambda e, o=out, i=in_: e.dma_start(
                out=o, in_=i, allow_slow_non_contiguous=True), inc=(sem, 16))
        else:
            self.op(eng, lambda e, o=out, i=in_: e.dma_start(out=o, in_=i), inc=(sem, 16))
        return sem.n

    def barrier(self, engines=ENGINES):
        if not hasattr(self, "dma_owner"):
            snap = [(s, s.n) for s in self.sems.values() if s.n > 0]
            for e in engines:
                for s, v in snap:
                    self.wait(e, s, v)
            return
        comp = [self.sems[n] for n in ("sT", "sV", "sA", "sG") if n in self.sems]
        for e in ENGINES:
            for s in self.dma_owner[e]:
                if s.n > 0:
                    self.wait(e, s, s.n)
            self.streams[e].append(lambda eng, b=self.bar: eng.sem_inc(b.h, 1))
            self.bar.n += 1
        tgt = self.bar.n
        for e in ENGINES:
            self.wait(e, self.bar, tgt)
            for s in comp:
                if s.n > 0:
                    self.wait(e, s, s.n)

    def finalize(self):
        with self.nc.Block() as blk:
            for name in ENGINES:
                def body(eng, name=name):
                    for f in self.streams[name]:
                        f(eng)
                getattr(blk, name)(body)
        return self.nc



def host_arrays(inputs):
    f32 = np.float32
    def T(a):
        return np.ascontiguousarray(np.asarray(a, f32).T)
    def chanvec(v, d):
        return np.ascontiguousarray(np.asarray(v, f32).reshape(d // 128, 128).T)

    sh = {
        "w_inproj": T(inputs["L0_attn_in_proj_w"]),
        "w_xproj_f": T(inputs["L0_attn_x_proj_w"]),
        "w_xproj_b": T(inputs["L0_attn_x_proj_wb"]),
        "w_dtproj_f": T(inputs["L0_attn_dt_proj_w"]),
        "w_dtproj_b": T(inputs["L0_attn_dt_proj_wb"]),
        "w_outproj": T(inputs["L0_attn_out_proj_w"]),
        "convw_f": np.ascontiguousarray(np.asarray(inputs["L0_attn_conv_w"], f32)
                                        .reshape(16, 128, 4).transpose(1, 0, 2).reshape(128, 64)),
        "convw_b": np.ascontiguousarray(np.asarray(inputs["L0_attn_conv_wb"], f32)
                                        .reshape(16, 128, 4).transpose(1, 0, 2).reshape(128, 64)),
        "convb_f": chanvec(inputs["L0_attn_conv_b"], DI),
        "convb_b": chanvec(inputs["L0_attn_conv_bb"], DI),
        "dtb_f": chanvec(inputs["L0_attn_dt_proj_b"], DI),
        "dtb_b": chanvec(inputs["L0_attn_dt_proj_bb"], DI),
        "alog_f": np.ascontiguousarray(np.asarray(inputs["L0_attn_A_log"], f32)
                                       .reshape(16, 128, NS).transpose(1, 0, 2).reshape(128, 256)),
        "alog_b": np.ascontiguousarray(np.asarray(inputs["L0_attn_A_b_log"], f32)
                                       .reshape(16, 128, NS).transpose(1, 0, 2).reshape(128, 256)),
        "dvec_f": chanvec(inputs["L0_attn_D"], DI),
        "dvec_b": chanvec(inputs["L0_attn_D_b"], DI),
        "w_q": T(inputs["L0_ffn_wq"]), "w_k": T(inputs["L0_ffn_wk"]),
        "w_v": T(inputs["L0_ffn_wv"]), "w_o": T(inputs["L0_ffn_wo"]),
        "w_gate": T(inputs["L0_ffn_gate_w"]),
        "n1w": chanvec(inputs["L0_ffn_n1_w"], D),
        "n2w": chanvec(inputs["L0_ffn_n2_w"], D),
        "qnw": np.ascontiguousarray(np.tile(np.asarray(inputs["L0_ffn_qn_w"], f32), 2)[:, None]),
        "knw": np.ascontiguousarray(np.tile(np.asarray(inputs["L0_ffn_kn_w"], f32), 2)[:, None]),
        "qnrow": np.asarray(inputs["L0_ffn_qn_w"], f32)[None, :].copy(),
        "knrow": np.asarray(inputs["L0_ffn_kn_w"], f32)[None, :].copy(),
        "w_e0up": T(inputs["L0_ffn_e0_up"]), "w_e0dn": T(inputs["L0_ffn_e0_down"]),
        "w_e1a": T(inputs["L0_ffn_e1_w1"]), "w_e1b": T(inputs["L0_ffn_e1_w2"]),
        "w_e1dn": T(inputs["L0_ffn_e1_down"]),
        "w_e2a": T(inputs["L0_ffn_e2_l1"]), "w_e2b": T(inputs["L0_ffn_e2_l2"]),
        "w_e2c": T(inputs["L0_ffn_e2_l3"]), "w_e2d": T(inputs["L0_ffn_e2_l4"]),
        "w_e3a": T(inputs["L0_ffn_e3_down"]), "w_e3b": T(inputs["L0_ffn_e3_up"]),
        "w_e3c": T(inputs["L0_ffn_e3_out"]),
        "c_id4": np.eye(4, dtype=f32),
    }
    kk, qq = np.meshgrid(np.arange(128), np.arange(128), indexing="ij")
    sh["c_maskT"] = np.where(kk <= qq, 0.0, NEG).astype(f32)
    sh["c_ones"] = np.ones((128, 128), NPBF)
    bd = np.zeros((128, 2), NPBF); bd[:64, 0] = 1; bd[64:, 1] = 1
    sh["c_bd64"] = bd
    ex = np.zeros((2, 128), NPBF); ex[0, :64] = 1; ex[1, 64:] = 1
    sh["c_ex64"] = ex
    z1 = np.zeros((128, 65), NPBF); z1[:, 64] = 1
    sh["c_z64o"] = z1
    hl = np.zeros((128, 128), NPBF); hl[:, :64] = 1
    sh["c_hlo"] = hl
    hh = np.zeros((128, 128), NPBF); hh[:, 64:] = 1
    sh["c_hhi"] = hh
    sh["c_idbf"] = np.eye(128, dtype=NPBF)
    sh["c_sc"] = np.tile(np.array([EPS_LN, EPS_RMS, 1.0 / HD, 1.0 / D,
                                   -1.0, 15.0, 1.0, 0.0], f32), (128, 1))
    nm = np.ones((128, 1), f32); nm[64:80] = -1.0
    sh["c_negmix"] = nm
    x = np.asarray(inputs["L0_attn_hidden_states"], f32)
    percore = [{"x_fm": np.ascontiguousarray(x[c].T)} for c in range(x.shape[0])]
    return sh, percore


WSPEC_F32 = [
    ("w_inproj", [D, 2 * DI]), ("w_xproj_f", [DI, 96]), ("w_xproj_b", [DI, 96]),
    ("w_dtproj_f", [DTR, DI]), ("w_dtproj_b", [DTR, DI]), ("w_outproj", [DI, D]),
    ("convw_f", [128, 64]), ("convw_b", [128, 64]),
    ("convb_f", [128, 16]), ("convb_b", [128, 16]),
    ("dtb_f", [128, 16]), ("dtb_b", [128, 16]),
    ("alog_f", [128, 256]), ("alog_b", [128, 256]),
    ("dvec_f", [128, 16]), ("dvec_b", [128, 16]),
    ("w_q", [D, D]), ("w_k", [D, D]), ("w_v", [D, D]), ("w_o", [D, D]),
    ("w_gate", [D, 4]), ("n1w", [128, 8]), ("n2w", [128, 8]),
    ("qnw", [128, 1]), ("knw", [128, 1]), ("qnrow", [1, HD]), ("knrow", [1, HD]),
    ("w_e0up", [D, DI]), ("w_e0dn", [DI, D]),
    ("w_e1a", [D, DI]), ("w_e1b", [D, DI]), ("w_e1dn", [DI, D]),
    ("w_e2a", [D, DI]), ("w_e2b", [DI, DI]), ("w_e2c", [DI, DI]), ("w_e2d", [DI, D]),
    ("w_e3a", [D, D]), ("w_e3b", [D, DI]), ("w_e3c", [DI, D]),
    ("c_maskT", [128, 128]), ("c_id4", [4, 4]), ("c_sc", [128, 8]),
    ("c_negmix", [128, 1]),
]
WSPEC_BF = [("c_ones", [128, 128]), ("c_bd64", [128, 2]), ("c_ex64", [2, 128]),
            ("c_z64o", [128, 65]), ("c_hlo", [128, 128]), ("c_hhi", [128, 128]),
            ("c_idbf", [128, 128])]

SMALL = ["convw_f", "convw_b", "convb_f", "convb_b", "dtb_f", "dtb_b",
         "alog_f", "alog_b", "dvec_f", "dvec_b", "n1w", "n2w",
         "qnw", "knw", "qnrow", "knrow", "c_maskT", "c_id4", "c_sc", "c_negmix"]


class Prog:
    def __init__(self, taps=()):
        self.k = KB()
        self.taps = set(taps)
        self.dbg = {}
        nc = self.k.nc
        self.dx = nc.dram_tensor("x_fm", [D, L], F32, kind="ExternalInput")
        self.dout = nc.dram_tensor("out_fm", [D, L], F32, kind="ExternalOutput")
        self.dw = {n: nc.dram_tensor(n, s, F32, kind="ExternalInput") for n, s in WSPEC_F32}
        for n, s in WSPEC_BF:
            self.dw[n] = nc.dram_tensor(n, s, BF16, kind="ExternalInput")
        self.scr_xn = nc.dram_tensor("scr_xn", [D, L], F32, kind="Internal")
        self.scr_xffn = nc.dram_tensor("scr_xffn", [D, L], F32, kind="Internal")
        self.scr_sf = nc.dram_tensor("scr_sf", [D, L], BF16, kind="Internal")
        self.scr_bc = nc.dram_tensor("scr_bc", [2, 32 * TC], BF16, kind="Internal")
        self.scr_row = nc.dram_tensor("scr_row", [8, L], F32, kind="Internal")
        self.scr_gw = nc.dram_tensor("scr_gw", [4, L], BF16, kind="Internal")
        self.scr_wip = nc.dram_tensor("scr_wip", [D, 2 * DI], BF16, kind="Internal")
        self.scr_wop = nc.dram_tensor("scr_wop", [DI, D], BF16, kind="Internal")
        self.scrw = {}
        for n, s, fold in [
            ("w_q", [D, D], 1), ("w_k", [D, D], 1), ("w_v", [D, D], 1),
            ("w_o", [D, D], 0),
            ("w_e0up", [D, DI], 2), ("w_e0dn", [DI, D], 0),
            ("w_e1a", [D, DI], 2), ("w_e1b", [D, DI], 2), ("w_e1dn", [DI, D], 0),
            ("w_e2a", [D, DI], 2), ("w_e2b", [DI, DI], 0), ("w_e2c", [DI, DI], 0),
            ("w_e2d", [DI, D], 0),
            ("w_e3a", [D, D], 2), ("w_e3b", [D, DI], 0), ("w_e3c", [DI, D], 0),
        ]:
            self.scrw[n] = (nc.dram_tensor("scrw_" + n, s, BF16, kind="Internal"), fold)

        k = self.k
        self.sW = k.sem("sW")
        self.sX = k.sem("sX")
        self.sT = k.sem("sT")
        self.sV = k.sem("sV")
        self.sA = k.sem("sA")
        self.sG = k.sem("sG")
        self.sO = k.sem("sO")
        self.sWp = [k.sem("sWa"), k.sem("sWb")]
        self.sXp = [k.sem("sXa"), k.sem("sXb")]
        self.sB = k.sem("sB")
        self.sC = k.sem("sC")
        self.sD = k.sem("sD")
        self.sBar = k.sem("sBar")
        k.dma_owner = {"sync": [self.sW, self.sWp[0], self.sWp[1], self.sB],
                       "gpsimd": [self.sX, self.sXp[0], self.sXp[1], self.sO, self.sC],
                       "scalar": [self.sD], "vector": [], "tensor": []}
        k.bar = self.sBar

        self.pb = [k.ps(f"pb{i}", [128, 512]) for i in range(8)]

        self.xnbf = [k.sb(f"xnbf{i}", [128, L], BF16) for i in range(8)]
        self.gbf = [k.sb(f"gbf{i}", [128, L], BF16) for i in range(8)]
        self.cst = {}
        for n, s in WSPEC_BF:
            self.cst[n] = k.sb(n + "_s", s, BF16)
        for n in SMALL:
            self.cst[n] = k.sb(n + "_s", dict(WSPEC_F32)[n], F32)
        self.aexp = {"f": k.sb("aexp_f", [128, 256], F32),
                     "b": k.sb("aexp_b", [128, 256], F32)}
        self.ndtb = {"f": k.sb("ndtb_f", [128, 16], F32),
                     "b": k.sb("ndtb_b", [128, 16], F32)}
        self.qnw8 = k.sb("qnw8", [128, 1], F32)
        self.bcap = k.sb("bcap", [128, 1], F32)
        self.xpbf = {"f": k.sb("xpbf_f", [128, 16 * 96], BF16),
                     "b": k.sb("xpbf_b", [128, 16 * 96], BF16)}
        self.dtpbf = {"f": k.sb("dtpbf_f", [DTR, DI], BF16),
                      "b": k.sb("dtpbf_b", [DTR, DI], BF16)}
        self.state = k.sb("state", [128, 256], F32)
        self.lgt = k.sb("lgt", [4, 512], F32)

        ACOLS = 60928
        self.ar = Arena(k.sb("arena", [128, ACOLS], BF16), ACOLS)

    def tap(self, name, ap_or_list):
        if name not in self.taps:
            return
        aps = ap_or_list if isinstance(ap_or_list, list) else [ap_or_list]
        w = aps[0].shape[-1]
        t = self.k.nc.dram_tensor("dbg_" + name, [len(aps) * 128, w],
                                  aps[0].dtype, kind="ExternalOutput")
        self.dbg[name] = t
        self.k.wait("gpsimd", self.sV)
        self.k.wait("gpsimd", self.sA)
        self.k.wait("gpsimd", self.sT)
        for j, a in enumerate(aps):
            self.k.dma("gpsimd", t.ap()[j * 128:j * 128 + a.shape[0], :], a, self.sO)
        for e in ENGINES:
            self.k.wait(e, self.sO)

    def build(self, upto="all"):
        k = self.k
        self.phase_load()
        self.phase_ln()
        k.barrier()
        self.phase_precast()
        if upto == "ln":
            return k.finalize()
        k.barrier()
        self.phase_attn()
        if upto == "attn":
            return k.finalize()
        k.barrier()
        self.phase_moe()
        if upto == "moe":
            return k.finalize()
        k.barrier()
        self.phase_mamba("f")
        if upto == "mambaf":
            return k.finalize()
        self.phase_mamba("b")
        k.barrier()
        return k.finalize()

    def phase_load(self):
        k, dw, cst = self.k, self.dw, self.cst
        self.x32, self.x32_key = self.ar.t32(8 * L)
        for i in range(8):
            k.dma("sync", self.x32[:, i * L:(i + 1) * L],
                  self.dx.ap()[i * 128:(i + 1) * 128, :], self.sW)
        for n, _ in WSPEC_BF:
            k.dma("sync", cst[n][:], dw[n].ap(), self.sW)
        for n in SMALL:
            k.dma("sync", cst[n][:], dw[n].ap(), self.sW)
        wl = self.sW.n
        for e in ("vector", "scalar", "tensor"):
            k.wait(e, self.sW, wl)
        for d in ("f", "b"):
            k.op("scalar", lambda e, d=d: e.activation(
                self.aexp[d][:], cst["alog_" + d][:], AF.Exp), inc=(self.sA, 1))
            k.op("vector", lambda e, d=d: e.tensor_scalar(
                self.ndtb[d][:], cst["dtb_" + d][:], -1.0, None, OP.mult), inc=(self.sV, 1))
        k.op("vector", lambda e: e.tensor_scalar(
            self.qnw8[:], cst["qnw"][:], 0.125, None, OP.mult), inc=(self.sV, 1))
        tq, kq = self.ar.t32(4)
        k.op("vector", lambda e: e.reduce_max(
            tq[0:1, 0:1], cst["qnrow"][:], AX.X, apply_absolute_value=True), inc=(self.sV, 1))
        k.op("vector", lambda e: e.reduce_max(
            tq[0:1, 1:2], cst["knrow"][:], AX.X, apply_absolute_value=True), inc=(self.sV, 1))
        k.vw()
        k.op("vector", lambda e: e.tensor_tensor(
            tq[0:1, 2:3], tq[0:1, 0:1], tq[0:1, 1:2], OP.mult), inc=(self.sV, 1))
        k.vw()
        k.op("vector", lambda e: e.tensor_scalar(
            tq[0:1, 3:4], tq[0:1, 2:3], -8.0, 15.0, OP.mult, OP.add), inc=(self.sV, 1))
        k.vw()
        k.wait("gpsimd", self.sV)
        k.dma("gpsimd", self.scr_row.ap()[7:8, 0:1], tq[0:1, 3:4], self.sO)
        k.wait("sync", self.sO)
        k.dma("sync", self.bcap[:], bass.AP(self.scr_row, 7 * L, [[0, 128], [1, 1]]), self.sB)
        self.ar.release(kq)
        st32, key = self.ar.t32(2048)
        for d in ("f", "b"):
            for kc in range(16):
                k.dma("sync", st32[0:128, kc * 96:(kc + 1) * 96],
                      dw["w_xproj_" + d].ap()[kc * 128:(kc + 1) * 128, :], self.sB)
            wv = self.sB.n
            k.wait("vector", self.sB, wv)
            k.op("vector", lambda e, d=d: e.tensor_copy(
                self.xpbf[d][:], st32[:, 0:16 * 96]), inc=(self.sV, 1))
            k.wait("sync", self.sV)
        for d in ("f", "b"):
            k.dma("sync", st32[0:DTR, 0:DI], dw["w_dtproj_" + d].ap(), self.sB)
            k.wait("vector", self.sB)
            k.op("vector", lambda e, d=d: e.tensor_copy(
                self.dtpbf[d][:], st32[0:DTR, 0:DI]), inc=(self.sV, 1))
            k.wait("sync", self.sV)
        k.op("vector", lambda e: e.memset(self.state[:], 0.0), inc=(self.sV, 1))
        self.ar.release(key)

    def phase_ln(self):
        k, cst, pb = self.k, self.cst, self.pb
        x32 = self.x32
        mu, mu_key = self.ar.t32(L)
        rb, rb_key = self.ar.t32(L)
        sqt, sq_key = self.ar.t16(2 * L)
        sq = [sqt[:, 0:L], sqt[:, L:2 * L]]
        cast_done = []
        sq_done = []
        for i in range(8):
            k.op("vector", lambda e, i=i: e.tensor_copy(
                self.xnbf[i][:], x32[:, i * L:(i + 1) * L]), inc=(self.sV, 1))
            cast_done.append(self.sV.n)
        sq_cons = [0, 0]
        for i in range(8):
            if i >= 2:
                k.wait("scalar", self.sT, sq_cons[i % 2])
            k.op("scalar", lambda e, i=i: e.activation(
                sq[i % 2][:], x32[:, i * L:(i + 1) * L], AF.Square), inc=(self.sA, 1))
            sq_done.append(self.sA.n)
            k.wait("tensor", self.sV, cast_done[i])
            k.wait("tensor", self.sA, sq_done[i])
            for qc in range(4):
                k.op("tensor", lambda e, i=i, qc=qc: e.matmul(
                    pb[qc][:, :], cst["c_ones"][:], self.xnbf[i][:, qc * 512:(qc + 1) * 512],
                    start=(i == 0), stop=(i == 7)), inc=(self.sT, 1))
                k.op("tensor", lambda e, i=i, qc=qc: e.matmul(
                    pb[4 + qc][:, :], cst["c_ones"][:], sq[i % 2][:, qc * 512:(qc + 1) * 512],
                    start=(i == 0), stop=(i == 7)), inc=(self.sT, 1))
            sq_cons[i % 2] = self.sT.n
        mm_all = self.sT.n
        k.wait("vector", self.sT, mm_all)
        musq, msq_key = self.ar.t32(L)
        for qc in range(4):
            sl = slice(qc * 512, (qc + 1) * 512)
            k.op("vector", lambda e, qc=qc, sl=sl: e.tensor_scalar(
                mu[:, sl], pb[qc][:, :], 1.0 / D, None, OP.mult), inc=(self.sV, 1))
        k.wait("scalar", self.sV)
        for qc in range(4):
            sl = slice(qc * 512, (qc + 1) * 512)
            k.op("scalar", lambda e, sl=sl: e.activation(
                musq[:, sl], mu[:, sl], AF.Square), inc=(self.sA, 1))
        k.wait("vector", self.sA)
        for qc in range(4):
            sl = slice(qc * 512, (qc + 1) * 512)
            k.op("vector", lambda e, qc=qc, sl=sl: e.scalar_tensor_tensor(
                musq[:, sl], pb[4 + qc][:, :], 1.0 / D, musq[:, sl],
                OP.mult, OP.subtract), inc=(self.sV, 1))
        k.wait("scalar", self.sV)
        k.op("scalar", lambda e: e.activation(
            musq[:, :], musq[:, :], AF.Sqrt, bias=cst["c_sc"][:, 0:1]), inc=(self.sA, 1))
        k.wait("vector", self.sA)
        k.op("vector", lambda e: e.reciprocal(rb[:, :], musq[:, :]), inc=(self.sV, 1))
        k.vw()
        for i in range(8):
            sl = slice(i * L, (i + 1) * L)
            k.op("vector", lambda e, sl=sl: e.tensor_tensor(
                x32[:, sl], x32[:, sl], mu[:, :], OP.subtract), inc=(self.sV, 1))
            k.vw()
            k.op("vector", lambda e, sl=sl: e.tensor_tensor(
                x32[:, sl], x32[:, sl], rb[:, :], OP.mult), inc=(self.sV, 1))
            k.vw()
            k.op("vector", lambda e, i=i, sl=sl: e.tensor_copy(
                self.xnbf[i][:], x32[:, sl]), inc=(self.sV, 1))
            k.wait("gpsimd", self.sV)
            k.dma("gpsimd", self.scr_xn.ap()[i * 128:(i + 1) * 128, :],
                  x32[:, sl], self.sO)
        self.tap("xn", [x32[:, i * L:(i + 1) * L] for i in range(8)])
        k.barrier()
        for key in (mu_key, rb_key, sq_key, msq_key, self.x32_key):
            self.ar.release(key)

    def phase_precast(self):
        k, dw, cst = self.k, self.dw, self.cst
        st, skey = self.ar.t32(2 * 4096)
        bt, bkey = self.ar.t16(2 * 4096)
        jobs = []
        for r in range(8):
            jobs.append((dw["w_inproj"].ap()[r * 128:(r + 1) * 128, :],
                         self.scr_wip.ap()[r * 128:(r + 1) * 128, :], 4096, None))
        for r in range(16):
            jobs.append((dw["w_outproj"].ap()[r * 128:(r + 1) * 128, :],
                         self.scr_wop.ap()[r * 128:(r + 1) * 128, :], 1024, None))
        for n, (scrt, fold) in self.scrw.items():
            kin, kout = dict(WSPEC_F32)[n]
            for r in range(kin // 128):
                fc = None
                if fold == 1:
                    fc = cst["n1w"][:, r:r + 1]
                elif fold == 2:
                    fc = cst["n2w"][:, r:r + 1]
                jobs.append((dw[n].ap()[r * 128:(r + 1) * 128, :],
                             scrt.ap()[r * 128:(r + 1) * 128, :], kout, fc))
        cast = [0, 0]
        spill = [0, 0]
        for j, (srcap, dst, w, fc) in enumerate(jobs):
            b = j % 2
            k.wait("sync", self.sV, cast[b])
            ld = k.dma("sync", st[:, b * 4096:b * 4096 + w], srcap, self.sWp[b])
            k.wait("vector", self.sWp[b], ld)
            k.wait("vector", self.sO, spill[b])
            if fc is None:
                k.op("vector", lambda e, b=b, w=w: e.tensor_copy(
                    bt[:, b * 4096:b * 4096 + w], st[:, b * 4096:b * 4096 + w]),
                    inc=(self.sV, 1))
            else:
                k.op("vector", lambda e, b=b, w=w, fc=fc: e.tensor_scalar(
                    bt[:, b * 4096:b * 4096 + w], st[:, b * 4096:b * 4096 + w],
                    fc, None, OP.mult), inc=(self.sV, 1))
            cast[b] = self.sV.n
            k.wait("gpsimd", self.sV, cast[b])
            spill[b] = k.dma("gpsimd", dst, bt[:, b * 4096:b * 4096 + w], self.sO)
        k.barrier()
        self.ar.release(skey)
        self.ar.release(bkey)

    def _headrms(self, pq_bank, gain_col, out_ap, rot):
        k, cst, pb = self.k, self.cst, self.pb
        sqq, rr32, rrb, rqb = self.hr_bufs[rot % 2]
        bank = self.pb[2 + rot % 2]
        k.wait("scalar", self.sT)
        k.op("scalar", lambda e: e.activation(sqq, pq_bank[:, :], AF.Square),
             inc=(self.sA, 1))
        k.wait("tensor", self.sA)
        k.op("tensor", lambda e, bank=bank: e.matmul(
            bank[0:2, :], cst["c_bd64"][:], sqq, start=True, stop=True), inc=(self.sT, 1))
        k.wait("scalar", self.sT)
        k.op("scalar", lambda e, bank=bank: e.activation(
            rr32, bank[0:2, :], AF.Sqrt, bias=cst["c_sc"][0:2, 1:2],
            scale=cst["c_sc"][0:2, 2:3]), inc=(self.sA, 1))
        k.wait("vector", self.sA)
        k.op("vector", lambda e: e.reciprocal(rr32, rr32), inc=(self.sV, 1))
        k.vw()
        k.op("vector", lambda e: e.tensor_copy(rrb, rr32), inc=(self.sV, 1))
        k.wait("tensor", self.sV)
        k.op("tensor", lambda e, bank=bank: e.matmul(
            bank[:, :], cst["c_ex64"][:], rrb, start=True, stop=True), inc=(self.sT, 1))
        k.wait("vector", self.sT)
        k.op("vector", lambda e, bank=bank: e.tensor_copy(rqb, bank[:, :]), inc=(self.sV, 1))
        k.vw()
        k.op("vector", lambda e: e.scalar_tensor_tensor(
            out_ap, pq_bank[:, :], gain_col, rqb, OP.mult, OP.mult), inc=(self.sV, 1))
        return self.sV.n

    def _wslab(self, wname, mc, wbf, sem, queue="sync", ncols=128):
        k = self.k
        scrt = self.scrw[wname][0] if wname in self.scrw else None
        assert scrt is not None
        kin = scrt.shape[0]
        nkc = kin // 128
        for kc in range(nkc):
            k.dma(queue, wbf[:, kc * ncols:(kc + 1) * ncols],
                  scrt.ap()[kc * 128:(kc + 1) * 128, mc * ncols:(mc + 1) * ncols], sem)
        return sem.n

    def phase_attn(self):
        k, cst, pb = self.k, self.cst, self.pb
        n1cols = [cst["n1w"][:, i:i + 1] for i in range(8)]
        sqt, sq_key = self.ar.t16(2 * L)
        sq = [sqt[:, 0:L], sqt[:, L:2 * L]]
        r1b, r1_key = self.ar.t32(L)
        sq_cons = [0, 0]
        for i in range(8):
            if i >= 2:
                k.wait("scalar", self.sT, sq_cons[i % 2])
            k.op("scalar", lambda e, i=i: e.activation(
                sq[i % 2][:], self.xnbf[i][:], AF.Square), inc=(self.sA, 1))
            k.wait("tensor", self.sA)
            for qc in range(4):
                k.op("tensor", lambda e, i=i, qc=qc: e.matmul(
                    pb[qc][:, :], cst["c_ones"][:], sq[i % 2][:, qc * 512:(qc + 1) * 512],
                    start=(i == 0), stop=(i == 7)), inc=(self.sT, 1))
            sq_cons[i % 2] = self.sT.n
        k.wait("scalar", self.sT)
        for qc in range(4):
            k.op("scalar", lambda e, qc=qc: e.activation(
                r1b[:, qc * 512:(qc + 1) * 512], pb[qc][:, :], AF.Sqrt,
                bias=cst["c_sc"][:, 1:2], scale=cst["c_sc"][:, 3:4]), inc=(self.sA, 1))
        k.wait("vector", self.sA)
        k.op("vector", lambda e: e.reciprocal(r1b[:, :], r1b[:, :]), inc=(self.sV, 1))
        k.wait("gpsimd", self.sV)
        k.dma("gpsimd", self.scr_row.ap()[0:1, :], r1b[0:1, :], self.sO)
        k.wait("sync", self.sO)
        r1tm, r1tm_key = self.ar.t32(16)
        k.dma("sync", r1tm[:, 0:16],
              bass.AP(self.scr_row, 0, [[1, 128], [128, 16]]), self.sB, slow=True)
        self.ar.release(sq_key)
        self.ar.release(r1_key)
        k.barrier()

        kh, kh_key = self.ar.t16(8 * L)
        wbf, wbf_key = self.ar.t16(2 * 1024)
        hrb32, hrb32_key = self.ar.t32(2 * 1024)
        hrb16, hrb16_key = self.ar.t16(2 * 1024)
        self.hrb32, self.hrb16 = hrb32, hrb16
        self.hr_bufs = []
        for b in range(2):
            o32 = b * 1024
            o16 = b * 1024
            sqq = hrb16[:, o16:o16 + 512]
            rr32 = hrb32[0:2, o32:o32 + 512]
            rrb = hrb16[0:2, o16 + 512:o16 + 1024]
            rqb = hrb32[:, o32 + 512:o32 + 1024]
            self.hr_bufs.append((sqq, rr32, rrb, rqb))

        wready = [0, 0]
        wcons = [0, 0]
        for mc in range(8):
            b = mc % 2
            k.wait("sync", self.sT, wcons[b])
            wready[b] = self._wslab("w_k", mc, wbf[:, b * 1024:(b + 1) * 1024],
                                    self.sWp[b])
            k.wait("tensor", self.sWp[b], wready[b])
            for qc in range(4):
                r = mc * 4 + qc
                bank = pb[r % 2]
                k.wait("tensor", self.sV)
                for kc in range(8):
                    k.op("tensor", lambda e, b=b, kc=kc, qc=qc, bank=bank: e.matmul(
                        bank[:, :], wbf[:, b * 1024 + kc * 128:b * 1024 + (kc + 1) * 128],
                        self.xnbf[kc][:, qc * 512:(qc + 1) * 512],
                        start=(kc == 0), stop=(kc == 7)), inc=(self.sT, 1))
                self._headrms(bank, cst["knw"][:, 0:1],
                              kh[:, mc * L + qc * 512:mc * L + (qc + 1) * 512], r)
            wcons[b] = self.sT.n
        k.barrier()

        vt, vt_key = self.ar.t16(16 * 1024)
        vwbf, vwbf_key = self.ar.t16(8 * 1024)
        for dc in range(8):
            k.dma("sync", vwbf[:, dc * 1024:(dc + 1) * 1024],
                  self.scrw["w_v"][0].ap()[dc * 128:(dc + 1) * 128, :], self.sB)
        k.wait("tensor", self.sB)
        k.wait("vector", self.sB)
        vevac = 0
        for kc in range(16):
            for hf in range(2):
                bank = pb[(kc * 2 + hf) % 2]
                k.wait("tensor", self.sV, vevac)
                for dc in range(8):
                    k.op("tensor", lambda e, kc=kc, hf=hf, dc=dc, bank=bank: e.matmul(
                        bank[:, :], self.xnbf[dc][:, kc * 128:(kc + 1) * 128],
                        vwbf[:, dc * 1024 + hf * 512:dc * 1024 + (hf + 1) * 512],
                        start=(dc == 0), stop=(dc == 7)), inc=(self.sT, 1))
                k.wait("vector", self.sT)
                k.op("vector", lambda e, kc=kc, hf=hf, bank=bank: e.tensor_scalar(
                    vt[:, kc * 1024 + hf * 512:kc * 1024 + (hf + 1) * 512],
                    bank[:, :], r1tm[:, kc:kc + 1], None, OP.mult), inc=(self.sV, 1))
                vevac = self.sV.n
        self.ar.release(vwbf_key)
        k.barrier()
        self.tap("bcap", self.bcap[:])
        self.tap("r1tm", r1tm[:, 0:16])
        self.tap("kh", [kh[:, i * L:(i + 1) * L] for i in range(8)])
        self.tap("vt", [vt[:, i * 1024:(i + 1) * 1024] for i in range(16)])

        pt, pt_key = self.ar.t16(2 * 16 * 512)
        qh, qh_key = self.ar.t16(2 * 512)
        avdone = 0
        for mc in range(8):
            b = mc % 2
            k.wait("sync", self.sT, wcons[b])
            wready[b] = self._wslab("w_q", mc, wbf[:, b * 1024:(b + 1) * 1024],
                                    self.sWp[b])
            k.wait("tensor", self.sWp[b], wready[b])
            for qc in range(4):
                r = mc * 4 + qc
                bank = pb[r % 2]
                k.wait("tensor", self.sV)
                k.wait("vector", self.sT)
                for kc in range(8):
                    k.op("tensor", lambda e, b=b, kc=kc, qc=qc, bank=bank: e.matmul(
                        bank[:, :], wbf[:, b * 1024 + kc * 128:b * 1024 + (kc + 1) * 128],
                        self.xnbf[kc][:, qc * 512:(qc + 1) * 512],
                        start=(kc == 0), stop=(kc == 7)), inc=(self.sT, 1))
                qv = self._headrms(bank, self.qnw8[:, 0:1], qh[:, b * 512:(b + 1) * 512], r)
                k.wait("tensor", self.sV, qv)
                o32 = (r % 2) * 1024
                o16 = (r % 2) * 1024
                nrow = self.hrb32[:, o32:o32 + 512]
                nrowb = self.hrb16[:, o16 + 512:o16 + 1024]
                nb32 = self.hrb32[:, o32 + 512:o32 + 1024]
                nkc = 4 * qc + 4
                for h2 in range(2):
                    h = 2 * mc + h2
                    for kc in range(nkc):
                        sbank = pb[4 + (kc % 2)]
                        k.wait("tensor", self.sA)
                        k.op("tensor", lambda e, h2=h2, kc=kc, mc=mc, b=b, sbank=sbank: e.matmul(
                            sbank[:, :],
                            kh[64 * h2:64 * h2 + 64,
                               mc * L + kc * 128:mc * L + (kc + 1) * 128],
                            qh[64 * h2:64 * h2 + 64, b * 512:(b + 1) * 512],
                            start=True, stop=True), inc=(self.sT, 1))
                        s = kc - 4 * qc
                        if 0 <= s <= 3:
                            k.wait("vector", self.sT)
                            k.op("vector", lambda e, s=s, sbank=sbank: e.tensor_tensor(
                                sbank[:, s * 128:(s + 1) * 128],
                                sbank[:, s * 128:(s + 1) * 128],
                                cst["c_maskT"][:], OP.add), inc=(self.sV, 1))
                            k.wait("scalar", self.sV)
                        k.wait("scalar", self.sT)
                        k.op("scalar", lambda e, h2=h2, kc=kc, sbank=sbank: e.activation(
                            pt[:, h2 * 8192 + kc * 512:h2 * 8192 + (kc + 1) * 512],
                            sbank[:, :], AF.Exp, bias=self.bcap[:, 0:1]), inc=(self.sA, 1))
                        for s2 in range(min(kc - 4 * qc, 4)):
                            k.wait("vector", self.sA)
                            k.op("vector", lambda e, h2=h2, kc=kc, s2=s2: e.memset(
                                pt[:, h2 * 8192 + kc * 512 + s2 * 128:
                                   h2 * 8192 + kc * 512 + (s2 + 1) * 128], 0.0),
                                 inc=(self.sV, 1))
                ptv = self.sA.n
                ptv2 = self.sV.n
                k.wait("tensor", self.sA, ptv)
                k.wait("tensor", self.sV, ptv2)
                for kc in range(nkc):
                    k.op("tensor", lambda e, mc=mc, kc=kc: e.matmul(
                        pb[6][0:64, :], vt[:, kc * 1024 + 128 * mc:kc * 1024 + 128 * mc + 64],
                        pt[:, 0 * 8192 + kc * 512:0 * 8192 + (kc + 1) * 512],
                        start=(kc == 0), stop=False), inc=(self.sT, 1))
                    k.op("tensor", lambda e, mc=mc, kc=kc, nkc=nkc: e.matmul(
                        pb[6][64:128, :], vt[:, kc * 1024 + 128 * mc + 64:kc * 1024 + 128 * mc + 128],
                        pt[:, 1 * 8192 + kc * 512:1 * 8192 + (kc + 1) * 512],
                        start=(kc == 0), stop=(kc == nkc - 1), tile_position=(0, 64)),
                        inc=(self.sT, 1))
                    k.op("tensor", lambda e, kc=kc: e.matmul(
                        pb[7][0:1, :], cst["c_ones"][:, 0:1],
                        pt[:, 0 * 8192 + kc * 512:0 * 8192 + (kc + 1) * 512],
                        start=(kc == 0), stop=False), inc=(self.sT, 1))
                    k.op("tensor", lambda e, kc=kc, nkc=nkc: e.matmul(
                        pb[7][64:65, :], cst["c_ones"][:, 0:1],
                        pt[:, 1 * 8192 + kc * 512:1 * 8192 + (kc + 1) * 512],
                        start=(kc == 0), stop=(kc == nkc - 1), tile_position=(0, 64)),
                        inc=(self.sT, 1))
                avmm = self.sT.n
                k.wait("vector", self.sT, avmm)
                if mc == 0 and qc == 0 and ("avraw00" in self.taps):
                    avr = qh[:, 512:1024]
                    smr = self.gbf[7][:, 0:512]
                    k.op("vector", lambda e, avr=avr: e.tensor_copy(avr, pb[6][:, :]),
                         inc=(self.sV, 1))
                    k.op("vector", lambda e, smr=smr: e.tensor_copy(smr[0:1, :], pb[7][0:1, :]),
                         inc=(self.sV, 1))
                    k.op("vector", lambda e, smr=smr: e.tensor_copy(smr[64:65, :], pb[7][64:65, :]),
                         inc=(self.sV, 1))
                    k.vw()
                    self.tap("avraw00", avr)
                    self.tap("sums00", smr)
                k.op("vector", lambda e, nrow=nrow: e.reciprocal(nrow[0:1, :], pb[7][0:1, :]),
                     inc=(self.sV, 1))
                k.op("vector", lambda e, nrow=nrow: e.reciprocal(nrow[64:65, :], pb[7][64:65, :]),
                     inc=(self.sV, 1))
                k.vw()
                k.op("vector", lambda e, nrow=nrow, nrowb=nrowb: e.tensor_copy(
                    nrowb[0:1, :], nrow[0:1, :]), inc=(self.sV, 1))
                k.op("vector", lambda e, nrow=nrow, nrowb=nrowb: e.tensor_copy(
                    nrowb[64:65, :], nrow[64:65, :]), inc=(self.sV, 1))
                k.wait("tensor", self.sV)
                k.op("tensor", lambda e, nrowb=nrowb: e.matmul(
                    pb[7][0:64, :], cst["c_ones"][0:1, 0:64], nrowb[0:1, :],
                    start=True, stop=False), inc=(self.sT, 1))
                k.op("tensor", lambda e, nrowb=nrowb: e.matmul(
                    pb[7][64:128, :], cst["c_ones"][64:65, 0:64], nrowb[64:65, :],
                    start=True, stop=True, tile_position=(64, 64)), inc=(self.sT, 1))
                k.wait("vector", self.sT)
                k.op("vector", lambda e, nb32=nb32: e.tensor_copy(nb32[:, :], pb[7][:, :]),
                     inc=(self.sV, 1))
                k.vw()
                k.op("vector", lambda e, mc=mc, qc=qc, nb32=nb32: e.tensor_tensor(
                    self.gbf[mc][:, qc * 512:(qc + 1) * 512], pb[6][:, :],
                    nb32[:, :], OP.mult), inc=(self.sV, 1))
                avdone = self.sV.n
                if mc == 0 and qc == 0:
                    self.tap("pt00", [pt[:, h2 * 8192:h2 * 8192 + 4 * 512]
                                      for h2 in range(2)])
                    self.tap("avb00", self.gbf[0][:, 0:512])
                    self.tap("qh00", qh[:, 0:512])
            wcons[b] = self.sT.n
        k.barrier()
        for key in (pt_key, qh_key, kh_key, vt_key,
                    r1tm_key):
            self.ar.release(key)

        xf32, self.xf32_key = self.ar.t32(8 * L)
        for i in range(8):
            k.dma("sync", xf32[:, i * L:(i + 1) * L],
                  self.scr_xn.ap()[i * 128:(i + 1) * 128, :], self.sB)
        xloaded = self.sB.n
        k.wait("vector", self.sB, xloaded)
        aocons = 0
        for mc in range(2, 8):
            b = mc % 2
            k.wait("sync", self.sT, wcons[b])
            wready[b] = self._wslab("w_o", mc, wbf[:, b * 1024:(b + 1) * 1024],
                                    self.sWp[b])
            k.wait("tensor", self.sWp[b], wready[b])
            for qc in range(4):
                bank = pb[(mc * 4 + qc) % 2]
                k.wait("tensor", self.sV, aocons)
                for kc in range(8):
                    k.op("tensor", lambda e, b=b, kc=kc, qc=qc, bank=bank: e.matmul(
                        bank[:, :], wbf[:, b * 1024 + kc * 128:b * 1024 + (kc + 1) * 128],
                        self.gbf[kc][:, qc * 512:(qc + 1) * 512],
                        start=(kc == 0), stop=(kc == 7)), inc=(self.sT, 1))
                k.wait("vector", self.sT)
                k.op("vector", lambda e, mc=mc, qc=qc, bank=bank: e.tensor_tensor(
                    xf32[:, mc * L + qc * 512:mc * L + (qc + 1) * 512],
                    xf32[:, mc * L + qc * 512:mc * L + (qc + 1) * 512],
                    bank[:, :], OP.add), inc=(self.sV, 1))
                aocons = self.sV.n
            wcons[b] = self.sT.n
        self.tap("xffn", [xf32[:, i * L:(i + 1) * L] for i in range(8)])
        k.barrier()
        self.ar.release(wbf_key)
        self.ar.release(hrb32_key)
        self.ar.release(hrb16_key)

        sqt2, sq2_key = self.ar.t16(2 * L)
        sq2 = [sqt2[:, 0:L], sqt2[:, L:2 * L]]
        self.r2b, self.r2b_key = self.ar.t32(L)
        sq_cons = [0, 0]
        for i in range(8):
            if i >= 2:
                k.wait("scalar", self.sT, sq_cons[i % 2])
            k.op("scalar", lambda e, i=i: e.activation(
                sq2[i % 2][:], xf32[:, i * L:(i + 1) * L], AF.Square), inc=(self.sA, 1))
            k.wait("tensor", self.sA)
            for qc in range(4):
                k.op("tensor", lambda e, i=i, qc=qc: e.matmul(
                    pb[qc][:, :], cst["c_ones"][:], sq2[i % 2][:, qc * 512:(qc + 1) * 512],
                    start=(i == 0), stop=(i == 7)), inc=(self.sT, 1))
            sq_cons[i % 2] = self.sT.n
        k.wait("scalar", self.sT)
        for qc in range(4):
            k.op("scalar", lambda e, qc=qc: e.activation(
                self.r2b[:, qc * 512:(qc + 1) * 512], pb[qc][:, :], AF.Sqrt,
                bias=cst["c_sc"][:, 1:2], scale=cst["c_sc"][:, 3:4]), inc=(self.sA, 1))
        k.wait("vector", self.sA)
        k.op("vector", lambda e: e.reciprocal(self.r2b[:, :], self.r2b[:, :]),
             inc=(self.sV, 1))
        k.wait("gpsimd", self.sV)
        for i in range(8):
            k.dma("gpsimd", self.scr_xffn.ap()[i * 128:(i + 1) * 128, :],
                  xf32[:, i * L:(i + 1) * L], self.sO)
        k.barrier()
        self.ar.release(sq2_key)
        self.ar.release(self.xf32_key)

    def phase_moe(self):
        k, cst, pb = self.k, self.cst, self.pb
        n2cols = [cst["n2w"][:, i:i + 1] for i in range(8)]
        xfh, xfh_key = self.ar.t16(8 * L)
        xt32, xt_key = self.ar.t32(2 * 512)
        for i in range(8):
            for qc in range(4):
                b = (i * 4 + qc) % 2
                k.wait("sync", self.sV)
                ld = k.dma("sync", xt32[:, b * 512:(b + 1) * 512],
                           self.scr_xffn.ap()[i * 128:(i + 1) * 128, qc * 512:(qc + 1) * 512],
                           self.sWp[b])
                k.wait("vector", self.sWp[b], ld)
                k.op("vector", lambda e, i=i, qc=qc, b=b: e.tensor_tensor(
                    xfh[:, i * L + qc * 512:i * L + (qc + 1) * 512],
                    xt32[:, b * 512:(b + 1) * 512],
                    self.r2b[:, qc * 512:(qc + 1) * 512], OP.mult), inc=(self.sV, 1))
        self.tap("xfh", [xfh[:, i * L:(i + 1) * L] for i in range(8)])
        k.barrier()
        self.ar.release(xt_key)
        self.ar.release(self.r2b_key)

        wgbf, wgbf_key = self.ar.t16(32)
        wg32, wg32_key = self.ar.t32(32)
        for kc in range(8):
            k.dma("sync", wg32[:, kc * 4:(kc + 1) * 4],
                  self.dw["w_gate"].ap()[kc * 128:(kc + 1) * 128, :], self.sB)
        k.wait("vector", self.sB)
        for kc in range(8):
            k.op("vector", lambda e, kc=kc: e.tensor_scalar(
                wgbf[:, kc * 4:(kc + 1) * 4], wg32[:, kc * 4:(kc + 1) * 4],
                n2cols[kc], None, OP.mult), inc=(self.sV, 1))
        k.wait("tensor", self.sV)
        self.tap("wgbf", wgbf[:, 0:32])
        lg32 = self.lgt[:]
        gt, gt_key = self.ar.t32(16 + 16 + 16 + 16 + 4 + 16)
        wfm, wfm_key = self.ar.t16(L)
        k.wait("tensor", self.sV)
        for qc in range(4):
            for kc in range(8):
                k.op("tensor", lambda e, kc=kc, qc=qc: e.matmul(
                    pb[2][0:4, :], wgbf[:, kc * 4:(kc + 1) * 4],
                    xfh[:, kc * L + qc * 512:kc * L + (qc + 1) * 512],
                    start=(kc == 0), stop=(kc == 7)), inc=(self.sT, 1))
            k.wait("vector", self.sT)
            k.op("vector", lambda e: e.tensor_copy(lg32[0:4, :], pb[2][0:4, :]),
                 inc=(self.sV, 1))
            k.wait("tensor", self.sV)
            for sub in range(4):
                k.op("tensor", lambda e, sub=sub: e.transpose(
                    pb[3][0:128, sub * 4:(sub + 1) * 4], lg32[0:4, sub * 128:(sub + 1) * 128],
                    cst["c_id4"][:]), inc=(self.sT, 1))
            k.wait("vector", self.sT)
            k.vw()
            k.op("vector", lambda e: e.tensor_copy(gt[:, 0:16], pb[3][:, 0:16]),
                 inc=(self.sV, 1))
            if qc == 0:
                k.vw()
                self.tap("lgfm", lg32[0:4, :])
                k.op("vector", lambda e: e.tensor_copy(wfm[:, 0:16], gt[:, 0:16]),
                     inc=(self.sV, 1))
                k.vw()
                self.tap("gtlg", wfm[:, 0:16])
            lgv = gt[:, 0:16].rearrange("p (s e) -> p s e", e=4)
            k.vw()
            k.op("vector", lambda e, lgv=lgv: e.tensor_reduce(
                gt[:, 64:68], lgv, AX.X, OP.max), inc=(self.sV, 1))
            k.vw()
            k.op("vector", lambda e: e.tensor_scalar(
                gt[:, 64:68], gt[:, 64:68], -1.0, None, OP.mult), inc=(self.sV, 1))
            k.wait("scalar", self.sV)
            for sub in range(4):
                k.op("scalar", lambda e, sub=sub: e.activation(
                    gt[:, 68 + sub * 4:68 + (sub + 1) * 4], gt[:, sub * 4:(sub + 1) * 4],
                    AF.Exp, bias=gt[:, 64 + sub:65 + sub]), inc=(self.sA, 1))
            k.wait("vector", self.sA)
            ev = gt[:, 68:84].rearrange("p (s e) -> p s e", e=4)
            k.vw()
            k.op("vector", lambda e, ev=ev: e.tensor_reduce(
                gt[:, 64:68], ev, AX.X, OP.add), inc=(self.sV, 1))
            k.vw()
            k.op("vector", lambda e: e.reciprocal(gt[:, 64:68], gt[:, 64:68]),
                 inc=(self.sV, 1))
            for sub in range(4):
                k.vw()
                k.op("vector", lambda e, sub=sub: e.tensor_scalar(
                    gt[:, 16 + sub * 4:16 + (sub + 1) * 4],
                    gt[:, 68 + sub * 4:68 + (sub + 1) * 4],
                    gt[:, 64 + sub:65 + sub], None, OP.mult), inc=(self.sV, 1))
            k.vw()
            k.op("vector", lambda e: e.memset(gt[:, 32:48], 0.0), inc=(self.sV, 1))
            for a in range(4):
                for e2 in range(4):
                    if a == e2:
                        continue
                    k.vw()
                    k.op("vector", lambda e, a=a, e2=e2: e.tensor_tensor(
                        gt[:, 48 + e2:64:4][:, 0:4],
                        gt[:, 16 + a:32:4][:, 0:4],
                        gt[:, 16 + e2:32:4][:, 0:4], OP.is_gt), inc=(self.sV, 1))
                    k.vw()
                    k.op("vector", lambda e, e2=e2: e.tensor_tensor(
                        gt[:, 32 + e2:48:4][:, 0:4],
                        gt[:, 32 + e2:48:4][:, 0:4],
                        gt[:, 48 + e2:64:4][:, 0:4], OP.add), inc=(self.sV, 1))
            k.vw()
            k.op("vector", lambda e: e.tensor_scalar(
                gt[:, 32:48], gt[:, 32:48], 1.0, None, OP.is_le), inc=(self.sV, 1))
            k.vw()
            k.op("vector", lambda e: e.tensor_tensor(
                gt[:, 48:64], gt[:, 16:32], gt[:, 32:48], OP.mult), inc=(self.sV, 1))
            wv2 = gt[:, 48:64].rearrange("p (s e) -> p s e", e=4)
            k.vw()
            k.op("vector", lambda e, wv2=wv2: e.tensor_reduce(
                gt[:, 64:68], wv2, AX.X, OP.add), inc=(self.sV, 1))
            k.vw()
            k.op("vector", lambda e: e.tensor_scalar(
                gt[:, 64:68], gt[:, 64:68], 1e-8, None, OP.add), inc=(self.sV, 1))
            k.vw()
            k.op("vector", lambda e: e.reciprocal(gt[:, 64:68], gt[:, 64:68]),
                 inc=(self.sV, 1))
            for sub in range(4):
                k.vw()
                k.op("vector", lambda e, sub=sub: e.tensor_scalar(
                    gt[:, 48 + sub * 4:48 + (sub + 1) * 4],
                    gt[:, 48 + sub * 4:48 + (sub + 1) * 4],
                    gt[:, 64 + sub:65 + sub], None, OP.mult), inc=(self.sV, 1))
            wtb, wtb_key = self.ar.t16(16)
            k.vw()
            k.op("vector", lambda e, wtb=wtb: e.tensor_copy(wtb[:, 0:16], gt[:, 48:64]),
                 inc=(self.sV, 1))
            k.wait("tensor", self.sV)
            pb3bf = pb[3][:].bitcast(BF16)
            for sub in range(4):
                k.op("tensor", lambda e, sub=sub, wtb=wtb, pb3bf=pb3bf: e.transpose(
                    pb3bf[0:4, sub * 128:(sub + 1) * 128],
                    wtb[:, sub * 4:(sub + 1) * 4], cst["c_idbf"][:]), inc=(self.sT, 1))
            k.wait("vector", self.sT)
            k.vw()
            k.op("vector", lambda e, qc=qc, pb3bf=pb3bf: e.tensor_copy(
                wfm[0:4, qc * 512:(qc + 1) * 512], pb3bf[0:4, 0:512]), inc=(self.sV, 1))
            self.ar.release(wtb_key)
        k.wait("gpsimd", self.sV)
        for e2 in range(4):
            k.dma("gpsimd", self.scr_gw.ap()[e2:e2 + 1, :], wfm[e2:e2 + 1, :], self.sO)
        k.wait("sync", self.sO)
        self.tap("gatew", wfm[0:4, :])
        k.barrier()
        for key in (wgbf_key, wg32_key, gt_key, wfm_key):
            self.ar.release(key)

        mid0, mid0_key = self.ar.t16(16 * 1024)
        mid1h, mid1h_key = self.ar.t16(8 * 1024)
        macc, macc_key = self.ar.t32(6 * 1024)
        wbc, wbc_key = self.ar.t16(1024)
        t32b, t32b_key = self.ar.t32(2 * 512)
        wbf2, wbf2_key = self.ar.t16(2 * 2048)
        slab_cons = [0, 0]
        slab_ready = [0, 0]

        for tch in range(2):
            jid = 0

            def load_slab(wname, mc, ncols=128):
                nonlocal jid
                b = jid % 2
                jid += 1
                k.wait("sync", self.sT, slab_cons[b])
                slab_ready[b] = self._wslab(wname, mc, wbf2[:, b * 2048:(b + 1) * 2048],
                                            self.sWp[b], ncols=ncols)
                k.wait("tensor", self.sWp[b], slab_ready[b])
                return b

            def l1(wname, outbuf, act, ncin=8, src=None, obase=1024):
                nonlocal jid
                for mc in range(dict(WSPEC_F32)[wname][1] // 128):
                    b = load_slab(wname, mc)
                    for q2 in range(2):
                        bank = pb[(jid * 2 + q2) % 2]
                        k.wait("tensor", self.sA)
                        for kc in range(ncin):
                            if src is None:
                                rhs = xfh[:, kc * L + tch * 1024 + q2 * 512:
                                          kc * L + tch * 1024 + (q2 + 1) * 512]
                            else:
                                rhs = src[:, kc * 1024 + q2 * 512:kc * 1024 + (q2 + 1) * 512]
                            k.op("tensor", lambda e, b=b, kc=kc, rhs=rhs, bank=bank: e.matmul(
                                bank[:, :], wbf2[:, b * 2048 + kc * 128:b * 2048 + (kc + 1) * 128],
                                rhs, start=(kc == 0), stop=(kc == ncin - 1)), inc=(self.sT, 1))
                        k.wait("scalar", self.sT)
                        k.op("scalar", lambda e, mc=mc, q2=q2, act=act, bank=bank,
                             outbuf=outbuf, obase=obase: e.activation(
                                 outbuf[:, mc * obase + q2 * 512:mc * obase + (q2 + 1) * 512],
                                 bank[:, :], act), inc=(self.sA, 1))
                    slab_cons[b] = self.sT.n

            def dn(wname, e2, srcbuf, ncin=16):
                nonlocal jid
                k.wait("sync", self.sV)
                wb_ld = k.dma("sync", wbc[:, 0:1024],
                              bass.AP(self.scr_gw, e2 * L + tch * 1024,
                                      [[0, 128], [1, 1024]]), self.sB)
                k.wait("vector", self.sB, wb_ld)
                for mc in range(2, 8):
                    b = load_slab(wname, mc)
                    for q2 in range(2):
                        bank = pb[(jid * 2 + q2) % 2]
                        k.wait("tensor", self.sV)
                        for kc in range(ncin):
                            k.op("tensor", lambda e, b=b, kc=kc, q2=q2, srcbuf=srcbuf,
                                 bank=bank: e.matmul(
                                     bank[:, :],
                                     wbf2[:, b * 2048 + kc * 128:b * 2048 + (kc + 1) * 128],
                                     srcbuf[:, kc * 1024 + q2 * 512:kc * 1024 + (q2 + 1) * 512],
                                     start=(kc == 0), stop=(kc == ncin - 1)), inc=(self.sT, 1))
                        k.wait("vector", self.sT)
                        bt = q2 % 2
                        if e2 == 0:
                            k.op("vector", lambda e, mc=mc, q2=q2, bank=bank: e.tensor_tensor(
                                macc[:, (mc - 2) * 1024 + q2 * 512:(mc - 2) * 1024 + (q2 + 1) * 512],
                                bank[:, :], wbc[:, q2 * 512:(q2 + 1) * 512],
                                OP.mult), inc=(self.sV, 1))
                        else:
                            k.op("vector", lambda e, q2=q2, bt=bt, bank=bank: e.tensor_tensor(
                                t32b[:, bt * 512:(bt + 1) * 512], bank[:, :],
                                wbc[:, q2 * 512:(q2 + 1) * 512], OP.mult), inc=(self.sV, 1))
                            k.vw()
                            k.op("vector", lambda e, mc=mc, q2=q2, bt=bt: e.tensor_tensor(
                                macc[:, (mc - 2) * 1024 + q2 * 512:(mc - 2) * 1024 + (q2 + 1) * 512],
                                macc[:, (mc - 2) * 1024 + q2 * 512:(mc - 2) * 1024 + (q2 + 1) * 512],
                                t32b[:, bt * 512:(bt + 1) * 512], OP.add), inc=(self.sV, 1))
                    slab_cons[b] = self.sT.n

            l1("w_e0up", mid0, AF.Gelu)
            if tch == 0:
                self.tap("mide0", [mid0[:, i * 1024:(i + 1) * 1024] for i in range(16)])
            dn("w_e0dn", 0, mid0)
            if tch == 0:
                self.tap("macc_e0", [macc[:, i * 1024:(i + 1) * 1024] for i in range(6)])
            for mc in range(16):
                ba = load_slab("w_e1a", mc)
                bb = load_slab("w_e1b", mc)
                if tch == 0 and mc == 0:
                    self.k.wait("gpsimd", self.sW)
                    self.tap("we1a0", wbf2[:, ba * 2048:(ba + 1) * 2048])
                    self.tap("we1b0", wbf2[:, bb * 2048:(bb + 1) * 2048])
                for q2 in range(2):
                    bka, bkb = pb[4], pb[5]
                    k.wait("tensor", self.sA)
                    k.wait("tensor", self.sV)
                    for kc in range(8):
                        k.op("tensor", lambda e, ba=ba, kc=kc, q2=q2, bka=bka, tch=tch: e.matmul(
                            bka[:, :], wbf2[:, ba * 2048 + kc * 128:ba * 2048 + (kc + 1) * 128],
                            xfh[:, kc * L + tch * 1024 + q2 * 512:kc * L + tch * 1024 + (q2 + 1) * 512],
                            start=(kc == 0), stop=(kc == 7)), inc=(self.sT, 1))
                    for kc in range(8):
                        k.op("tensor", lambda e, bb=bb, kc=kc, q2=q2, bkb=bkb, tch=tch: e.matmul(
                            bkb[:, :], wbf2[:, bb * 2048 + kc * 128:bb * 2048 + (kc + 1) * 128],
                            xfh[:, kc * L + tch * 1024 + q2 * 512:kc * L + tch * 1024 + (q2 + 1) * 512],
                            start=(kc == 0), stop=(kc == 7)), inc=(self.sT, 1))
                    k.wait("scalar", self.sT)
                    k.wait("scalar", self.sV)
                    if tch == 0 and mc == 0 and q2 == 0 and "pe1raw" in self.taps:
                        k.wait("vector", self.sT)
                        k.op("vector", lambda e, bka=bka: e.tensor_copy(
                            mid1h[:, 0:512], bka[:, :]), inc=(self.sV, 1))
                        k.op("vector", lambda e, bkb=bkb: e.tensor_copy(
                            mid1h[:, 512:1024], bkb[:, :]), inc=(self.sV, 1))
                        k.vw()
                        self.tap("pe1raw", mid1h[:, 0:1024])
                    bt = q2 % 2
                    k.op("scalar", lambda e, bt=bt, bka=bka: e.activation(
                        t32b[:, bt * 512:(bt + 1) * 512], bka[:, :], AF.Silu),
                        inc=(self.sA, 1))
                    k.wait("vector", self.sA)
                    k.op("vector", lambda e, mc=mc, q2=q2, bt=bt, bkb=bkb: e.tensor_tensor(
                        mid0[:, mc * 1024 + q2 * 512:mc * 1024 + (q2 + 1) * 512],
                        t32b[:, bt * 512:(bt + 1) * 512], bkb[:, :], OP.mult),
                        inc=(self.sV, 1))
                slab_cons[ba] = self.sT.n
                slab_cons[bb] = self.sT.n
            if tch == 0:
                self.tap("mide1", [mid0[:, i * 1024:(i + 1) * 1024] for i in range(16)])
            dn("w_e1dn", 1, mid0)
            if tch == 0:
                self.tap("macc_e1", [macc[:, i * 1024:(i + 1) * 1024] for i in range(6)])
            l1("w_e2a", mid0, AF.Gelu)
            for q2h in range(2):
                for mc in range(16):
                    b = load_slab("w_e2b", mc)
                    bank = pb[jid % 2]
                    k.wait("tensor", self.sA)
                    for kc in range(16):
                        k.op("tensor", lambda e, b=b, kc=kc, q2h=q2h, bank=bank: e.matmul(
                            bank[:, :], wbf2[:, b * 2048 + kc * 128:b * 2048 + (kc + 1) * 128],
                            mid0[:, kc * 1024 + q2h * 512:kc * 1024 + (q2h + 1) * 512],
                            start=(kc == 0), stop=(kc == 15)), inc=(self.sT, 1))
                    k.wait("scalar", self.sT)
                    k.op("scalar", lambda e, mc=mc, bank=bank: e.activation(
                        mid1h[:, mc * 512:(mc + 1) * 512], bank[:, :], AF.Gelu),
                        inc=(self.sA, 1))
                    slab_cons[b] = self.sT.n
                for mc in range(16):
                    b = load_slab("w_e2c", mc)
                    bank = pb[jid % 2]
                    k.wait("tensor", self.sA)
                    for kc in range(16):
                        k.op("tensor", lambda e, b=b, kc=kc, bank=bank: e.matmul(
                            bank[:, :], wbf2[:, b * 2048 + kc * 128:b * 2048 + (kc + 1) * 128],
                            mid1h[:, kc * 512:(kc + 1) * 512],
                            start=(kc == 0), stop=(kc == 15)), inc=(self.sT, 1))
                    k.wait("scalar", self.sT)
                    k.op("scalar", lambda e, mc=mc, q2h=q2h, bank=bank: e.activation(
                        mid0[:, mc * 1024 + q2h * 512:mc * 1024 + (q2h + 1) * 512],
                        bank[:, :], AF.Gelu), inc=(self.sA, 1))
                    slab_cons[b] = self.sT.n
            dn("w_e2d", 2, mid0)
            if tch == 0:
                self.tap("macc_e2", [macc[:, i * 1024:(i + 1) * 1024] for i in range(6)])
            l1("w_e3a", mid1h, AF.Gelu, obase=1024)
            l1("w_e3b", mid0, AF.Gelu, ncin=8, src=mid1h)
            dn("w_e3c", 3, mid0)
            if tch == 0:
                self.tap("macc_all", [macc[:, i * 1024:(i + 1) * 1024] for i in range(6)])
            if tch == 1:
                self.tap("macc_t1", [macc[:, i * 1024:(i + 1) * 1024] for i in range(6)])
            for mc in range(8):
                for q2 in range(2):
                    b = (mc * 2 + q2) % 2
                    k.wait("sync", self.sA)
                    k.wait("sync", self.sV)
                    ld = k.dma("sync", t32b[:, b * 512:(b + 1) * 512],
                               self.scr_xffn.ap()[mc * 128:(mc + 1) * 128,
                                                  tch * 1024 + q2 * 512:tch * 1024 + (q2 + 1) * 512],
                               self.sB)
                    if mc >= 2:
                        k.wait("vector", self.sB, ld)
                        k.wait("vector", self.sT)
                        k.op("vector", lambda e, mc=mc, q2=q2, b=b: e.tensor_tensor(
                            t32b[:, b * 512:(b + 1) * 512], t32b[:, b * 512:(b + 1) * 512],
                            macc[:, (mc - 2) * 1024 + q2 * 512:(mc - 2) * 1024 + (q2 + 1) * 512],
                            OP.add), inc=(self.sV, 1))
                        k.wait("scalar", self.sV)
                    else:
                        k.wait("scalar", self.sB, ld)
                    k.op("scalar", lambda e, mc=mc, tch=tch, q2=q2, b=b: e.activation(
                        self.gbf[mc][:, tch * 1024 + q2 * 512:tch * 1024 + (q2 + 1) * 512],
                        t32b[:, b * 512:(b + 1) * 512], AF.Silu), inc=(self.sA, 1))
        self.tap("g", [self.gbf[i][:] for i in range(8)])
        k.barrier()
        for key in (xfh_key, mid0_key, mid1h_key, macc_key, wbc_key, t32b_key,
                    wbf2_key):
            self.ar.release(key)

    def phase_mamba(self, d):
        k, cst, pb = self.k, self.cst, self.pb
        fwd = (d == "f")
        ext, ext_key = self.ar.t32(16 * (TC + 3))
        ubf, ubf_key = self.ar.t16(16 * TC)
        Lt, Lt_key = self.ar.t32(16 * TC)
        dtu, dtu_key = self.ar.t16(16 * TC)
        sgb, sgb_key = self.ar.t16(16 * TC)
        hbuf, hbuf_key = self.ar.t32(16 * TC)
        yg, yg_key = self.ar.t16(16 * TC)
        yt, yt_key = self.ar.t32(2 * TC)
        ab, ab_key = self.ar.t32(2 * TC)
        bb16, bb16_key = self.ar.t16(2 * TC)
        stk, stk_key = self.ar.t16(2 * 16 * TC)
        dtv, dtv_key = self.ar.t16(2 * TC)
        bc16, bc16_key = self.ar.t16(TC)
        wsl, wsl_key = self.ar.t16(2 * 2048)
        sfb, sfb_key = self.ar.t16(TC)
        xre, xre_key = self.ar.t32(TC)
        sfr, sfr_key = self.ar.t16(TC)
        state = self.state
        k.op("vector", lambda e: e.memset(state[:], 0.0), inc=(self.sV, 1))
        for dc in range(16):
            k.op("vector", lambda e, dc=dc: e.memset(
                ext[:, dc * (TC + 3):dc * (TC + 3) + (TC + 3)], 0.0), inc=(self.sV, 1))
        k.barrier(("vector", "tensor", "scalar", "sync", "gpsimd"))

        convw = cst["convw_" + d]
        convb = cst["convb_" + d]
        aexp = self.aexp[d]
        ndtb = self.ndtb[d]
        dvec = cst["dvec_" + d]
        xp = self.xpbf[d]
        dtp = self.dtpbf[d]
        chunks = list(range(NCH)) if fwd else list(range(NCH - 1, -1, -1))
        slab_cons = [0, 0]
        stk_cons = 0

        for ci, c in enumerate(chunks):
            tsl = slice(c * TC, (c + 1) * TC)
            if ci > 0:
                for dc in range(16):
                    eb = ext[:, dc * (TC + 3):(dc + 1) * (TC + 3)]
                    if fwd:
                        k.op("vector", lambda e, eb=eb: e.tensor_copy(
                            eb[:, 0:3], eb[:, TC:TC + 3]), inc=(self.sV, 1))
                    else:
                        k.op("vector", lambda e, eb=eb: e.tensor_copy(
                            eb[:, TC:TC + 3], eb[:, 0:3]), inc=(self.sV, 1))
            hist_saved = self.sV.n
            jid = 0
            evac_done = 0
            for mcp in range(8):
                b = mcp % 2
                k.wait("gpsimd", self.sT, slab_cons[b])
                for kc in range(8):
                    k.dma("gpsimd", wsl[:, b * 2048 + kc * 256:b * 2048 + (kc + 1) * 256],
                          self.scr_wip.ap()[kc * 128:(kc + 1) * 128,
                                            mcp * 256:(mcp + 1) * 256], self.sXp[b])
                wld = self.sXp[b].n
                k.wait("tensor", self.sXp[b], wld)
                for m2 in range(2):
                    mc = mcp * 2 + m2
                    bank = pb[jid % 2]
                    k.wait("tensor", self.sV, evac_done if jid >= 2 else 0)
                    if jid < 2:
                        k.wait("tensor", self.sV, hist_saved)
                        k.wait("tensor", self.sA)
                    for kc in range(8):
                        k.op("tensor", lambda e, b=b, kc=kc, m2=m2, bank=bank, tsl=tsl: e.matmul(
                            bank[:, 0:TC], wsl[:, b * 2048 + kc * 256 + m2 * 128:
                                               b * 2048 + kc * 256 + (m2 + 1) * 128],
                            self.xnbf[kc][:, tsl], start=(kc == 0), stop=(kc == 7)),
                            inc=(self.sT, 1))
                    k.wait("vector", self.sT)
                    k.op("vector", lambda e, mc=mc, bank=bank: e.tensor_copy(
                        ext[:, mc * (TC + 3) + (3 if fwd else 0):
                            mc * (TC + 3) + (3 if fwd else 0) + TC], bank[:, 0:TC]),
                        inc=(self.sV, 1))
                    evac_done = self.sV.n
                    jid += 1
                slab_cons[b] = self.sT.n
            k.wait("vector", self.sV, evac_done)
            k.wait("vector", self.sO)
            conv_done = []
            for dc in range(16):
                eb = ext[:, dc * (TC + 3):(dc + 1) * (TC + 3)]
                tt = yt[:, (dc % 2) * TC:(dc % 2) * TC + TC]
                if len(conv_done) >= 2:
                    k.wait("vector", self.sA, conv_done[-2])
                wcol = lambda kk2, dc=dc: convw[:, dc * 4 + kk2:dc * 4 + kk2 + 1]
                offs = (lambda kk2: kk2) if fwd else (lambda kk2: 3 - kk2)
                k.op("vector", lambda e, eb=eb, tt=tt, wcol=wcol, offs=offs: e.tensor_scalar(
                    tt, eb[:, offs(0):offs(0) + TC], wcol(0), None, OP.mult), inc=(self.sV, 1))
                for kk2 in range(1, 4):
                    k.vw()
                    k.op("vector", lambda e, eb=eb, tt=tt, wcol=wcol, offs=offs, kk2=kk2:
                         e.scalar_tensor_tensor(tt, eb[:, offs(kk2):offs(kk2) + TC],
                                                wcol(kk2), tt, OP.mult, OP.add),
                         inc=(self.sV, 1))
                k.wait("scalar", self.sV)
                k.op("scalar", lambda e, dc=dc, tt=tt: e.activation(
                    ubf[:, dc * TC:(dc + 1) * TC], tt, AF.Silu,
                    bias=convb[:, dc:dc + 1]), inc=(self.sA, 1))
                conv_done.append(self.sA.n)
            usil = self.sA.n
            for mcp in range(8, 16):
                b = mcp % 2
                k.wait("gpsimd", self.sT, slab_cons[b])
                for kc in range(8):
                    k.dma("gpsimd", wsl[:, b * 2048 + kc * 256:b * 2048 + (kc + 1) * 256],
                          self.scr_wip.ap()[kc * 128:(kc + 1) * 128,
                                            mcp * 256:(mcp + 1) * 256], self.sXp[b])
                wld = self.sXp[b].n
                k.wait("tensor", self.sXp[b], wld)
                for m2 in range(2):
                    mc = (mcp - 8) * 2 + m2
                    bank = pb[jid % 2]
                    k.wait("tensor", self.sA)
                    k.wait("tensor", self.sV, evac_done)
                    for kc in range(8):
                        k.op("tensor", lambda e, b=b, kc=kc, m2=m2, bank=bank, tsl=tsl: e.matmul(
                            bank[:, 0:TC], wsl[:, b * 2048 + kc * 256 + m2 * 128:
                                               b * 2048 + kc * 256 + (m2 + 1) * 128],
                            self.xnbf[kc][:, tsl], start=(kc == 0), stop=(kc == 7)),
                            inc=(self.sT, 1))
                    k.wait("scalar", self.sT)
                    k.op("scalar", lambda e, mc=mc, bank=bank: e.activation(
                        sgb[:, mc * TC:(mc + 1) * TC], bank[:, 0:TC], AF.Silu),
                        inc=(self.sA, 1))
                    jid += 1
                slab_cons[b] = self.sT.n
            k.wait("tensor", self.sA, usil)
            for dc in range(16):
                k.op("tensor", lambda e, dc=dc: e.matmul(
                    pb[4][0:96, 0:TC], xp[:, dc * 96:(dc + 1) * 96],
                    ubf[:, dc * TC:(dc + 1) * TC], start=(dc == 0), stop=(dc == 15)),
                    inc=(self.sT, 1))
            xdblv = self.sT.n
            k.wait("vector", self.sT, xdblv)
            k.op("vector", lambda e: e.tensor_copy(dtv[0:DTR, 0:TC], pb[4][0:DTR, 0:TC]),
                 inc=(self.sV, 1))
            k.wait("scalar", self.sT, xdblv)
            k.op("scalar", lambda e: e.activation(
                bc16[DTR:96, 0:TC], pb[4][DTR:96, 0:TC], AF.Copy,
                scale=cst["c_negmix"][DTR:96, 0:1]), inc=(self.sA, 1))
            bcv = self.sA.n
            bcb = ci % 2
            k.wait("scalar", self.sA)
            sp = k.dma("scalar", self.scr_bc.ap()[bcb:bcb + 1, :].rearrange(
                "a (r t) -> (a r) t", t=TC), bc16[DTR:96, 0:TC], self.sD)
            k.wait("gpsimd", self.sD, sp)
            k.wait("gpsimd", self.sV, stk_cons)
            k.dma("gpsimd", stk[:, 0:16 * TC],
                  bass.AP(self.scr_bc, bcb * 32 * TC, [[0, 128], [1, 16 * TC]]), self.sC)
            k.dma("gpsimd", stk[:, 16 * TC:32 * TC],
                  bass.AP(self.scr_bc, bcb * 32 * TC + 16 * TC, [[0, 128], [1, 16 * TC]]),
                  self.sC)
            stks = self.sC.n
            if ci == 0 and fwd and "stk0" in self.taps:
                self.k.wait("gpsimd", self.sX)
                self.tap("stk0", stk[:, 0:32 * TC])
            dtmm = []
            k.wait("tensor", self.sV)
            k.wait("tensor", self.sA, bcv)
            for mc in range(16):
                bank = pb[4 + mc % 2]
                if mc >= 2:
                    k.wait("tensor", self.sA, dtmm[-2])
                k.op("tensor", lambda e, mc=mc, bank=bank: e.matmul(
                    bank[:, 0:TC], dtp[:, mc * 128:(mc + 1) * 128], dtv[0:DTR, 0:TC],
                    start=True, stop=True), inc=(self.sT, 1))
                k.wait("scalar", self.sT)
                k.op("scalar", lambda e, mc=mc, bank=bank: e.activation(
                    Lt[:, mc * TC:(mc + 1) * TC], bank[:, 0:TC], AF.Sigmoid,
                    bias=ndtb[:, mc:mc + 1], scale=cst["c_sc"][:, 4:5]), inc=(self.sA, 1))
                dtmm.append(self.sA.n)
            for mc in range(16):
                k.op("scalar", lambda e, mc=mc: e.activation(
                    Lt[:, mc * TC:(mc + 1) * TC], Lt[:, mc * TC:(mc + 1) * TC],
                    AF.Ln), inc=(self.sA, 1))
            lnv = self.sA.n
            k.wait("vector", self.sA, lnv)
            for dc in range(16):
                k.op("vector", lambda e, dc=dc: e.tensor_tensor(
                    dtu[:, dc * TC:(dc + 1) * TC], Lt[:, dc * TC:(dc + 1) * TC],
                    ubf[:, dc * TC:(dc + 1) * TC], OP.mult), inc=(self.sV, 1))
            dtuv = self.sV.n
            k.wait("vector", self.sC, stks)
            k.wait("scalar", self.sA, lnv)
            scan_done = []

            def bform(e, dc, n, b_t):
                return e.tensor_tensor(b_t, dtu[:, dc * TC:(dc + 1) * TC],
                                       stk[:, n * TC:(n + 1) * TC], OP.mult)
            for dc in range(16):
                k.op("vector", lambda e, dc=dc: bform(e, dc, 0, bb16[:, 0:TC]),
                     inc=(self.sV, 1))
                for n in range(NS):
                    rr = n % 2
                    a_t = ab[:, rr * TC:(rr + 1) * TC]
                    b_t = bb16[:, rr * TC:(rr + 1) * TC]
                    if len(scan_done) >= 2:
                        k.wait("scalar", self.sV, scan_done[-2])
                    k.op("scalar", lambda e, dc=dc, n=n, a_t=a_t: e.activation(
                        a_t, Lt[:, dc * TC:(dc + 1) * TC], AF.Exp,
                        scale=aexp[:, dc * 16 + n:dc * 16 + n + 1]), inc=(self.sA, 1))
                    if n < NS - 1:
                        nrr = (n + 1) % 2
                        k.op("vector", lambda e, dc=dc, n=n, nrr=nrr: bform(
                            e, dc, n + 1, bb16[:, nrr * TC:(nrr + 1) * TC]),
                            inc=(self.sV, 1))
                    k.wait("vector", self.sA)
                    hsl = hbuf[:, n * TC:(n + 1) * TC]
                    if fwd:
                        k.op("vector", lambda e, hsl=hsl, a_t=a_t, b_t=b_t, dc=dc, n=n:
                             e.tensor_tensor_scan(hsl, a_t, b_t,
                                                  state[:, dc * 16 + n:dc * 16 + n + 1],
                                                  OP.mult, OP.add), inc=(self.sV, 1))
                    else:
                        k.op("vector", lambda e, hsl=hsl, a_t=a_t, b_t=b_t, dc=dc, n=n:
                             e.tensor_tensor_scan(hsl[:, ::-1], a_t[:, ::-1], b_t[:, ::-1],
                                                  state[:, dc * 16 + n:dc * 16 + n + 1],
                                                  OP.mult, OP.add), inc=(self.sV, 1))
                    scan_done.append(self.sV.n)
                lastcol = TC - 1 if fwd else 0
                k.vw()
                k.op("vector", lambda e, dc=dc, lastcol=lastcol: e.tensor_copy(
                    state[:, dc * 16:(dc + 1) * 16], hbuf[:, lastcol::TC]),
                    inc=(self.sV, 1))
                for n in range(NS):
                    k.op("vector", lambda e, n=n: e.tensor_tensor(
                        hbuf[:, n * TC:(n + 1) * TC], hbuf[:, n * TC:(n + 1) * TC],
                        stk[:, (16 + n) * TC:(16 + n + 1) * TC], OP.mult), inc=(self.sV, 1))
                ytile = yt[:, (dc % 2) * TC:(dc % 2) * TC + TC]
                hv = hbuf[:, 0:16 * TC].rearrange("p (n t) -> p t n", n=16)
                k.op("vector", lambda e, hv=hv, ytile=ytile: e.tensor_reduce(
                    ytile, hv, AX.X, OP.add), inc=(self.sV, 1))
                k.vw()
                k.op("vector", lambda e, dc=dc, ytile=ytile: e.scalar_tensor_tensor(
                    ytile, ubf[:, dc * TC:(dc + 1) * TC], dvec[:, dc:dc + 1],
                    ytile, OP.mult, OP.add), inc=(self.sV, 1))
                k.op("vector", lambda e, dc=dc, ytile=ytile: e.tensor_tensor(
                    yg[:, dc * TC:(dc + 1) * TC], ytile, sgb[:, dc * TC:(dc + 1) * TC],
                    OP.mult), inc=(self.sV, 1))
            stk_cons = self.sV.n
            ygv = self.sV.n
            if ci == 0 and fwd and "yg0" in self.taps:
                self.tap("ub0", ubf[:, 0:16 * TC])
                self.tap("lt0", Lt[:, 0:16 * TC])
                self.tap("sg0", sgb[:, 0:16 * TC])
                self.tap("yg0", yg[:, 0:16 * TC])
            k.wait("tensor", self.sV, ygv)
            prev_comb = 0
            opbank_cons = [0, 0]
            for mc in range(8):
                b = mc % 2
                k.wait("gpsimd", self.sT, slab_cons[b])
                for kc in range(16):
                    k.dma("gpsimd", wsl[:, b * 2048 + kc * 128:b * 2048 + (kc + 1) * 128],
                          self.scr_wop.ap()[kc * 128:(kc + 1) * 128,
                                            mc * 128:(mc + 1) * 128], self.sXp[b])
                wld = self.sXp[b].n
                k.wait("tensor", self.sXp[b], wld)
                bank = pb[6 + mc % 2]
                k.wait("tensor", self.sV, opbank_cons[mc % 2])
                for kc in range(16):
                    k.op("tensor", lambda e, b=b, kc=kc, bank=bank: e.matmul(
                        bank[:, 0:TC], wsl[:, b * 2048 + kc * 128:b * 2048 + (kc + 1) * 128],
                        yg[:, kc * TC:(kc + 1) * TC], start=(kc == 0), stop=(kc == 15)),
                        inc=(self.sT, 1))
                smm = self.sT.n
                slab_cons[b] = smm
                if fwd:
                    k.wait("vector", self.sT, smm)
                    k.wait("vector", self.sO)
                    k.op("vector", lambda e, bank=bank: e.tensor_copy(
                        sfb[:, 0:TC], bank[:, 0:TC]), inc=(self.sV, 1))
                    opbank_cons[mc % 2] = self.sV.n
                    k.wait("gpsimd", self.sV)
                    k.dma("gpsimd", self.scr_sf.ap()[mc * 128:(mc + 1) * 128, tsl],
                          sfb[:, 0:TC], self.sO)
                else:
                    k.wait("gpsimd", self.sV, prev_comb)
                    ldx = k.dma("gpsimd", xre[:, 0:TC],
                                self.dx.ap()[mc * 128:(mc + 1) * 128, tsl], self.sX)
                    lds = k.dma("gpsimd", sfr[:, 0:TC],
                                self.scr_sf.ap()[mc * 128:(mc + 1) * 128, tsl], self.sX)
                    k.wait("vector", self.sT, smm)
                    k.wait("vector", self.sX, lds)
                    k.wait("vector", self.sO)
                    k.op("vector", lambda e, bank=bank: e.tensor_tensor(
                        yt[:, 0:TC], bank[:, 0:TC], sfr[:, 0:TC], OP.add), inc=(self.sV, 1))
                    opbank_cons[mc % 2] = self.sV.n
                    k.vw()
                    k.op("vector", lambda e, mc=mc, tsl=tsl: e.tensor_tensor(
                        yt[:, 0:TC], yt[:, 0:TC], self.gbf[mc][:, tsl], OP.mult),
                        inc=(self.sV, 1))
                    k.vw()
                    k.op("vector", lambda e: e.tensor_tensor(
                        yt[:, 0:TC], yt[:, 0:TC], xre[:, 0:TC], OP.add), inc=(self.sV, 1))
                    prev_comb = self.sV.n
                    k.wait("gpsimd", self.sV, prev_comb)
                    k.dma("gpsimd", self.dout.ap()[mc * 128:(mc + 1) * 128, tsl],
                          yt[:, 0:TC], self.sO)
            if CHUNK_BARRIER:
                k.barrier()
        k.barrier()
        for key in (ext_key, ubf_key, Lt_key, dtu_key, sgb_key, hbuf_key, yg_key,
                    yt_key, ab_key, bb16_key, stk_key, dtv_key, bc16_key, wsl_key,
                    sfb_key, xre_key, sfr_key):
            self.ar.release(key)


_CACHE = {}


def _get_nc(taps=()):
    key = tuple(sorted(taps))
    if key not in _CACHE:
        p = Prog(taps)
        nc = p.build()
        _CACHE[key] = (p, nc)
    return _CACHE[key]


def kernel(**inputs):
    sh, percore = host_arrays(inputs)
    p, nc = _get_nc()
    in_maps = []
    for c in range(8):
        m = dict(sh)
        m["x_fm"] = percore[c]["x_fm"]
        in_maps.append(m)
    res = run_bass_kernel_spmd(nc, in_maps, core_ids=list(range(8)))
    outs = [res.results[c]["out_fm"].T for c in range(8)]
    return np.ascontiguousarray(np.stack(outs)).astype(np.float32)
